# version 04
# baseline (speedup 1.0000x reference)
import contextlib
import math

import numpy as np
import ml_dtypes

import concourse.bass as bass
import concourse.mybir as mybir
from concourse.alu_op_type import AluOpType as OP
from concourse.bass_utils import run_bass_kernel_spmd

F32 = mybir.dt.float32
BF16 = mybir.dt.bfloat16
AF = mybir.ActivationFunctionType
AX = mybir.AxisListType

NEG = float(np.finfo(np.float32).min)
HID = 4096
SEQ = 4096
NB = 4
NQH = 32
NKV = 8
DH = 128
OWN = 2048
NVT = 16
NGRP = 4
HC = 32


def bc16(ap, reps=16):
    return bass.AP(ap.tensor, ap.offset, list(ap.ap) + [[0, reps]])


class Prog:
    def __init__(self):
        self.plan = {}
        self.emitting = False
        self.cnt = {}
        self.sem = {}

    def begin(self, emitting):
        self.emitting = emitting
        self.cnt = {}
        self.psum_readers = {}
        self.buf_consumer = {}

    def inc(self, inst, semname, key, n=1):
        c = self.cnt.get(semname, 0) + n
        self.cnt[semname] = c
        if self.emitting:
            assert self.plan[key] == (semname, c), (key, semname, c, self.plan.get(key))
            inst.then_inc(self.sem[semname], n)
        else:
            assert key not in self.plan, key
            self.plan[key] = (semname, c)

    def wait(self, eng, key):
        if self.emitting:
            semname, c = self.plan[key]
            eng.wait_ge(self.sem[semname], c)

    def wait_min(self, eng, semname, c):
        if self.emitting and c > 0:
            eng.wait_ge(self.sem[semname], c)

    def psum_claim(self, eng, bank, reader_keys):
        prev = self.psum_readers.get(bank)
        if prev:
            for k in prev:
                self.wait(eng, k)
        self.psum_readers[bank] = list(reader_keys)

    def buf_claim(self, eng, buf, consumer_key):
        prev = self.buf_consumer.get(buf)
        if prev is not None:
            self.wait(eng, prev)
        self.buf_consumer[buf] = consumer_key


def geom(g, qt):
    j = 4 * g + qt
    ext = 256 * (j + 1)
    n512 = (j + 2) // 2
    nkb = 2 * (j + 1)
    nbat = (nkb + 7) // 8
    return j, ext, n512, nkb, nbat


def blkN(j, b, n512, ext):
    return 512 if b < n512 - 1 else ext - 512 * (n512 - 1)


def build_program(upto=99, debug=False):
    nc = bass.Bass("TRN2", target_bir_lowering=False)

    def din(name, shape, dt=F32):
        return nc.dram_tensor(name, list(shape), dt, kind="ExternalInput").ap()

    x_full = din("x_full", (SEQ, HID))
    x_own = din("x_own", (OWN, HID))
    wqT = din("wqT", (HID, NQH * DH))
    wkT = din("wkT", (HID, NKV * DH))
    wvT = din("wvT", (HID, NKV * DH))
    woT = din("woT", (HID, HID))
    wupT = din("wupT", (HID, HID))
    A_padT = din("A_padT", (HID, 128))
    B_padT = din("B_padT", (128, HID))
    ebias = din("ebias", (1, 4))
    cosq = din("cosq", (DH, OWN))
    sinq = din("sinq", (DH, OWN))
    cosk = din("cosk", (DH, SEQ))
    sink = din("sink", (DH, SEQ))
    identb = din("identb", (128, 128), BF16)
    identf = din("identf", (128, 128))
    rotT = din("rotT", (128, 128), BF16)
    trimask2 = din("trimask2", (128, 256))
    recent2 = din("recent2", (128, 256))
    ut2prev = din("ut2prev", (128, 128))
    ones1 = din("ones1", (1, 128))

    y_out = nc.dram_tensor("y", [OWN, HID], F32, kind="ExternalOutput").ap()

    KIND = "ExternalOutput" if debug else "Internal"
    xnT_d = nc.dram_tensor("xnT_d", [HID, SEQ], BF16, kind=KIND).ap()
    xnoT_d = nc.dram_tensor("xnoT_d", [HID, OWN], BF16, kind=KIND).ap()
    qT_d = nc.dram_tensor("qT_d", [NQH, DH, OWN], BF16, kind=KIND).ap()
    kT_d = nc.dram_tensor("kT_d", [NKV, DH, SEQ], BF16, kind=KIND).ap()
    vS_d = nc.dram_tensor("vS_d", [NKV, SEQ, DH], BF16, kind=KIND).ap()
    oT_d = nc.dram_tensor("oT_d", [NVT, NQH, DH, 128], BF16, kind=KIND).ap()

    P = Prog()

    semnames = [
        "t", "v", "a", "p", "bar", "dcst",
        "dx0", "dx1", "dxo0", "dxo1", "wxnt", "wxno",
        "gwk", "gwv", "gwq", "gwo", "gwup", "gab",
        "dnb0", "dnb1",
        "wkt", "wvs0", "wvs1", "wqt",
        "dk0", "dk1", "dv0", "dv1", "dq0", "dq1", "dq2", "dq3",
        "wot0", "wot1",
        "dot0", "dot1", "dxl0", "dxl1", "dxs0", "dxs1",
        "wy",
    ]

    with contextlib.ExitStack() as top:
        for s in semnames:
            P.sem[s] = top.enter_context(nc.semaphore(s))
        block = top.enter_context(nc.Block())

        c_id = top.enter_context(nc.sbuf_tensor("c_id", [128, 128], BF16))
        c_idf = top.enter_context(nc.sbuf_tensor("c_idf", [128, 128], F32))
        c_rt = top.enter_context(nc.sbuf_tensor("c_rt", [128, 128], BF16))
        c_tri = top.enter_context(nc.sbuf_tensor("c_tri", [128, 256], F32))
        c_rec = top.enter_context(nc.sbuf_tensor("c_rec", [128, 256], F32))
        c_utp = top.enter_context(nc.sbuf_tensor("c_utp", [128, 128], F32))
        c_ones = top.enter_context(nc.sbuf_tensor("c_ones", [1, 128], F32))
        c_eb = top.enter_context(nc.sbuf_tensor("c_eb", [1, 4], F32))
        c_bb = top.enter_context(nc.sbuf_tensor("c_bb", [128, 4], F32))
        c_eps = top.enter_context(nc.sbuf_tensor("c_eps", [128, 1], F32))
        c_dmv = top.enter_context(nc.sbuf_tensor("c_dmv", [128, 1], F32))
        c_dma_ = top.enter_context(nc.sbuf_tensor("c_dma_", [128, 1], F32))
        c_dmg = top.enter_context(nc.sbuf_tensor("c_dmg", [128, 1], F32))

        barcnt = [0]

        def run():
            E = P.emitting
            barcnt[0] = 0

            def eng_sections(fns):
                engines = {}
                if E:
                    for nm in ["sync", "gpsimd", "tensor", "scalar", "vector"]:
                        f = fns.get(nm)
                        dec = getattr(block, nm)

                        def wrapper(eng, _f=f, _nm=nm):
                            engines[_nm] = eng
                            if _f:
                                _f(eng)
                        dec(wrapper)
                else:
                    for nm in ["sync", "gpsimd", "tensor", "scalar", "vector"]:
                        engines[nm] = None
                        f = fns.get(nm)
                        if f:
                            f(None)
                return engines

            def barrier():
                barcnt[0] += 1
                tgt = 5 * barcnt[0]
                pfx = {"sync": "d", "scalar": "w", "gpsimd": "g"}

                def mk(nm, op):
                    def f(e):
                        pf = pfx.get(nm)
                        if pf:
                            for s in P.sem:
                                if s.startswith(pf) and P.cnt.get(s):
                                    P.wait_min(e, s, P.cnt[s])
                        i = op(e) if P.emitting else None
                        P.inc(i, "bar", ("bar", barcnt[0], nm), 1)
                        P.wait_min(e, "bar", tgt)
                    return f
                eng_sections(dict(
                    sync=mk("sync", lambda e: e.nop()),
                    gpsimd=mk("gpsimd", lambda e: e.memset(c_dmg[:], 0.0)),
                    tensor=mk("tensor", lambda e: e.nop()),
                    scalar=mk("scalar", lambda e: e.memzero(c_dma_[:])),
                    vector=mk("vector", lambda e: e.memset(c_dmv[:], 0.0)),
                ))

            with contextlib.ExitStack() as ph:
                bbp = (ph.enter_context(nc.psum_tensor("bbp", [128, 512], F32)) if E else None)

                def p0_sync(e):
                    loads = [
                        (c_id, identb), (c_rt, rotT), (c_tri, trimask2),
                        (c_rec, recent2), (c_utp, ut2prev), (c_ones, ones1),
                        (c_eb, ebias), (c_idf, identf),
                    ]
                    for i2, (dst, src) in enumerate(loads):
                        inst = e.dma_start(out=dst[:], in_=src) if E else None
                        P.inc(inst, "dcst", ("dcst", i2), 16)

                def p0_tensor(e):
                    P.wait_min(e, "dcst", 16 * 8)
                    i = e.matmul(bbp[:, 0:4], c_ones[:], c_eb[:], start=True, stop=True) if E else None
                    P.inc(i, "t", ("bbmm",))

                def p0_vector(e):
                    P.wait(e, ("bbmm",))
                    i = e.tensor_copy(c_bb[:], bbp[:, 0:4]) if E else None
                    P.inc(i, "v", ("bbev",))
                    i = e.memset(c_eps[:], 1e-5) if E else None
                    P.inc(i, "v", ("epss",))

                eng_sections(dict(sync=p0_sync, tensor=p0_tensor, vector=p0_vector))
                barrier()

            if upto >= 1:
                with contextlib.ExitStack() as ph:
                    xb = [(ph.enter_context(nc.sbuf_tensor(f"xb{i}", [128, HID], F32)) if E else None) for i in range(2)]
                    stats = (ph.enter_context(nc.sbuf_tensor("stats", [128, 8, 6], F32)) if E else None)
                    mv = (ph.enter_context(nc.sbuf_tensor("mv", [128, 2], F32)) if E else None)
                    nmn = (ph.enter_context(nc.sbuf_tensor("nmn", [128, 1], F32)) if E else None)
                    lnv = (ph.enter_context(nc.sbuf_tensor("lnv", [128, 1], F32)) if E else None)
                    rstd = (ph.enter_context(nc.sbuf_tensor("rstd", [128, 1], F32)) if E else None)
                    xnb = (ph.enter_context(nc.sbuf_tensor("xnb", [128, HID], BF16)) if E else None)
                    col = (ph.enter_context(nc.sbuf_tensor("col", [128, HC, 4, 128], BF16)) if E else None)
                    tp = [(ph.enter_context(nc.psum_tensor(f"p1tp{i}", [128, 1024], BF16)) if E else None) for i in range(2)]

                    jobs = [
                        ("full", x_full, 32, xnT_d, "dx0", "dx1", "wxnt"),
                        ("own", x_own, 16, xnoT_d, "dxo0", "dxo1", "wxno"),
                    ]

                    def p1_sync(e):
                        for jn, src, nt, dst, s0, s1, ws in jobs:
                            for t in range(nt):
                                slot = t % 2
                                sname = s0 if slot == 0 else s1
                                P.buf_claim(e, ("xb", slot), ("p1xn", jn, t))
                                i = e.dma_start(out=xb[slot][:], in_=src[t * 128:(t + 1) * 128, :]) if E else None
                                P.inc(i, sname, ("p1x", jn, t), 16)

                    def p1_vector(e):
                        for jn, src, nt, dst, s0, s1, ws in jobs:
                            for t in range(nt):
                                slot = t % 2
                                sname = s0 if slot == 0 else s1
                                P.wait_min(e, sname, 16 * (t // 2 + 1))
                                for i8 in range(8):
                                    i = e.bn_stats(stats[:, i8, :], xb[slot][:, i8 * 512:(i8 + 1) * 512]) if E else None
                                P.inc(i, "v", ("p1bn", jn, t))
                                P.wait(e, ("p1bn", jn, t))
                                i = e.bn_aggr(mv[:], stats[:]) if E else None
                                P.inc(i, "v", ("p1ag", jn, t))
                                P.wait(e, ("p1ag", jn, t))
                                i = e.tensor_scalar(nmn[:], mv[:, 0:1], -1.0, None, OP.mult) if E else None
                                P.inc(i, "v", ("p1nm", jn, t))
                                P.wait(e, ("p1nm", jn, t))
                                P.wait(e, ("p1rs", jn, t))
                                P.buf_claim(e, "xnb", ("p1tp", jn, t, 3))
                                i = e.tensor_scalar(xnb[:], xb[slot][:], nmn[:], rstd[:], OP.add, OP.mult) if E else None
                                P.inc(i, "v", ("p1xn", jn, t))

                    def p1_scalar(e):
                        for jn, src, nt, dst, s0, s1, ws in jobs:
                            for t in range(nt):
                                P.wait(e, ("p1ag", jn, t))
                                i = e.activation(lnv[:], mv[:, 1:2], AF.Ln, bias=c_eps[:], scale=1.0) if E else None
                                P.inc(i, "a", ("p1ln", jn, t))
                                P.wait(e, ("p1ln", jn, t))
                                i = e.activation(rstd[:], lnv[:], AF.Exp, scale=-0.5) if E else None
                                P.inc(i, "a", ("p1rs", jn, t))
                                for b in range(4):
                                    if t % 4 == 0 and b == 0:
                                        if jn == "own" and t == 0:
                                            P.wait_min(e, "wxnt", 16 * 8)
                                        g = t // 4
                                        P.wait_min(e, ws, 16 * g)
                                    P.wait(e, ("p1tp", jn, t, b))
                                    i = e.copy(col[:, b * 8:(b + 1) * 8, t % 4, :],
                                               tp[b % 2][:].rearrange("p (c n) -> p c n", n=128)) if E else None
                                    P.inc(i, "a", ("p1ev", jn, t, b))
                                if t % 4 == 3:
                                    g = t // 4
                                    P.wait(e, ("p1ev", jn, t, 3))
                                    for _tt in range(t - 3, t):
                                        P.wait(e, ("p1ev", jn, _tt, 3))
                                    if E:
                                        dd = dst.rearrange("(c p) s -> p c s", p=128)
                                        i = e.dma_start(out=dd[:, :, g * 512:(g + 1) * 512],
                                                        in_=col[:].rearrange("p c f n -> p c (f n)"))
                                    else:
                                        i = None
                                    P.inc(i, ws, ("p1st", jn, g), 16)

                    def p1_tensor(e):
                        for jn, src, nt, dst, s0, s1, ws in jobs:
                            for t in range(nt):
                                P.wait(e, ("p1xn", jn, t))
                                for b in range(4):
                                    P.psum_claim(e, ("p1", b % 2), [("p1ev", jn, t, b)])
                                    for i8 in range(8):
                                        hcix = b * 8 + i8
                                        i = e.transpose(tp[b % 2][:, i8 * 128:(i8 + 1) * 128],
                                                        xnb[:, hcix * 128:(hcix + 1) * 128], c_id[:]) if E else None
                                    P.inc(i, "t", ("p1tp", jn, t, b))

                    eng_sections(dict(sync=p1_sync, vector=p1_vector,
                                        scalar=p1_scalar, tensor=p1_tensor))
                    barrier()

            if upto >= 2:
                with contextlib.ExitStack() as ph:
                    wkb = (ph.enter_context(nc.sbuf_tensor("wkb", [128, HC, NKV * DH], BF16)) if E else None)
                    xnbk = (ph.enter_context(nc.sbuf_tensor("xnbk", [128, HC, 512], BF16)) if E else None)
                    ksb = (ph.enter_context(nc.sbuf_tensor("ksb", [128, 512], BF16)) if E else None)
                    kt1 = (ph.enter_context(nc.sbuf_tensor("kt1", [128, 512], F32)) if E else None)
                    kt2 = (ph.enter_context(nc.sbuf_tensor("kt2", [128, 512], F32)) if E else None)
                    krsb = (ph.enter_context(nc.sbuf_tensor("krsb", [128, 512], BF16)) if E else None)
                    kout = (ph.enter_context(nc.sbuf_tensor("kout", [128, 512], BF16)) if E else None)
                    cskl = (ph.enter_context(nc.sbuf_tensor("cskl", [128, SEQ], F32)) if E else None)
                    sskl = (ph.enter_context(nc.sbuf_tensor("sskl", [128, SEQ], F32)) if E else None)
                    kpr = [(ph.enter_context(nc.psum_tensor(f"kpr{i}", [128, 512], F32)) if E else None) for i in range(2)]
                    rpr = [(ph.enter_context(nc.psum_tensor(f"rpr{i}", [128, 512], F32)) if E else None) for i in range(2)]

                    def p2k_gp(e):
                        i = e.dma_start(out=wkb[:], in_=wkT.rearrange("(c p) n -> p c n", p=128)) if E else None
                        P.inc(i, "gwk", ("gwk",), 16)
                        P.wait_min(e, "dcst", 16 * 10)
                        for tb in range(8):
                            for kh in range(NKV):
                                n = tb * NKV + kh
                                P.wait(e, ("p2kev", tb, kh))
                                i = e.tensor_tensor(kt1[:], ksb[:], cskl[:, tb * 512:(tb + 1) * 512], OP.mult) if E else None
                                P.inc(i, "p", ("p2kt1", tb, kh))
                                P.wait(e, ("p2krev", tb, kh))
                                i = e.tensor_tensor(kt2[:], krsb[:], sskl[:, tb * 512:(tb + 1) * 512], OP.mult) if E else None
                                P.inc(i, "p", ("p2kt2", tb, kh))
                                P.wait(e, ("p2kt2", tb, kh))
                                P.wait_min(e, "wkt", 16 * n)
                                i = e.tensor_tensor(kout[:], kt1[:], kt2[:], OP.add) if E else None
                                P.inc(i, "p", ("p2kr3", tb, kh))

                    def p2k_sync(e):
                        i = e.dma_start(out=cskl[:], in_=cosk) if E else None
                        P.inc(i, "dcst", ("csk",), 16)
                        i = e.dma_start(out=sskl[:], in_=sink) if E else None
                        P.inc(i, "dcst", ("ssk",), 16)
                        for tb in range(8):
                            if tb >= 1:
                                P.wait(e, ("p2kmm", tb - 1, NKV - 1))
                            if E:
                                dd = xnT_d.rearrange("(c p) s -> p c s", p=128)
                                i = e.dma_start(out=xnbk[:], in_=dd[:, :, tb * 512:(tb + 1) * 512])
                            else:
                                i = None
                            P.inc(i, "dnb0", ("p2kx", tb), 16)

                    def p2k_tensor(e):
                        P.wait_min(e, "gwk", 16)
                        for tb in range(8):
                            P.wait_min(e, "dnb0", 16 * (tb + 1))
                            for kh in range(NKV):
                                s = kh % 2
                                P.psum_claim(e, ("p2kk", s), [("p2kev", tb, kh)])
                                for hc in range(HC):
                                    i = e.matmul(kpr[s][:], wkb[:, hc, kh * 128:(kh + 1) * 128],
                                                 xnbk[:, hc, :], start=(hc == 0), stop=(hc == HC - 1)) if E else None
                                P.inc(i, "t", ("p2kmm", tb, kh))
                                P.wait(e, ("p2kev", tb, kh))
                                P.psum_claim(e, ("p2kr", s), [("p2krev", tb, kh)])
                                i = e.matmul(rpr[s][:], c_rt[:], ksb[:], start=True, stop=True) if E else None
                                P.inc(i, "t", ("p2krk", tb, kh))

                    def p2k_scalar(e):
                        for tb in range(8):
                            for kh in range(NKV):
                                n = tb * NKV + kh
                                s = kh % 2
                                P.wait(e, ("p2kmm", tb, kh))
                                if n >= 1:
                                    ptb, pkh = divmod(n - 1, NKV)
                                    P.wait(e, ("p2kt1", ptb, pkh))
                                    P.wait(e, ("p2krk", ptb, pkh))
                                i = e.copy(ksb[:], kpr[s][:]) if E else None
                                P.inc(i, "a", ("p2kev", tb, kh))
                                P.wait(e, ("p2krk", tb, kh))
                                if n >= 1:
                                    ptb, pkh = divmod(n - 1, NKV)
                                    P.wait(e, ("p2kt2", ptb, pkh))
                                i = e.copy(krsb[:], rpr[s][:]) if E else None
                                P.inc(i, "a", ("p2krev", tb, kh))
                                P.wait(e, ("p2kr3", tb, kh))
                                i = e.dma_start(out=kT_d[kh, :, tb * 512:(tb + 1) * 512], in_=kout[:]) if E else None
                                P.inc(i, "wkt", ("p2kst", tb, kh), 16)

                    eng_sections(dict(gpsimd=p2k_gp, sync=p2k_sync,
                                        tensor=p2k_tensor, scalar=p2k_scalar))
                    barrier()

            if upto >= 3:
                with contextlib.ExitStack() as ph:
                    wvb = (ph.enter_context(nc.sbuf_tensor("wvb", [128, HC, NKV * DH], BF16)) if E else None)
                    xnbv = (ph.enter_context(nc.sbuf_tensor("xnbv", [128, HC, 512], BF16)) if E else None)
                    vsb = [(ph.enter_context(nc.sbuf_tensor(f"vsb{i}", [128, 512], BF16)) if E else None) for i in range(2)]
                    vpr = [(ph.enter_context(nc.psum_tensor(f"vpr{i}", [128, 512], F32)) if E else None) for i in range(2)]

                    def p2v_gp(e):
                        i = e.dma_start(out=wvb[:], in_=wvT.rearrange("(c p) n -> p c n", p=128)) if E else None
                        P.inc(i, "gwv", ("gwv",), 16)

                    def p2v_sync(e):
                        for tb in range(8):
                            if tb >= 1:
                                P.wait(e, ("p2vmm", tb - 1, 3, 1))
                            if E:
                                dd = xnT_d.rearrange("(c p) s -> p c s", p=128)
                                i = e.dma_start(out=xnbv[:], in_=dd[:, :, tb * 512:(tb + 1) * 512])
                            else:
                                i = None
                            P.inc(i, "dnb1", ("p2vx", tb), 16)

                    def p2v_tensor(e):
                        P.wait_min(e, "gwv", 16)
                        for tb in range(8):
                            P.wait_min(e, "dnb1", 16 * (tb + 1))
                            for tt in range(4):
                                for vbk in range(2):
                                    n = tb * 8 + tt * 2 + vbk
                                    s = n % 2
                                    P.psum_claim(e, ("p2v", s), [("p2vev", tb, tt, vbk)])
                                    for hc in range(HC):
                                        i = e.matmul(vpr[s][:], xnbv[:, hc, tt * 128:(tt + 1) * 128],
                                                     wvb[:, hc, vbk * 512:(vbk + 1) * 512],
                                                     start=(hc == 0), stop=(hc == HC - 1)) if E else None
                                    P.inc(i, "t", ("p2vmm", tb, tt, vbk))

                    def p2v_scalar(e):
                        for tb in range(8):
                            for tt in range(4):
                                for vbk in range(2):
                                    n = tb * 8 + tt * 2 + vbk
                                    s = n % 2
                                    wsem = "wvs0" if s == 0 else "wvs1"
                                    P.wait(e, ("p2vmm", tb, tt, vbk))
                                    P.wait_min(e, wsem, 16 * 4 * (n // 2))
                                    i = e.copy(vsb[s][:], vpr[s][:]) if E else None
                                    P.inc(i, "a", ("p2vev", tb, tt, vbk))
                                    P.wait(e, ("p2vev", tb, tt, vbk))
                                    trow = tb * 512 + tt * 128
                                    for hh in range(4):
                                        kh = vbk * 4 + hh
                                        i = e.dma_start(out=vS_d[kh, trow:trow + 128, :],
                                                        in_=vsb[s][:, hh * 128:(hh + 1) * 128]) if E else None
                                        P.inc(i, wsem, ("p2vst", tb, tt, vbk, hh), 16)

                    eng_sections(dict(gpsimd=p2v_gp, sync=p2v_sync,
                                        tensor=p2v_tensor, scalar=p2v_scalar))
                    barrier()

            if upto >= 4:
                with contextlib.ExitStack() as ph:
                    wqb = (ph.enter_context(nc.sbuf_tensor("wqb", [128, HC, 8 * DH], BF16)) if E else None)
                    xnbq = (ph.enter_context(nc.sbuf_tensor("xnbq", [128, HC, 512], BF16)) if E else None)
                    qsb = (ph.enter_context(nc.sbuf_tensor("qsb", [128, 512], BF16)) if E else None)
                    qt1 = (ph.enter_context(nc.sbuf_tensor("qt1", [128, 512], F32)) if E else None)
                    qt2 = (ph.enter_context(nc.sbuf_tensor("qt2", [128, 512], F32)) if E else None)
                    qrsb = (ph.enter_context(nc.sbuf_tensor("qrsb", [128, 512], BF16)) if E else None)
                    qout = (ph.enter_context(nc.sbuf_tensor("qout", [128, 512], BF16)) if E else None)
                    cql = (ph.enter_context(nc.sbuf_tensor("cql", [128, OWN], F32)) if E else None)
                    sql = (ph.enter_context(nc.sbuf_tensor("sql", [128, OWN], F32)) if E else None)
                    qpr = [(ph.enter_context(nc.psum_tensor(f"qpr{i}", [128, 512], F32)) if E else None) for i in range(2)]
                    qrp = [(ph.enter_context(nc.psum_tensor(f"qrp{i}", [128, 512], F32)) if E else None) for i in range(2)]

                    def p3_gp(e):
                        for qq in range(4):
                            if qq >= 1:
                                P.wait(e, ("p3mm", qq - 1, 3, 7))
                            if E:
                                i = e.dma_start(out=wqb[:],
                                                in_=wqT.rearrange("(c p) n -> p c n", p=128)[:, :, qq * 1024:(qq + 1) * 1024])
                            else:
                                i = None
                            P.inc(i, "gwq", ("gwq", qq), 16)
                            for sp in range(4):
                                for q8 in range(8):
                                    P.wait(e, ("p3ev", qq, sp, q8))
                                    i = e.tensor_tensor(qt1[:], qsb[:], cql[:, sp * 512:(sp + 1) * 512], OP.mult) if E else None
                                    P.inc(i, "p", ("p3t1", qq, sp, q8))
                                    P.wait(e, ("p3rev", qq, sp, q8))
                                    i = e.tensor_tensor(qt2[:], qrsb[:], sql[:, sp * 512:(sp + 1) * 512], OP.mult) if E else None
                                    P.inc(i, "p", ("p3t2", qq, sp, q8))
                                    P.wait(e, ("p3t2", qq, sp, q8))
                                    idx = (qq * 4 + sp) * 8 + q8
                                    P.wait_min(e, "wqt", 16 * idx)
                                    i = e.tensor_tensor(qout[:], qt1[:], qt2[:], OP.add) if E else None
                                    P.inc(i, "p", ("p3r3", qq, sp, q8))

                    def p3_sync(e):
                        i = e.dma_start(out=cql[:], in_=cosq) if E else None
                        P.inc(i, "dcst", ("csq",), 16)
                        i = e.dma_start(out=sql[:], in_=sinq) if E else None
                        P.inc(i, "dcst", ("ssq",), 16)
                        n = 0
                        for qq in range(4):
                            for sp in range(4):
                                if n >= 1:
                                    pq, psp = divmod(n - 1, 4)
                                    P.wait(e, ("p3mm", pq, psp, 7))
                                if E:
                                    dd = xnoT_d.rearrange("(c p) s -> p c s", p=128)
                                    i = e.dma_start(out=xnbq[:], in_=dd[:, :, sp * 512:(sp + 1) * 512])
                                else:
                                    i = None
                                P.inc(i, "dnb0", ("p3x", qq, sp), 16)
                                n += 1

                    def p3_tensor(e):
                        P.wait_min(e, "dcst", 16 * 12)
                        n = 0
                        for qq in range(4):
                            P.wait_min(e, "gwq", 16 * (qq + 1))
                            for sp in range(4):
                                n += 1
                                P.wait_min(e, "dnb0", 16 * (8 + n))
                                for q8 in range(8):
                                    s = q8 % 2
                                    P.psum_claim(e, ("p3q", s), [("p3ev", qq, sp, q8)])
                                    for hc in range(HC):
                                        i = e.matmul(qpr[s][:], wqb[:, hc, q8 * 128:(q8 + 1) * 128],
                                                     xnbq[:, hc, :], start=(hc == 0), stop=(hc == HC - 1)) if E else None
                                    P.inc(i, "t", ("p3mm", qq, sp, q8))
                                    P.wait(e, ("p3ev", qq, sp, q8))
                                    P.psum_claim(e, ("p3r", s), [("p3rev", qq, sp, q8)])
                                    i = e.matmul(qrp[s][:], c_rt[:], qsb[:], start=True, stop=True) if E else None
                                    P.inc(i, "t", ("p3rk", qq, sp, q8))

                    def p3_scalar(e):
                        for qq in range(4):
                            for sp in range(4):
                                for q8 in range(8):
                                    n = (qq * 4 + sp) * 8 + q8
                                    s = q8 % 2
                                    P.wait(e, ("p3mm", qq, sp, q8))
                                    if n >= 1:
                                        pq, prem = divmod(n - 1, 32)
                                        psp, pq8 = divmod(prem, 8)
                                        P.wait(e, ("p3t1", pq, psp, pq8))
                                        P.wait(e, ("p3rk", pq, psp, pq8))
                                    i = e.copy(qsb[:], qpr[s][:]) if E else None
                                    P.inc(i, "a", ("p3ev", qq, sp, q8))
                                    P.wait(e, ("p3rk", qq, sp, q8))
                                    if n >= 1:
                                        P.wait(e, ("p3t2", pq, psp, pq8))
                                    i = e.copy(qrsb[:], qrp[s][:]) if E else None
                                    P.inc(i, "a", ("p3rev", qq, sp, q8))
                                    P.wait(e, ("p3r3", qq, sp, q8))
                                    qh = qq * 8 + q8
                                    i = e.dma_start(out=qT_d[qh, :, sp * 512:(sp + 1) * 512], in_=qout[:]) if E else None
                                    P.inc(i, "wqt", ("p3st", qq, sp, q8), 16)

                    eng_sections(dict(gpsimd=p3_gp, sync=p3_sync,
                                        tensor=p3_tensor, scalar=p3_scalar))
                    barrier()

            if upto >= 5:
                with contextlib.ExitStack() as ph:
                    kbuf = [(ph.enter_context(nc.sbuf_tensor(f"kbuf{i}", [128, SEQ], BF16)) if E else None) for i in range(2)]
                    vbuf = [(ph.enter_context(nc.sbuf_tensor(f"vbuf{i}", [128, 32, 128], BF16)) if E else None) for i in range(2)]
                    qbuf = [(ph.enter_context(nc.sbuf_tensor(f"qbuf{i}", [128, 512], BF16)) if E else None) for i in range(4)]
                    Ssb = (ph.enter_context(nc.sbuf_tensor("Ssb", [128, SEQ], F32)) if E else None)
                    praw = (ph.enter_context(nc.sbuf_tensor("praw", [128, SEQ], BF16)) if E else None)
                    Pm = (ph.enter_context(nc.sbuf_tensor("Pm", [128, SEQ], BF16)) if E else None)
                    cmax = [(ph.enter_context(nc.sbuf_tensor(f"cmax{i}", [128, 256], F32)) if E else None) for i in range(2)]
                    m8 = [(ph.enter_context(nc.sbuf_tensor(f"m8_{i}", [128, 8], F32)) if E else None) for i in range(2)]
                    nm1 = [(ph.enter_context(nc.sbuf_tensor(f"nm1_{i}", [128, 1], F32)) if E else None) for i in range(2)]
                    K16 = [(ph.enter_context(nc.sbuf_tensor(f"K16_{i}", [128, 256], F32)) if E else None) for i in range(2)]
                    KbD = [(ph.enter_context(nc.sbuf_tensor(f"KbD{i}", [128, 256], F32)) if E else None) for i in range(2)]
                    KbP = [(ph.enter_context(nc.sbuf_tensor(f"KbP{i}", [128, 128], F32)) if E else None) for i in range(2)]
                    racc = [(ph.enter_context(nc.sbuf_tensor(f"racc{i}", [128, 3], F32)) if E else None) for i in range(2)]
                    rr = [(ph.enter_context(nc.sbuf_tensor(f"rr{i}", [128, 1], F32)) if E else None) for i in range(2)]
                    rcp = [(ph.enter_context(nc.sbuf_tensor(f"rcp{i}", [128, 1], F32)) if E else None) for i in range(2)]
                    PTs = [(ph.enter_context(nc.sbuf_tensor(f"PTs{i}", [128, 1024], BF16)) if E else None) for i in range(2)]
                    ob = [(ph.enter_context(nc.sbuf_tensor(f"ob{i}", [128, 128], BF16)) if E else None) for i in range(2)]
                    Rbs = [(ph.enter_context(nc.sbuf_tensor(f"Rbs{i}", [128, 128], F32)) if E else None) for i in range(2)]
                    rrow = [(ph.enter_context(nc.sbuf_tensor(f"rrow{i}", [1, 128], F32)) if E else None) for i in range(2)]
                    Sps = [(ph.enter_context(nc.psum_tensor(f"Sps{i}", [128, 512], F32)) if E else None) for i in range(4)]
                    PTp = [(ph.enter_context(nc.psum_tensor(f"PTp{i}", [128, 1024], BF16)) if E else None) for i in range(2)]
                    AVp = (ph.enter_context(nc.psum_tensor("AVp", [128, 512], F32)) if E else None)
                    RBp = (ph.enter_context(nc.psum_tensor("RBp", [128, 512], F32)) if E else None)

                    ptpar = {}
                    _c = 0
                    for g in range(NGRP):
                        for lh in range(NQH):
                            for qt in range(4):
                                j, ext, n512, nkb, nbat = geom(g, qt)
                                for bat in range(nbat):
                                    ptpar[(g, lh, qt, bat)] = _c % 2
                                    _c += 1

                    def tile_readers(g, lh, qt):
                        j, ext, n512, nkb, nbat = geom(g, qt)
                        out = []
                        for b in range(n512):
                            rd = []
                            if not (j % 2 == 0 and b == n512 - 1):
                                rd.append(("sev", g, lh, qt, b))
                            if b == n512 - 1:
                                rd.append(("dstt", g, lh, qt))
                            out.append(rd)
                        return out

                    def prev_tile(g, lh, qt):
                        idx = (g * NQH + lh) * 4 + qt
                        if idx == 0:
                            return None
                        pg, rem = divmod(idx - 1, NQH * 4)
                        plh, pqt = divmod(rem, 4)
                        return (pg, plh, pqt)

                    def p4_sync(e):
                        P.wait_min(e, "wkt", 16 * 64)
                        P.wait_min(e, "wvs0", 16 * 128)
                        P.wait_min(e, "wvs1", 16 * 128)
                        P.wait_min(e, "wqt", 16 * 128)
                        LHS = [(g, lh) for g in range(NGRP) for lh in range(NQH)]

                        def emit_loads(li):
                            g, lh = LHS[li]
                            kv = lh // 4
                            ext_g = 1024 * (g + 1)
                            if lh % 4 == 0:
                                kvi = kvmap[(g, lh)]
                                slot = kvi % 2
                                if kvi >= 2:
                                    og, okv = divmod(kvi - 2, 8)
                                    olh = okv * 4 + 3
                                    oj, oext, on512, onkb, onbat = geom(og, 3)
                                    P.wait(e, ("qk", og, olh, 3, on512 - 1))
                                    P.wait(e, ("avb", og, olh, 3, onbat - 1))
                                i = e.dma_start(out=kbuf[slot][:, 0:ext_g], in_=kT_d[kv, :, 0:ext_g]) if E else None
                                P.inc(i, "dk0" if slot == 0 else "dk1", ("p4k", kvi), 16)
                                if E:
                                    i = e.dma_start(out=vbuf[slot][:, 0:8 * (g + 1), :],
                                                    in_=vS_d[kv].rearrange("(n p) d -> p n d", p=128)[:, 0:8 * (g + 1), :])
                                else:
                                    i = None
                                P.inc(i, "dv0" if slot == 0 else "dv1", ("p4v", kvi), 16)
                            qi = qmap[(g, lh)]
                            qslot = qi % 4
                            if qi >= 4:
                                og, olh = divmod(qi - 4, NQH)
                                oj, oext, on512, onkb, onbat = geom(og, 3)
                                P.wait(e, ("qk", og, olh, 3, on512 - 1))
                            i = e.dma_start(out=qbuf[qslot][:], in_=qT_d[lh, :, g * 512:(g + 1) * 512]) if E else None
                            P.inc(i, ["dq0", "dq1", "dq2", "dq3"][qslot], ("p4q", qi), 16)

                        LOOK = 2
                        for li in range(LOOK + 1):
                            emit_loads(li)
                        for i in range(NT):
                            g, lh, qt = TILES[i]
                            if qt == 0:
                                li = g * NQH + lh
                                if li + LOOK + 1 < len(LHS):
                                    emit_loads(li + LOOK + 1)
                            P.wait(e, ("oev", g, lh, qt))
                            wsem = "wot0" if i % 2 == 0 else "wot1"
                            vt = 4 * g + qt
                            i2 = e.dma_start(out=oT_d[vt, lh, :, :], in_=ob[i % 2][:]) if E else None
                            P.inc(i2, wsem, ("otst", g, lh, qt), 16)

                    TILES = [(g, lh, qt) for g in range(NGRP) for lh in range(NQH) for qt in range(4)]
                    NT = len(TILES)
                    kvmap = {}
                    qmap = {}
                    _kvi = 0
                    _qi = 0
                    for g in range(NGRP):
                        for lh in range(NQH):
                            if lh % 4 == 0:
                                _kvi += 1
                            kvmap[(g, lh)] = _kvi - 1
                            qmap[(g, lh)] = _qi
                            _qi += 1

                    def p4_tensor(e):
                        def emit_qk(i, phase):
                            g, lh, qt = TILES[i]
                            j, ext, n512, nkb, nbat = geom(g, qt)
                            if phase == 0 and qt == 0:
                                P.wait(e, ("p4k", kvmap[(g, lh)]))
                                P.wait(e, ("p4v", kvmap[(g, lh)]))
                                P.wait(e, ("p4q", qmap[(g, lh)]))
                            rds = tile_readers(g, lh, qt)
                            rng = range(0, min(2, n512)) if phase == 0 else range(2, n512)
                            for b in rng:
                                bank = 2 * (qt % 2) + (b % 2)
                                P.psum_claim(e, ("S", bank), rds[b])
                                N = blkN(j, b, n512, ext)
                                i2 = e.matmul(Sps[bank][:, 0:N], qbuf[qmap[(g, lh)] % 4][:, qt * 128:(qt + 1) * 128],
                                              kbuf[kvmap[(g, lh)] % 2][:, b * 512:b * 512 + N],
                                              start=True, stop=True) if E else None
                                P.inc(i2, "t", ("qk", g, lh, qt, b))

                        def emit_ptav(i):
                            g, lh, qt = TILES[i]
                            j, ext, n512, nkb, nbat = geom(g, qt)
                            par = i % 2
                            P.psum_claim(e, ("AV",), [("oev", g, lh, qt)])
                            P.wait(e, ("rcp", g, lh, qt))
                            P.psum_claim(e, ("RB",), [("rrowev", g, lh, qt), ("rbev", g, lh, qt)])
                            i2 = e.transpose(RBp[0:1, 128:256], rcp[par][:], c_idf[:]) if E else None
                            P.inc(i2, "t", ("rbt", g, lh, qt))
                            for bat in range(nbat):
                                lo = bat * 8
                                hi = min(nkb, lo + 8)
                                pslot = ptpar[(g, lh, qt, bat)]
                                P.psum_claim(e, ("PT", pslot), [("ptev", g, lh, qt, bat)])
                                if bat == 0:
                                    P.wait(e, ("pmask", g, lh, qt))
                                for k in range(lo, hi):
                                    i2 = e.transpose(PTp[pslot][:, (k - lo) * 128:(k - lo + 1) * 128],
                                                     Pm[:, k * 128:(k + 1) * 128], c_id[:]) if E else None
                                P.inc(i2, "t", ("pt", g, lh, qt, bat))
                            P.wait(e, ("rrowev", g, lh, qt))
                            i2 = e.matmul(RBp[:, 0:128], c_ones[:], rrow[par][:], start=True, stop=True) if E else None
                            P.inc(i2, "t", ("rbmm", g, lh, qt))
                            for bat in range(nbat):
                                lo = bat * 8
                                hi = min(nkb, lo + 8)
                                pslot = ptpar[(g, lh, qt, bat)]
                                P.wait(e, ("ptev", g, lh, qt, bat))
                                for k in range(lo, hi):
                                    i2 = e.matmul(AVp[:, 0:128], vbuf[kvmap[(g, lh)] % 2][:, k, :],
                                                  PTs[pslot][:, (k - lo) * 128:(k - lo + 1) * 128],
                                                  start=(k == 0), stop=(k == nkb - 1)) if E else None
                                P.inc(i2, "t", ("avb", g, lh, qt, bat))

                        emit_qk(0, 0)
                        emit_qk(0, 1)
                        emit_qk(1, 0)
                        for i in range(NT):
                            if i + 2 < NT:
                                emit_qk(i + 2, 0)
                            if i + 1 < NT:
                                emit_qk(i + 1, 1)
                            emit_ptav(i)

                    def p4_scalar(e):
                        def emit_sev(i):
                            g, lh, qt = TILES[i]
                            j, ext, n512, nkb, nbat = geom(g, qt)
                            pt_ = prev_tile(g, lh, qt)
                            first = True
                            for b in range(n512):
                                if j % 2 == 0 and b == n512 - 1:
                                    continue
                                P.wait(e, ("qk", g, lh, qt, b))
                                if first and pt_ is not None:
                                    P.wait(e, ("cmax",) + pt_)
                                    P.wait(e, ("exp",) + pt_)
                                first = False
                                bank = 2 * (qt % 2) + (b % 2)
                                N = blkN(j, b, n512, ext)
                                Nc = N - 256 if b == n512 - 1 else N
                                i2 = e.copy(Ssb[:, b * 512:b * 512 + Nc], Sps[bank][:, 0:Nc]) if E else None
                                P.inc(i2, "a", ("sev", g, lh, qt, b))

                        def emit_exp(i):
                            g, lh, qt = TILES[i]
                            j, ext, n512, nkb, nbat = geom(g, qt)
                            pt_ = prev_tile(g, lh, qt)
                            P.wait(e, ("nm1", g, lh, qt))
                            P.wait(e, ("dstt", g, lh, qt))
                            if pt_ is not None:
                                P.wait(e, ("pmask",) + pt_)
                            i2 = e.activation(praw[:, 0:ext], Ssb[:, 0:ext], AF.Exp,
                                              bias=nm1[i % 2][:], scale=1.0) if E else None
                            P.inc(i2, "a", ("exp", g, lh, qt))

                        def emit_aux(i):
                            g, lh, qt = TILES[i]
                            j, ext, n512, nkb, nbat = geom(g, qt)
                            par = i % 2
                            P.wait(e, ("rbt", g, lh, qt))
                            i2 = e.copy(rrow[par][:], RBp[0:1, 128:256]) if E else None
                            P.inc(i2, "a", ("rrowev", g, lh, qt))
                            for bat in range(nbat):
                                lo = bat * 8
                                hi = min(nkb, lo + 8)
                                pslot = ptpar[(g, lh, qt, bat)]
                                P.wait(e, ("pt", g, lh, qt, bat))
                                P.buf_claim(e, ("PTs", pslot), ("avb", g, lh, qt, bat))
                                i2 = e.copy(PTs[pslot][:, 0:(hi - lo) * 128],
                                            PTp[pslot][:, 0:(hi - lo) * 128]) if E else None
                                P.inc(i2, "a", ("ptev", g, lh, qt, bat))

                        emit_sev(0)
                        for i in range(NT):
                            emit_exp(i)
                            if i + 1 < NT:
                                emit_sev(i + 1)
                            emit_aux(i)

                    def p4_vector(e):
                        def emit_s3(i):
                            g, lh, qt = TILES[i]
                            j, ext, n512, nkb, nbat = geom(g, qt)
                            par = i % 2
                            pt_ = prev_tile(g, lh, qt)
                            i2 = e.memset(cmax[par][:], NEG) if E else None
                            P.inc(i2, "v", ("cm0", g, lh, qt))
                            i2 = e.memset(K16[par][:, 0:8], 1.0) if E else None
                            P.inc(i2, "v", ("k160", g, lh, qt))
                            b = n512 - 1
                            bank = 2 * (qt % 2) + (b % 2)
                            N = blkN(j, b, n512, ext)
                            P.wait(e, ("qk", g, lh, qt, b))
                            if pt_ is not None:
                                P.wait(e, ("exp",) + pt_)
                            if j % 2 == 1:
                                P.wait(e, ("sev", g, lh, qt, b))
                            off = N - 256
                            i2 = e.scalar_tensor_tensor(Ssb[:, ext - 256:ext], Sps[bank][:, off:off + 256],
                                                        1.0, c_tri[:], OP.mult, OP.add) if E else None
                            P.inc(i2, "v", ("dstt", g, lh, qt))
                            if n512 >= 2:
                                lastsev = n512 - 2 if (j % 2 == 0) else n512 - 1
                                P.wait(e, ("sev", g, lh, qt, lastsev))
                            P.wait(e, ("dstt", g, lh, qt))
                            i2 = e.tensor_reduce(cmax[par][:, 0:16 * (j + 1)],
                                                 Ssb[:, 0:ext].rearrange("p (c s) -> p c s", s=16),
                                                 AX.X, OP.max) if E else None
                            P.inc(i2, "v", ("cmax", g, lh, qt))
                            P.wait(e, ("cmax", g, lh, qt))
                            i2 = e.max(m8[par][:], cmax[par][:]) if E else None
                            P.inc(i2, "v", ("max8", g, lh, qt))
                            P.wait(e, ("max8", g, lh, qt))
                            i2 = e.tensor_scalar(K16[par][:, 8:256], cmax[par][:, 8:256],
                                                 m8[par][:, 7:8], None, OP.is_ge) if E else None
                            P.inc(i2, "v", ("k16", g, lh, qt))
                            i2 = e.tensor_scalar(nm1[par][:], m8[par][:, 0:1], -1.0, None, OP.mult) if E else None
                            P.inc(i2, "v", ("nm1", g, lh, qt))
                            P.wait(e, ("k16", g, lh, qt))
                            i2 = e.tensor_tensor(KbD[par][:], bc16(K16[par][:, 16 * j:16 * j + 16]),
                                                 c_rec[:].rearrange("p (c s) -> p c s", s=16), OP.max) if E else None
                            P.inc(i2, "v", ("kbd", g, lh, qt))
                            if j >= 1:
                                i2 = e.tensor_tensor(KbP[par][:], bc16(K16[par][:, 16 * j - 8:16 * j]),
                                                     c_utp[:].rearrange("p (c s) -> p c s", s=16), OP.max) if E else None
                                P.inc(i2, "v", ("kbp", g, lh, qt))

                        def emit_s5(i):
                            g, lh, qt = TILES[i]
                            j, ext, n512, nkb, nbat = geom(g, qt)
                            par = i % 2
                            P.wait(e, ("exp", g, lh, qt))
                            P.wait(e, ("kbd", g, lh, qt))
                            if j >= 1:
                                P.wait(e, ("kbp", g, lh, qt))
                            P.buf_claim(e, "Pm", ("pt", g, lh, qt, nbat - 1))
                            if j >= 1:
                                i2 = e.scalar_tensor_tensor(
                                    Pm[:, 0:256 * j - 128].rearrange("p (c s) -> p c s", s=16),
                                    praw[:, 0:256 * j - 128].rearrange("p (c s) -> p c s", s=16),
                                    1.0, bc16(K16[par][:, 0:16 * j - 8]),
                                    OP.mult, OP.mult, accum_out=racc[par][:, 0:1]) if E else None
                                i2 = e.scalar_tensor_tensor(
                                    Pm[:, 256 * j - 128:256 * j], praw[:, 256 * j - 128:256 * j],
                                    1.0, KbP[par][:], OP.mult, OP.mult, accum_out=racc[par][:, 1:2]) if E else None
                            i2 = e.scalar_tensor_tensor(
                                Pm[:, 256 * j:ext], praw[:, 256 * j:ext],
                                1.0, KbD[par][:], OP.mult, OP.mult, accum_out=racc[par][:, 2:3]) if E else None
                            P.inc(i2, "v", ("pmask", g, lh, qt))
                            P.wait(e, ("pmask", g, lh, qt))
                            if j >= 1:
                                i2 = e.tensor_reduce(rr[par][:], racc[par][:], AX.X, OP.add) if E else None
                            else:
                                i2 = e.tensor_copy(rr[par][:], racc[par][:, 2:3]) if E else None
                            P.inc(i2, "v", ("rsum", g, lh, qt))
                            P.wait(e, ("rsum", g, lh, qt))
                            i2 = e.reciprocal(rcp[par][:], rr[par][:]) if E else None
                            P.inc(i2, "v", ("rcp", g, lh, qt))

                        def emit_fin(i):
                            g, lh, qt = TILES[i]
                            j, ext, n512, nkb, nbat = geom(g, qt)
                            par = i % 2
                            P.wait(e, ("rbmm", g, lh, qt))
                            i2 = e.tensor_copy(Rbs[par][:], RBp[:, 0:128]) if E else None
                            P.inc(i2, "v", ("rbev", g, lh, qt))
                            P.wait(e, ("rbev", g, lh, qt))
                            P.wait(e, ("avb", g, lh, qt, nbat - 1))
                            wsem = "wot0" if i % 2 == 0 else "wot1"
                            P.wait_min(e, wsem, 16 * (i // 2))
                            i2 = e.tensor_tensor(ob[i % 2][:], AVp[:, 0:128], Rbs[par][:], OP.mult) if E else None
                            P.inc(i2, "v", ("oev", g, lh, qt))

                        emit_s3(0)
                        for i in range(NT):
                            if i + 1 < NT:
                                emit_s3(i + 1)
                            emit_s5(i)
                            if i >= 1:
                                emit_fin(i - 1)
                        emit_fin(NT - 1)

                    eng_sections(dict(sync=p4_sync, tensor=p4_tensor,
                                        scalar=p4_scalar, vector=p4_vector))
                    barrier()

            if upto >= 6:
                with contextlib.ExitStack() as ph:
                    wob = (ph.enter_context(nc.sbuf_tensor("wob", [128, HC, 512], BF16)) if E else None)
                    wub = (ph.enter_context(nc.sbuf_tensor("wub", [128, HC, 512], BF16)) if E else None)
                    Ab = (ph.enter_context(nc.sbuf_tensor("Ab", [128, HC, 128], BF16)) if E else None)
                    Bb = (ph.enter_context(nc.sbuf_tensor("Bb", [128, HID], BF16)) if E else None)
                    otb = [(ph.enter_context(nc.sbuf_tensor(f"otb{i}", [128, 32, 128], BF16)) if E else None) for i in range(2)]
                    xnl = [(ph.enter_context(nc.sbuf_tensor(f"xnl{i}", [128, HC, 128], BF16)) if E else None) for i in range(2)]
                    xsl = [(ph.enter_context(nc.sbuf_tensor(f"xsl{i}", [128, 512], F32)) if E else None) for i in range(2)]
                    zT = [(ph.enter_context(nc.sbuf_tensor(f"zT{i}", [128, 128], BF16)) if E else None) for i in range(NVT)]
                    lg = (ph.enter_context(nc.sbuf_tensor("lg", [128, 4], F32)) if E else None)
                    mmx = (ph.enter_context(nc.sbuf_tensor("mmx", [128, 1], F32)) if E else None)
                    ge4 = (ph.enter_context(nc.sbuf_tensor("ge4", [128, 4], F32)) if E else None)
                    oh4 = (ph.enter_context(nc.sbuf_tensor("oh4", [128, 4], F32)) if E else None)
                    sc1 = (ph.enter_context(nc.sbuf_tensor("sc1", [128, 1], F32)) if E else None)
                    sc2 = (ph.enter_context(nc.sbuf_tensor("sc2", [128, 1], F32)) if E else None)
                    zsel = (ph.enter_context(nc.sbuf_tensor("zsel", [128, 128], BF16)) if E else None)
                    gsl = (ph.enter_context(nc.sbuf_tensor("gsl", [128, 512], F32)) if E else None)
                    ssl2 = (ph.enter_context(nc.sbuf_tensor("ssl2", [128, 512], F32)) if E else None)
                    sg = (ph.enter_context(nc.sbuf_tensor("sg", [128, 512], F32)) if E else None)
                    yb = (ph.enter_context(nc.sbuf_tensor("yb", [128, 512], F32)) if E else None)
                    spsP = (ph.enter_context(nc.psum_tensor("spsP", [128, 512], F32)) if E else None)
                    fpsP = (ph.enter_context(nc.psum_tensor("fpsP", [128, 512], F32)) if E else None)
                    zpsP = (ph.enter_context(nc.psum_tensor("zpsP", [128, 512], F32)) if E else None)
                    ztpP = (ph.enter_context(nc.psum_tensor("ztpP", [128, 512], BF16)) if E else None)

                    def p6_gp(e):
                        i = e.dma_start(out=Ab[:], in_=A_padT.rearrange("(c p) n -> p c n", p=128)) if E else None
                        P.inc(i, "gab", ("gA",), 16)
                        i = e.dma_start(out=Bb[:], in_=B_padT) if E else None
                        P.inc(i, "gab", ("gB",), 16)
                        for mb in range(8):
                            if mb >= 1:
                                P.wait(e, ("op", mb - 1, NVT - 1))
                                P.wait(e, ("fp", mb - 1, NVT - 1))
                            if E:
                                i = e.dma_start(out=wob[:],
                                                in_=woT.rearrange("(c p) m -> p c m", p=128)[:, :, mb * 512:(mb + 1) * 512])
                            else:
                                i = None
                            P.inc(i, "gwo", ("gwo", mb), 16)
                            if E:
                                i = e.dma_start(out=wub[:],
                                                in_=wupT.rearrange("(c p) m -> p c m", p=128)[:, :, mb * 512:(mb + 1) * 512])
                            else:
                                i = None
                            P.inc(i, "gwup", ("gwup", mb), 16)
                            for lt in range(NVT):
                                idx = mb * NVT + lt
                                P.wait(e, ("sev6", mb, lt))
                                P.wait(e, ("silu", mb, lt))
                                i = e.tensor_tensor(sg[:], ssl2[:], gsl[:], OP.mult) if E else None
                                P.inc(i, "p", ("sg", mb, lt))
                                P.wait(e, ("sg", mb, lt))
                                P.wait_min(e, "wy", 16 * idx)
                                P.wait_min(e, "dxs0" if idx % 2 == 0 else "dxs1", 16 * (idx // 2 + 1))
                                i = e.tensor_tensor(yb[:], sg[:], xsl[idx % 2][:], OP.add) if E else None
                                P.inc(i, "p", ("yb", mb, lt))

                    def p6_sync(e):
                        P.wait_min(e, "wot0", 16 * 256)
                        P.wait_min(e, "wot1", 16 * 256)
                        P.wait_min(e, "wxno", 16 * 4)
                        n = 0
                        for mb in range(8):
                            for lt in range(NVT):
                                slot = n % 2
                                if n >= 2:
                                    pmb, plt = divmod(n - 2, NVT)
                                    P.wait(e, ("op", pmb, plt))
                                    P.wait(e, ("fp", pmb, plt))
                                    P.wait(e, ("yb", pmb, plt))
                                if E:
                                    i = e.dma_start(out=otb[slot][:], in_=oT_d[lt].rearrange("h d q -> d h q"))
                                else:
                                    i = None
                                P.inc(i, "dot0" if slot == 0 else "dot1", ("p6o", mb, lt), 16)
                                if E:
                                    dd = xnoT_d.rearrange("(c p) s -> p c s", p=128)
                                    i = e.dma_start(out=xnl[slot][:], in_=dd[:, :, lt * 128:(lt + 1) * 128])
                                else:
                                    i = None
                                P.inc(i, "dxl0" if slot == 0 else "dxl1", ("p6n", mb, lt), 16)
                                i = e.dma_start(out=xsl[slot][:],
                                                in_=x_own[lt * 128:(lt + 1) * 128, mb * 512:(mb + 1) * 512]) if E else None
                                P.inc(i, "dxs0" if slot == 0 else "dxs1", ("p6x", mb, lt), 16)
                                n += 1

                    def p6_tensor(e):
                        P.wait_min(e, "gab", 32)
                        n = 0
                        for mb in range(8):
                            P.wait_min(e, "gwo", 16 * (mb + 1))
                            P.wait_min(e, "gwup", 16 * (mb + 1))
                            for lt in range(NVT):
                                slot = n % 2
                                sname_o = "dot0" if slot == 0 else "dot1"
                                sname_n = "dxl0" if slot == 0 else "dxl1"
                                P.wait_min(e, sname_o, 16 * (n // 2 + 1))
                                P.wait_min(e, sname_n, 16 * (n // 2 + 1))
                                if mb == 0:
                                    P.psum_claim(e, ("z",), [("zsel", lt)])
                                    for hc in range(HC):
                                        i = e.matmul(zpsP[:, 0:128], xnl[slot][:, hc, :], Ab[:, hc, :],
                                                     start=(hc == 0), stop=(hc == HC - 1)) if E else None
                                    P.inc(i, "t", ("zmm", lt))
                                    P.wait(e, ("zsel", lt))
                                    P.psum_claim(e, ("zt",), [("ztev", lt)])
                                    i = e.transpose(ztpP[:, 0:128], zsel[:], c_id[:]) if E else None
                                    P.inc(i, "t", ("ztr", lt))
                                P.psum_claim(e, ("sps",), [("sev6", mb, lt)])
                                for hd in range(NQH):
                                    i = e.matmul(spsP[:], otb[slot][:, hd, :], wob[:, hd, :],
                                                 start=(hd == 0), stop=(hd == NQH - 1)) if E else None
                                P.inc(i, "t", ("op", mb, lt))
                                P.psum_claim(e, ("fps",), [("silu", mb, lt)])
                                P.wait(e, ("ztev", lt))
                                for hc in range(HC):
                                    i = e.matmul(fpsP[:], xnl[slot][:, hc, :], wub[:, hc, :],
                                                 start=(hc == 0), stop=False) if E else None
                                i = e.matmul(fpsP[:], zT[lt][:], Bb[:, mb * 512:(mb + 1) * 512],
                                             start=False, stop=True) if E else None
                                P.inc(i, "t", ("fp", mb, lt))
                                n += 1

                    def p6_vector(e):
                        for lt in range(NVT):
                            P.wait(e, ("zmm", lt))
                            if E:
                                zz = zpsP[:, 0:128].rearrange("p (e r) -> p e r", r=32)[:, :, 31:32]
                                cb = bass.AP(c_bb[:].tensor, c_bb[:].offset, list(c_bb[:].ap) + [[0, 1]])
                                i = e.tensor_tensor(lg[:], zz, cb, OP.add)
                            else:
                                i = None
                            P.inc(i, "v", ("lg", lt))
                            P.wait(e, ("lg", lt))
                            i = e.tensor_reduce(mmx[:], lg[:], AX.X, OP.max) if E else None
                            P.inc(i, "v", ("lmx", lt))
                            P.wait(e, ("lmx", lt))
                            i = e.tensor_scalar(ge4[:], lg[:], mmx[:], None, OP.is_ge) if E else None
                            P.inc(i, "v", ("ge4", lt))
                            P.wait(e, ("ge4", lt))
                            i = e.tensor_copy(oh4[:, 0:1], ge4[:, 0:1]) if E else None
                            P.inc(i, "v", ("oh0", lt))
                            P.wait(e, ("oh0", lt))
                            i = e.tensor_scalar(sc1[:], oh4[:, 0:1], -1.0, 1.0, OP.mult, OP.add) if E else None
                            P.inc(i, "v", ("nh0", lt))
                            P.wait(e, ("nh0", lt))
                            i = e.tensor_tensor(oh4[:, 1:2], ge4[:, 1:2], sc1[:], OP.mult) if E else None
                            P.inc(i, "v", ("oh1", lt))
                            P.wait(e, ("oh1", lt))
                            i = e.tensor_scalar(sc2[:], oh4[:, 1:2], -1.0, 1.0, OP.mult, OP.add) if E else None
                            P.inc(i, "v", ("nh1", lt))
                            P.wait(e, ("nh1", lt))
                            i = e.tensor_tensor(sc1[:], sc1[:], sc2[:], OP.mult) if E else None
                            P.inc(i, "v", ("n01", lt))
                            P.wait(e, ("n01", lt))
                            i = e.tensor_tensor(oh4[:, 2:3], ge4[:, 2:3], sc1[:], OP.mult) if E else None
                            P.inc(i, "v", ("oh2", lt))
                            P.wait(e, ("oh2", lt))
                            i = e.tensor_scalar(sc2[:], oh4[:, 2:3], -1.0, 1.0, OP.mult, OP.add) if E else None
                            P.inc(i, "v", ("nh2", lt))
                            P.wait(e, ("nh2", lt))
                            i = e.tensor_tensor(sc1[:], sc1[:], sc2[:], OP.mult) if E else None
                            P.inc(i, "v", ("n012", lt))
                            P.wait(e, ("n012", lt))
                            i = e.tensor_tensor(oh4[:, 3:4], ge4[:, 3:4], sc1[:], OP.mult) if E else None
                            P.inc(i, "v", ("oh3", lt))
                            P.wait(e, ("oh3", lt))
                            for ex in range(4):
                                i = e.tensor_scalar(zsel[:, ex * 32:(ex + 1) * 32], zpsP[:, ex * 32:(ex + 1) * 32],
                                                    oh4[:, ex:ex + 1], None, OP.mult) if E else None
                            P.inc(i, "v", ("zsel", lt))
                            P.wait(e, ("ztr", lt))
                            i = e.tensor_copy(zT[lt][:], ztpP[:, 0:128]) if E else None
                            P.inc(i, "v", ("ztev", lt))

                    def p6_scalar(e):
                        for mb in range(8):
                            for lt in range(NVT):
                                idx = mb * NVT + lt
                                P.wait(e, ("op", mb, lt))
                                if idx >= 1:
                                    pmb, plt = divmod(idx - 1, NVT)
                                    P.wait(e, ("sg", pmb, plt))
                                i = e.copy(ssl2[:], spsP[:]) if E else None
                                P.inc(i, "a", ("sev6", mb, lt))
                                P.wait(e, ("fp", mb, lt))
                                i = e.activation(gsl[:], fpsP[:], AF.Silu) if E else None
                                P.inc(i, "a", ("silu", mb, lt))
                                P.wait(e, ("yb", mb, lt))
                                i = e.dma_start(out=y_out[lt * 128:(lt + 1) * 128, mb * 512:(mb + 1) * 512],
                                                in_=yb[:]) if E else None
                                P.inc(i, "wy", ("yst", mb, lt), 16)

                    eng_sections(dict(gpsimd=p6_gp, sync=p6_sync, tensor=p6_tensor,
                                        vector=p6_vector, scalar=p6_scalar))

                    def fin_sync(e):
                        P.wait_min(e, "wy", 16 * 128)
                    eng_sections(dict(sync=fin_sync))

            if upto < 6:
                def dbg_fin(e):
                    for s in ["wxnt", "wxno", "wkt", "wvs0", "wvs1", "wqt",
                              "wot0", "wot1"]:
                        if P.cnt.get(s):
                            P.wait_min(e, s, P.cnt[s])
                eng_sections(dict(sync=dbg_fin))

        P.begin(False)
        run()
        P.begin(True)
        run()

    return nc


_CACHED = {}


def kernel(L0_attn_hidden_states, L0_attn_wq, L0_attn_wk, L0_attn_wv, L0_attn_wo,
           L0_ffn_W_up, L0_ffn_gate_w, L0_ffn_expert_bias, L0_ffn_lora_A, L0_ffn_lora_B):
    x = np.ascontiguousarray(np.asarray(L0_attn_hidden_states, dtype=np.float32))
    wq = np.asarray(L0_attn_wq, np.float32)
    wk = np.asarray(L0_attn_wk, np.float32)
    wv = np.asarray(L0_attn_wv, np.float32)
    wo = np.asarray(L0_attn_wo, np.float32)
    wup = np.asarray(L0_ffn_W_up, np.float32)
    gw = np.asarray(L0_ffn_gate_w, np.float32)
    eb = np.asarray(L0_ffn_expert_bias, np.float32)
    lA = np.asarray(L0_ffn_lora_A, np.float32)
    lB = np.asarray(L0_ffn_lora_B, np.float32)

    inv = 1.0 / (10000.0 ** (np.arange(0, DH, 2, dtype=np.float32) / DH))
    ang = np.arange(SEQ, dtype=np.float32)[:, None] * inv[None, :]
    emb = np.concatenate([ang, ang], axis=1)
    cosf = np.cos(emb).astype(np.float32)
    sinf = np.sin(emb).astype(np.float32)
    scale = 1.0 / math.sqrt(DH)
    identb = np.eye(128, dtype=ml_dtypes.bfloat16)
    R = np.zeros((128, 128), np.float32)
    R[0:64, 64:128] = -np.eye(64)
    R[64:128, 0:64] = np.eye(64)
    rotT = np.ascontiguousarray(R.T).astype(ml_dtypes.bfloat16)
    ones1 = np.ones((1, 128), np.float32)

    A_padT = np.zeros((HID, 128), np.float32)
    for ex in range(4):
        A_padT[:, ex * 32:ex * 32 + 8] = lA[ex].T
        A_padT[:, ex * 32 + 31] = gw[ex]
    B_padT = np.zeros((128, HID), np.float32)
    for ex in range(4):
        B_padT[ex * 32:ex * 32 + 8, :] = lB[ex].T
    ebias = eb.reshape(1, 4).astype(np.float32)

    pcol = np.arange(128)[:, None]
    j1 = np.arange(256)[None, :]
    j2 = np.arange(128)[None, :]

    wqT = np.ascontiguousarray(wq.T)
    wkT = np.ascontiguousarray(wk.T)
    wvT = np.ascontiguousarray(wv.T)
    woT = np.ascontiguousarray(wo.T)
    wupT = np.ascontiguousarray(wup.T)

    in_maps = []
    for c in range(8):
        b, h = divmod(c, 2)
        own = np.arange(OWN) * 2 + h
        trim = np.where(j1 <= 2 * pcol + h, 0.0, NEG).astype(np.float32)
        rec = (j1 > 2 * pcol + h - 128).astype(np.float32)
        utp = (j2 > 2 * pcol + h).astype(np.float32)
        in_maps.append(dict(
            x_full=x[b],
            x_own=np.ascontiguousarray(x[b, own]),
            wqT=wqT, wkT=wkT, wvT=wvT, woT=woT, wupT=wupT,
            A_padT=A_padT, B_padT=B_padT, ebias=ebias,
            cosq=np.ascontiguousarray((cosf[own] * scale).T),
            sinq=np.ascontiguousarray((sinf[own] * scale).T),
            cosk=np.ascontiguousarray(cosf.T),
            sink=np.ascontiguousarray(sinf.T),
            identb=identb, identf=np.eye(128, dtype=np.float32), rotT=rotT,
            trimask2=trim, recent2=rec, ut2prev=utp,
            ones1=ones1,
        ))

    if "prog" not in _CACHED:
        _CACHED["prog"] = build_program()
    nc = _CACHED["prog"]

    res = run_bass_kernel_spmd(nc, in_maps, core_ids=list(range(8)))

    out = np.empty((NB, SEQ, HID), np.float32)
    for c in range(8):
        b, h = divmod(c, 2)
        out[b, h::2] = np.asarray(res.results[c]["y"], np.float32)
    return out


# version 06
# speedup vs baseline: 1.0090x; 1.0090x over previous
import contextlib
import math

import numpy as np
import ml_dtypes

import concourse.bass as bass
import concourse.mybir as mybir
from concourse.alu_op_type import AluOpType as OP
from concourse.bass_utils import run_bass_kernel_spmd

F32 = mybir.dt.float32
BF16 = mybir.dt.bfloat16
AF = mybir.ActivationFunctionType
AX = mybir.AxisListType

NEG = float(np.finfo(np.float32).min)
HID = 4096
SEQ = 4096
NB = 4
NQH = 32
NKV = 8
DH = 128
OWN = 2048
NVT = 16
NGRP = 4
HC = 32


def bc16(ap, reps=16):
    return bass.AP(ap.tensor, ap.offset, list(ap.ap) + [[0, reps]])


class Prog:
    def __init__(self):
        self.plan = {}
        self.emitting = False
        self.cnt = {}
        self.sem = {}

    def begin(self, emitting):
        self.emitting = emitting
        self.cnt = {}
        self.psum_readers = {}
        self.buf_consumer = {}

    def inc(self, inst, semname, key, n=1):
        c = self.cnt.get(semname, 0) + n
        self.cnt[semname] = c
        if self.emitting:
            assert self.plan[key] == (semname, c), (key, semname, c, self.plan.get(key))
            inst.then_inc(self.sem[semname], n)
        else:
            assert key not in self.plan, key
            self.plan[key] = (semname, c)

    def wait(self, eng, key):
        if self.emitting:
            semname, c = self.plan[key]
            eng.wait_ge(self.sem[semname], c)

    def wait_min(self, eng, semname, c):
        if self.emitting and c > 0:
            eng.wait_ge(self.sem[semname], c)

    def psum_claim(self, eng, bank, reader_keys):
        prev = self.psum_readers.get(bank)
        if prev:
            for k in prev:
                self.wait(eng, k)
        self.psum_readers[bank] = list(reader_keys)

    def buf_claim(self, eng, buf, consumer_key):
        prev = self.buf_consumer.get(buf)
        if prev is not None:
            self.wait(eng, prev)
        self.buf_consumer[buf] = consumer_key


def geom(g, qt):
    j = 4 * g + qt
    ext = 256 * (j + 1)
    n512 = (j + 2) // 2
    nkb = 2 * (j + 1)
    nbat = (nkb + 7) // 8
    return j, ext, n512, nkb, nbat


def blkN(j, b, n512, ext):
    return 512 if b < n512 - 1 else ext - 512 * (n512 - 1)


def build_program(upto=99, debug=False):
    nc = bass.Bass("TRN2", target_bir_lowering=False)

    def din(name, shape, dt=F32):
        return nc.dram_tensor(name, list(shape), dt, kind="ExternalInput").ap()

    x_full = din("x_full", (SEQ, HID))
    x_own = din("x_own", (OWN, HID))
    wqT = din("wqT", (HID, NQH * DH))
    wkT = din("wkT", (HID, NKV * DH))
    wvT = din("wvT", (HID, NKV * DH))
    woT = din("woT", (HID, HID))
    wupT = din("wupT", (HID, HID))
    A_padT = din("A_padT", (HID, 128))
    B_padT = din("B_padT", (128, HID))
    ebias = din("ebias", (1, 4))
    cosq = din("cosq", (DH, OWN))
    sinq = din("sinq", (DH, OWN))
    cosk = din("cosk", (DH, SEQ))
    sink = din("sink", (DH, SEQ))
    identb = din("identb", (128, 128), BF16)
    identf = din("identf", (128, 128))
    rotT = din("rotT", (128, 128), BF16)
    trimask2 = din("trimask2", (128, 256))
    recent2 = din("recent2", (128, 256))
    ut2prev = din("ut2prev", (128, 128))
    ones1 = din("ones1", (1, 128))

    y_out = nc.dram_tensor("y", [OWN, HID], F32, kind="ExternalOutput").ap()

    KIND = "ExternalOutput" if debug else "Internal"
    xnT_d = nc.dram_tensor("xnT_d", [HID, SEQ], BF16, kind=KIND).ap()
    xnoT_d = nc.dram_tensor("xnoT_d", [HID, OWN], BF16, kind=KIND).ap()
    qT_d = nc.dram_tensor("qT_d", [NQH, DH, OWN], BF16, kind=KIND).ap()
    kT_d = nc.dram_tensor("kT_d", [NKV, DH, SEQ], BF16, kind=KIND).ap()
    vS_d = nc.dram_tensor("vS_d", [NKV, SEQ, DH], BF16, kind=KIND).ap()
    oT_d = nc.dram_tensor("oT_d", [NVT, NQH, DH, 128], BF16, kind=KIND).ap()
    wo_bf_d = nc.dram_tensor("wo_bf_d", [HID, HID], BF16, kind="Internal").ap()
    wup_bf_d = nc.dram_tensor("wup_bf_d", [HID, HID], BF16, kind="Internal").ap()

    P = Prog()

    semnames = [
        "t", "v", "a", "p", "bar", "dcst",
        "dx0", "dx1", "dxo0", "dxo1", "wxnt", "wxno",
        "gwk", "gwv", "gwq", "gwo", "gwup", "gab", "gcwo", "gcwup", "gwo1", "gwup1",
        "dnb0", "dnb1",
        "wkt", "wvs0", "wvs1", "wqt",
        "dk0", "dk1", "dk2", "dv0", "dv1", "dv2", "dq0", "dq1", "dq2", "dq3", "dq4", "dq5", "dq6", "dq7",
        "wot0", "wot1",
        "dot0", "dot1", "dxl0", "dxl1", "dxs0", "dxs1",
        "wy",
    ]

    with contextlib.ExitStack() as top:
        for s in semnames:
            P.sem[s] = top.enter_context(nc.semaphore(s))
        block = top.enter_context(nc.Block())

        c_id = top.enter_context(nc.sbuf_tensor("c_id", [128, 128], BF16))
        c_idf = top.enter_context(nc.sbuf_tensor("c_idf", [128, 128], F32))
        c_rt = top.enter_context(nc.sbuf_tensor("c_rt", [128, 128], BF16))
        c_tri = top.enter_context(nc.sbuf_tensor("c_tri", [128, 256], F32))
        c_rec = top.enter_context(nc.sbuf_tensor("c_rec", [128, 256], F32))
        c_utp = top.enter_context(nc.sbuf_tensor("c_utp", [128, 128], F32))
        c_ones = top.enter_context(nc.sbuf_tensor("c_ones", [1, 128], F32))
        c_eb = top.enter_context(nc.sbuf_tensor("c_eb", [1, 4], F32))
        c_bb = top.enter_context(nc.sbuf_tensor("c_bb", [128, 4], F32))
        c_eps = top.enter_context(nc.sbuf_tensor("c_eps", [128, 1], F32))
        c_dmv = top.enter_context(nc.sbuf_tensor("c_dmv", [128, 1], F32))
        c_dma_ = top.enter_context(nc.sbuf_tensor("c_dma_", [128, 1], F32))
        c_dmg = top.enter_context(nc.sbuf_tensor("c_dmg", [128, 1], F32))

        barcnt = [0]

        def run():
            E = P.emitting
            barcnt[0] = 0

            def eng_sections(fns):
                engines = {}
                if E:
                    for nm in ["sync", "gpsimd", "tensor", "scalar", "vector"]:
                        f = fns.get(nm)
                        dec = getattr(block, nm)

                        def wrapper(eng, _f=f, _nm=nm):
                            engines[_nm] = eng
                            if _f:
                                _f(eng)
                        dec(wrapper)
                else:
                    for nm in ["sync", "gpsimd", "tensor", "scalar", "vector"]:
                        engines[nm] = None
                        f = fns.get(nm)
                        if f:
                            f(None)
                return engines

            def barrier():
                barcnt[0] += 1
                tgt = 5 * barcnt[0]
                pfx = {"sync": "d", "scalar": "w", "gpsimd": "g"}

                def mk(nm, op):
                    def f(e):
                        pf = pfx.get(nm)
                        if pf:
                            for s in P.sem:
                                if s.startswith(pf) and P.cnt.get(s):
                                    P.wait_min(e, s, P.cnt[s])
                        i = op(e) if P.emitting else None
                        P.inc(i, "bar", ("bar", barcnt[0], nm), 1)
                        P.wait_min(e, "bar", tgt)
                    return f
                eng_sections(dict(
                    sync=mk("sync", lambda e: e.nop()),
                    gpsimd=mk("gpsimd", lambda e: e.memset(c_dmg[:], 0.0)),
                    tensor=mk("tensor", lambda e: e.nop()),
                    scalar=mk("scalar", lambda e: e.memzero(c_dma_[:])),
                    vector=mk("vector", lambda e: e.memset(c_dmv[:], 0.0)),
                ))

            with contextlib.ExitStack() as ph:
                bbp = (ph.enter_context(nc.psum_tensor("bbp", [128, 512], F32)) if E else None)

                def p0_sync(e):
                    loads = [
                        (c_id, identb), (c_rt, rotT), (c_tri, trimask2),
                        (c_rec, recent2), (c_utp, ut2prev), (c_ones, ones1),
                        (c_eb, ebias), (c_idf, identf),
                    ]
                    for i2, (dst, src) in enumerate(loads):
                        inst = e.dma_start(out=dst[:], in_=src) if E else None
                        P.inc(inst, "dcst", ("dcst", i2), 16)

                def p0_gp(e):
                    for w_src, w_dst, tag in [(woT, wo_bf_d, "cwo"), (wupT, wup_bf_d, "cwup")]:
                        for ch in range(8):
                            i = e.dma_start(out=w_dst[ch * 512:(ch + 1) * 512, :],
                                            in_=w_src[ch * 512:(ch + 1) * 512, :],
                                            max_dma_last_dim=8192) if E else None
                            P.inc(i, "gcwo" if tag == "cwo" else "gcwup", (tag, ch), 16)

                def p0_tensor(e):
                    P.wait_min(e, "dcst", 16 * 8)
                    i = e.matmul(bbp[:, 0:4], c_ones[:], c_eb[:], start=True, stop=True) if E else None
                    P.inc(i, "t", ("bbmm",))

                def p0_vector(e):
                    P.wait(e, ("bbmm",))
                    i = e.tensor_copy(c_bb[:], bbp[:, 0:4]) if E else None
                    P.inc(i, "v", ("bbev",))
                    i = e.memset(c_eps[:], 1e-5) if E else None
                    P.inc(i, "v", ("epss",))

                eng_sections(dict(sync=p0_sync, gpsimd=p0_gp, tensor=p0_tensor, vector=p0_vector))
                barrier()

            if upto >= 1:
                with contextlib.ExitStack() as ph:
                    xb = [(ph.enter_context(nc.sbuf_tensor(f"xb{i}", [128, HID], F32)) if E else None) for i in range(2)]
                    stats = (ph.enter_context(nc.sbuf_tensor("stats", [128, 8, 6], F32)) if E else None)
                    mv = (ph.enter_context(nc.sbuf_tensor("mv", [128, 2], F32)) if E else None)
                    nmn = (ph.enter_context(nc.sbuf_tensor("nmn", [128, 1], F32)) if E else None)
                    lnv = (ph.enter_context(nc.sbuf_tensor("lnv", [128, 1], F32)) if E else None)
                    rstd = (ph.enter_context(nc.sbuf_tensor("rstd", [128, 1], F32)) if E else None)
                    xnb = (ph.enter_context(nc.sbuf_tensor("xnb", [128, HID], BF16)) if E else None)
                    col = (ph.enter_context(nc.sbuf_tensor("col", [128, HC, 4, 128], BF16)) if E else None)
                    tp = [(ph.enter_context(nc.psum_tensor(f"p1tp{i}", [128, 1024], BF16)) if E else None) for i in range(2)]

                    jobs = [
                        ("full", x_full, 32, xnT_d, "dx0", "dx1", "wxnt"),
                        ("own", x_own, 16, xnoT_d, "dxo0", "dxo1", "wxno"),
                    ]

                    def p1_sync(e):
                        for jn, src, nt, dst, s0, s1, ws in jobs:
                            for t in range(nt):
                                slot = t % 2
                                sname = s0 if slot == 0 else s1
                                P.buf_claim(e, ("xb", slot), ("p1xn", jn, t))
                                i = e.dma_start(out=xb[slot][:], in_=src[t * 128:(t + 1) * 128, :]) if E else None
                                P.inc(i, sname, ("p1x", jn, t), 16)

                    def p1_vector(e):
                        for jn, src, nt, dst, s0, s1, ws in jobs:
                            for t in range(nt):
                                slot = t % 2
                                sname = s0 if slot == 0 else s1
                                P.wait_min(e, sname, 16 * (t // 2 + 1))
                                for i8 in range(8):
                                    i = e.bn_stats(stats[:, i8, :], xb[slot][:, i8 * 512:(i8 + 1) * 512]) if E else None
                                P.inc(i, "v", ("p1bn", jn, t))
                                P.wait(e, ("p1bn", jn, t))
                                i = e.bn_aggr(mv[:], stats[:]) if E else None
                                P.inc(i, "v", ("p1ag", jn, t))
                                P.wait(e, ("p1ag", jn, t))
                                i = e.tensor_scalar(nmn[:], mv[:, 0:1], -1.0, None, OP.mult) if E else None
                                P.inc(i, "v", ("p1nm", jn, t))
                                P.wait(e, ("p1nm", jn, t))
                                P.wait(e, ("p1rs", jn, t))
                                P.buf_claim(e, "xnb", ("p1tp", jn, t, 3))
                                i = e.tensor_scalar(xnb[:], xb[slot][:], nmn[:], rstd[:], OP.add, OP.mult) if E else None
                                P.inc(i, "v", ("p1xn", jn, t))

                    def p1_scalar(e):
                        for jn, src, nt, dst, s0, s1, ws in jobs:
                            for t in range(nt):
                                P.wait(e, ("p1ag", jn, t))
                                i = e.activation(lnv[:], mv[:, 1:2], AF.Ln, bias=c_eps[:], scale=1.0) if E else None
                                P.inc(i, "a", ("p1ln", jn, t))
                                P.wait(e, ("p1ln", jn, t))
                                i = e.activation(rstd[:], lnv[:], AF.Exp, scale=-0.5) if E else None
                                P.inc(i, "a", ("p1rs", jn, t))
                                for b in range(4):
                                    if t % 4 == 0 and b == 0:
                                        if jn == "own" and t == 0:
                                            P.wait_min(e, "wxnt", 16 * 8)
                                        g = t // 4
                                        P.wait_min(e, ws, 16 * g)
                                    P.wait(e, ("p1tp", jn, t, b))
                                    i = e.copy(col[:, b * 8:(b + 1) * 8, t % 4, :],
                                               tp[b % 2][:].rearrange("p (c n) -> p c n", n=128)) if E else None
                                    P.inc(i, "a", ("p1ev", jn, t, b))
                                if t % 4 == 3:
                                    g = t // 4
                                    P.wait(e, ("p1ev", jn, t, 3))
                                    for _tt in range(t - 3, t):
                                        P.wait(e, ("p1ev", jn, _tt, 3))
                                    if E:
                                        dd = dst.rearrange("(c p) s -> p c s", p=128)
                                        i = e.dma_start(out=dd[:, :, g * 512:(g + 1) * 512],
                                                        in_=col[:].rearrange("p c f n -> p c (f n)"))
                                    else:
                                        i = None
                                    P.inc(i, ws, ("p1st", jn, g), 16)

                    def p1_tensor(e):
                        for jn, src, nt, dst, s0, s1, ws in jobs:
                            for t in range(nt):
                                P.wait(e, ("p1xn", jn, t))
                                for b in range(4):
                                    P.psum_claim(e, ("p1", b % 2), [("p1ev", jn, t, b)])
                                    for i8 in range(8):
                                        hcix = b * 8 + i8
                                        i = e.transpose(tp[b % 2][:, i8 * 128:(i8 + 1) * 128],
                                                        xnb[:, hcix * 128:(hcix + 1) * 128], c_id[:]) if E else None
                                    P.inc(i, "t", ("p1tp", jn, t, b))

                    eng_sections(dict(sync=p1_sync, vector=p1_vector,
                                        scalar=p1_scalar, tensor=p1_tensor))
                    barrier()

            if upto >= 2:
                with contextlib.ExitStack() as ph:
                    wkb = (ph.enter_context(nc.sbuf_tensor("wkb", [128, HC, NKV * DH], BF16)) if E else None)
                    xnbk = (ph.enter_context(nc.sbuf_tensor("xnbk", [128, HC, 512], BF16)) if E else None)
                    ksb = (ph.enter_context(nc.sbuf_tensor("ksb", [128, 512], BF16)) if E else None)
                    kt1 = (ph.enter_context(nc.sbuf_tensor("kt1", [128, 512], F32)) if E else None)
                    kt2 = (ph.enter_context(nc.sbuf_tensor("kt2", [128, 512], F32)) if E else None)
                    krsb = (ph.enter_context(nc.sbuf_tensor("krsb", [128, 512], BF16)) if E else None)
                    kout = (ph.enter_context(nc.sbuf_tensor("kout", [128, 512], BF16)) if E else None)
                    cskl = (ph.enter_context(nc.sbuf_tensor("cskl", [128, SEQ], F32)) if E else None)
                    sskl = (ph.enter_context(nc.sbuf_tensor("sskl", [128, SEQ], F32)) if E else None)
                    kpr = [(ph.enter_context(nc.psum_tensor(f"kpr{i}", [128, 512], F32)) if E else None) for i in range(2)]
                    rpr = [(ph.enter_context(nc.psum_tensor(f"rpr{i}", [128, 512], F32)) if E else None) for i in range(2)]

                    def p2k_gp(e):
                        i = e.dma_start(out=wkb[:], in_=wkT.rearrange("(c p) n -> p c n", p=128)) if E else None
                        P.inc(i, "gwk", ("gwk",), 16)
                        P.wait_min(e, "dcst", 16 * 10)
                        for tb in range(8):
                            for kh in range(NKV):
                                n = tb * NKV + kh
                                P.wait(e, ("p2kev", tb, kh))
                                i = e.tensor_tensor(kt1[:], ksb[:], cskl[:, tb * 512:(tb + 1) * 512], OP.mult) if E else None
                                P.inc(i, "p", ("p2kt1", tb, kh))
                                P.wait(e, ("p2krev", tb, kh))
                                i = e.tensor_tensor(kt2[:], krsb[:], sskl[:, tb * 512:(tb + 1) * 512], OP.mult) if E else None
                                P.inc(i, "p", ("p2kt2", tb, kh))
                                P.wait(e, ("p2kt2", tb, kh))
                                P.wait_min(e, "wkt", 16 * n)
                                i = e.tensor_tensor(kout[:], kt1[:], kt2[:], OP.add) if E else None
                                P.inc(i, "p", ("p2kr3", tb, kh))

                    def p2k_sync(e):
                        i = e.dma_start(out=cskl[:], in_=cosk) if E else None
                        P.inc(i, "dcst", ("csk",), 16)
                        i = e.dma_start(out=sskl[:], in_=sink) if E else None
                        P.inc(i, "dcst", ("ssk",), 16)
                        for tb in range(8):
                            if tb >= 1:
                                P.wait(e, ("p2kmm", tb - 1, NKV - 1))
                            if E:
                                dd = xnT_d.rearrange("(c p) s -> p c s", p=128)
                                i = e.dma_start(out=xnbk[:], in_=dd[:, :, tb * 512:(tb + 1) * 512])
                            else:
                                i = None
                            P.inc(i, "dnb0", ("p2kx", tb), 16)

                    def p2k_tensor(e):
                        P.wait_min(e, "gwk", 16)
                        for tb in range(8):
                            P.wait_min(e, "dnb0", 16 * (tb + 1))
                            for kh in range(NKV):
                                s = kh % 2
                                P.psum_claim(e, ("p2kk", s), [("p2kev", tb, kh)])
                                for hc in range(HC):
                                    i = e.matmul(kpr[s][:], wkb[:, hc, kh * 128:(kh + 1) * 128],
                                                 xnbk[:, hc, :], start=(hc == 0), stop=(hc == HC - 1)) if E else None
                                P.inc(i, "t", ("p2kmm", tb, kh))
                                P.wait(e, ("p2kev", tb, kh))
                                P.psum_claim(e, ("p2kr", s), [("p2krev", tb, kh)])
                                i = e.matmul(rpr[s][:], c_rt[:], ksb[:], start=True, stop=True) if E else None
                                P.inc(i, "t", ("p2krk", tb, kh))

                    def p2k_scalar(e):
                        for tb in range(8):
                            for kh in range(NKV):
                                n = tb * NKV + kh
                                s = kh % 2
                                P.wait(e, ("p2kmm", tb, kh))
                                if n >= 1:
                                    ptb, pkh = divmod(n - 1, NKV)
                                    P.wait(e, ("p2kt1", ptb, pkh))
                                    P.wait(e, ("p2krk", ptb, pkh))
                                i = e.copy(ksb[:], kpr[s][:]) if E else None
                                P.inc(i, "a", ("p2kev", tb, kh))
                                P.wait(e, ("p2krk", tb, kh))
                                if n >= 1:
                                    ptb, pkh = divmod(n - 1, NKV)
                                    P.wait(e, ("p2kt2", ptb, pkh))
                                i = e.copy(krsb[:], rpr[s][:]) if E else None
                                P.inc(i, "a", ("p2krev", tb, kh))
                                P.wait(e, ("p2kr3", tb, kh))
                                i = e.dma_start(out=kT_d[kh, :, tb * 512:(tb + 1) * 512], in_=kout[:]) if E else None
                                P.inc(i, "wkt", ("p2kst", tb, kh), 16)

                    eng_sections(dict(gpsimd=p2k_gp, sync=p2k_sync,
                                        tensor=p2k_tensor, scalar=p2k_scalar))
                    barrier()

            if upto >= 3:
                with contextlib.ExitStack() as ph:
                    wvb = (ph.enter_context(nc.sbuf_tensor("wvb", [128, HC, NKV * DH], BF16)) if E else None)
                    xnbv = (ph.enter_context(nc.sbuf_tensor("xnbv", [128, HC, 512], BF16)) if E else None)
                    vsb = [(ph.enter_context(nc.sbuf_tensor(f"vsb{i}", [128, 512], BF16)) if E else None) for i in range(2)]
                    vpr = [(ph.enter_context(nc.psum_tensor(f"vpr{i}", [128, 512], F32)) if E else None) for i in range(2)]

                    def p2v_gp(e):
                        i = e.dma_start(out=wvb[:], in_=wvT.rearrange("(c p) n -> p c n", p=128)) if E else None
                        P.inc(i, "gwv", ("gwv",), 16)

                    def p2v_sync(e):
                        for tb in range(8):
                            if tb >= 1:
                                P.wait(e, ("p2vmm", tb - 1, 3, 1))
                            if E:
                                dd = xnT_d.rearrange("(c p) s -> p c s", p=128)
                                i = e.dma_start(out=xnbv[:], in_=dd[:, :, tb * 512:(tb + 1) * 512])
                            else:
                                i = None
                            P.inc(i, "dnb1", ("p2vx", tb), 16)

                    def p2v_tensor(e):
                        P.wait_min(e, "gwv", 16)
                        for tb in range(8):
                            P.wait_min(e, "dnb1", 16 * (tb + 1))
                            for tt in range(4):
                                for vbk in range(2):
                                    n = tb * 8 + tt * 2 + vbk
                                    s = n % 2
                                    P.psum_claim(e, ("p2v", s), [("p2vev", tb, tt, vbk)])
                                    for hc in range(HC):
                                        i = e.matmul(vpr[s][:], xnbv[:, hc, tt * 128:(tt + 1) * 128],
                                                     wvb[:, hc, vbk * 512:(vbk + 1) * 512],
                                                     start=(hc == 0), stop=(hc == HC - 1)) if E else None
                                    P.inc(i, "t", ("p2vmm", tb, tt, vbk))

                    def p2v_scalar(e):
                        for tb in range(8):
                            for tt in range(4):
                                for vbk in range(2):
                                    n = tb * 8 + tt * 2 + vbk
                                    s = n % 2
                                    wsem = "wvs0" if s == 0 else "wvs1"
                                    P.wait(e, ("p2vmm", tb, tt, vbk))
                                    P.wait_min(e, wsem, 16 * 4 * (n // 2))
                                    i = e.copy(vsb[s][:], vpr[s][:]) if E else None
                                    P.inc(i, "a", ("p2vev", tb, tt, vbk))
                                    P.wait(e, ("p2vev", tb, tt, vbk))
                                    trow = tb * 512 + tt * 128
                                    for hh in range(4):
                                        kh = vbk * 4 + hh
                                        i = e.dma_start(out=vS_d[kh, trow:trow + 128, :],
                                                        in_=vsb[s][:, hh * 128:(hh + 1) * 128]) if E else None
                                        P.inc(i, wsem, ("p2vst", tb, tt, vbk, hh), 16)

                    eng_sections(dict(gpsimd=p2v_gp, sync=p2v_sync,
                                        tensor=p2v_tensor, scalar=p2v_scalar))
                    barrier()

            if upto >= 4:
                with contextlib.ExitStack() as ph:
                    wqb = (ph.enter_context(nc.sbuf_tensor("wqb", [128, HC, 8 * DH], BF16)) if E else None)
                    xnbq = (ph.enter_context(nc.sbuf_tensor("xnbq", [128, HC, 512], BF16)) if E else None)
                    qsb = (ph.enter_context(nc.sbuf_tensor("qsb", [128, 512], BF16)) if E else None)
                    qt1 = (ph.enter_context(nc.sbuf_tensor("qt1", [128, 512], F32)) if E else None)
                    qt2 = (ph.enter_context(nc.sbuf_tensor("qt2", [128, 512], F32)) if E else None)
                    qrsb = (ph.enter_context(nc.sbuf_tensor("qrsb", [128, 512], BF16)) if E else None)
                    qout = (ph.enter_context(nc.sbuf_tensor("qout", [128, 512], BF16)) if E else None)
                    cql = (ph.enter_context(nc.sbuf_tensor("cql", [128, OWN], F32)) if E else None)
                    sql = (ph.enter_context(nc.sbuf_tensor("sql", [128, OWN], F32)) if E else None)
                    qpr = [(ph.enter_context(nc.psum_tensor(f"qpr{i}", [128, 512], F32)) if E else None) for i in range(2)]
                    qrp = [(ph.enter_context(nc.psum_tensor(f"qrp{i}", [128, 512], F32)) if E else None) for i in range(2)]

                    def p3_gp(e):
                        for qq in range(4):
                            if qq >= 1:
                                P.wait(e, ("p3mm", qq - 1, 3, 7))
                            if E:
                                i = e.dma_start(out=wqb[:],
                                                in_=wqT.rearrange("(c p) n -> p c n", p=128)[:, :, qq * 1024:(qq + 1) * 1024])
                            else:
                                i = None
                            P.inc(i, "gwq", ("gwq", qq), 16)
                            for sp in range(4):
                                for q8 in range(8):
                                    P.wait(e, ("p3ev", qq, sp, q8))
                                    i = e.tensor_tensor(qt1[:], qsb[:], cql[:, sp * 512:(sp + 1) * 512], OP.mult) if E else None
                                    P.inc(i, "p", ("p3t1", qq, sp, q8))
                                    P.wait(e, ("p3rev", qq, sp, q8))
                                    i = e.tensor_tensor(qt2[:], qrsb[:], sql[:, sp * 512:(sp + 1) * 512], OP.mult) if E else None
                                    P.inc(i, "p", ("p3t2", qq, sp, q8))
                                    P.wait(e, ("p3t2", qq, sp, q8))
                                    idx = (qq * 4 + sp) * 8 + q8
                                    P.wait_min(e, "wqt", 16 * idx)
                                    i = e.tensor_tensor(qout[:], qt1[:], qt2[:], OP.add) if E else None
                                    P.inc(i, "p", ("p3r3", qq, sp, q8))

                    def p3_sync(e):
                        i = e.dma_start(out=cql[:], in_=cosq) if E else None
                        P.inc(i, "dcst", ("csq",), 16)
                        i = e.dma_start(out=sql[:], in_=sinq) if E else None
                        P.inc(i, "dcst", ("ssq",), 16)
                        n = 0
                        for qq in range(4):
                            for sp in range(4):
                                if n >= 1:
                                    pq, psp = divmod(n - 1, 4)
                                    P.wait(e, ("p3mm", pq, psp, 7))
                                if E:
                                    dd = xnoT_d.rearrange("(c p) s -> p c s", p=128)
                                    i = e.dma_start(out=xnbq[:], in_=dd[:, :, sp * 512:(sp + 1) * 512])
                                else:
                                    i = None
                                P.inc(i, "dnb0", ("p3x", qq, sp), 16)
                                n += 1

                    def p3_tensor(e):
                        P.wait_min(e, "dcst", 16 * 12)
                        n = 0
                        for qq in range(4):
                            P.wait_min(e, "gwq", 16 * (qq + 1))
                            for sp in range(4):
                                n += 1
                                P.wait_min(e, "dnb0", 16 * (8 + n))
                                for q8 in range(8):
                                    s = q8 % 2
                                    P.psum_claim(e, ("p3q", s), [("p3ev", qq, sp, q8)])
                                    for hc in range(HC):
                                        i = e.matmul(qpr[s][:], wqb[:, hc, q8 * 128:(q8 + 1) * 128],
                                                     xnbq[:, hc, :], start=(hc == 0), stop=(hc == HC - 1)) if E else None
                                    P.inc(i, "t", ("p3mm", qq, sp, q8))
                                    P.wait(e, ("p3ev", qq, sp, q8))
                                    P.psum_claim(e, ("p3r", s), [("p3rev", qq, sp, q8)])
                                    i = e.matmul(qrp[s][:], c_rt[:], qsb[:], start=True, stop=True) if E else None
                                    P.inc(i, "t", ("p3rk", qq, sp, q8))

                    def p3_scalar(e):
                        for qq in range(4):
                            for sp in range(4):
                                for q8 in range(8):
                                    n = (qq * 4 + sp) * 8 + q8
                                    s = q8 % 2
                                    P.wait(e, ("p3mm", qq, sp, q8))
                                    if n >= 1:
                                        pq, prem = divmod(n - 1, 32)
                                        psp, pq8 = divmod(prem, 8)
                                        P.wait(e, ("p3t1", pq, psp, pq8))
                                        P.wait(e, ("p3rk", pq, psp, pq8))
                                    i = e.copy(qsb[:], qpr[s][:]) if E else None
                                    P.inc(i, "a", ("p3ev", qq, sp, q8))
                                    P.wait(e, ("p3rk", qq, sp, q8))
                                    if n >= 1:
                                        P.wait(e, ("p3t2", pq, psp, pq8))
                                    i = e.copy(qrsb[:], qrp[s][:]) if E else None
                                    P.inc(i, "a", ("p3rev", qq, sp, q8))
                                    P.wait(e, ("p3r3", qq, sp, q8))
                                    qh = qq * 8 + q8
                                    i = e.dma_start(out=qT_d[qh, :, sp * 512:(sp + 1) * 512], in_=qout[:]) if E else None
                                    P.inc(i, "wqt", ("p3st", qq, sp, q8), 16)

                    eng_sections(dict(gpsimd=p3_gp, sync=p3_sync,
                                        tensor=p3_tensor, scalar=p3_scalar))
                    barrier()

            if upto >= 5:
                with contextlib.ExitStack() as ph:
                    kbuf = [(ph.enter_context(nc.sbuf_tensor(f"kbuf{i}", [128, SEQ], BF16)) if E else None) for i in range(3)]
                    vbuf = [(ph.enter_context(nc.sbuf_tensor(f"vbuf{i}", [128, 32, 128], BF16)) if E else None) for i in range(3)]
                    qbuf = [(ph.enter_context(nc.sbuf_tensor(f"qbuf{i}", [128, 512], BF16)) if E else None) for i in range(8)]
                    Ssb = [(ph.enter_context(nc.sbuf_tensor(f"Ssb{i}", [128, SEQ], F32)) if E else None) for i in range(2)]
                    praw = [(ph.enter_context(nc.sbuf_tensor(f"praw{i}", [128, SEQ], BF16)) if E else None) for i in range(2)]
                    Pm = [(ph.enter_context(nc.sbuf_tensor(f"Pm{i}", [128, SEQ], BF16)) if E else None) for i in range(2)]
                    cmax = [(ph.enter_context(nc.sbuf_tensor(f"cmax{i}", [128, 2, 256], F32)) if E else None) for i in range(2)]
                    m8 = [(ph.enter_context(nc.sbuf_tensor(f"m8_{i}", [128, 2, 8], F32)) if E else None) for i in range(2)]
                    nm1 = [(ph.enter_context(nc.sbuf_tensor(f"nm1_{i}", [128, 2], F32)) if E else None) for i in range(2)]
                    K16 = [(ph.enter_context(nc.sbuf_tensor(f"K16_{i}", [128, 2, 256], F32)) if E else None) for i in range(2)]
                    KbD = [(ph.enter_context(nc.sbuf_tensor(f"KbD{i}", [128, 2, 256], F32)) if E else None) for i in range(2)]
                    KbP = [(ph.enter_context(nc.sbuf_tensor(f"KbP{i}", [128, 2, 128], F32)) if E else None) for i in range(2)]
                    racc = [(ph.enter_context(nc.sbuf_tensor(f"racc{i}", [128, 2, 3], F32)) if E else None) for i in range(2)]
                    rr = [(ph.enter_context(nc.sbuf_tensor(f"rr{i}", [128, 2], F32)) if E else None) for i in range(2)]
                    rcp = [(ph.enter_context(nc.sbuf_tensor(f"rcp{i}", [128, 2], F32)) if E else None) for i in range(2)]
                    PTs = [(ph.enter_context(nc.sbuf_tensor(f"PTs{i}", [128, 1024], BF16)) if E else None) for i in range(2)]
                    ob = [(ph.enter_context(nc.sbuf_tensor(f"ob{i}", [128, 128], BF16)) if E else None) for i in range(2)]
                    Rbs = [(ph.enter_context(nc.sbuf_tensor(f"Rbs{i}", [128, 2, 128], F32)) if E else None) for i in range(2)]
                    rrow = [(ph.enter_context(nc.sbuf_tensor(f"rrow{i}", [1, 256], F32)) if E else None) for i in range(2)]
                    Sps = [(ph.enter_context(nc.psum_tensor(f"Sps{i}", [128, 512], F32)) if E else None) for i in range(4)]
                    PTp = [(ph.enter_context(nc.psum_tensor(f"PTp{i}", [128, 1024], BF16)) if E else None) for i in range(2)]
                    AVp = (ph.enter_context(nc.psum_tensor("AVp", [128, 512], F32)) if E else None)
                    RBp = (ph.enter_context(nc.psum_tensor("RBp", [128, 512], F32)) if E else None)

                    PAIRS = [(g, lhp, qt) for g in range(NGRP) for lhp in range(16) for qt in range(4)]
                    NP = len(PAIRS)

                    def subs_of(p):
                        g, lhp, qt = PAIRS[p]
                        return g, qt, (2 * lhp, 2 * lhp + 1)

                    kvmap = {}
                    qmap = {}
                    _kvi = 0
                    for g in range(NGRP):
                        for lhp in range(16):
                            if lhp % 2 == 0:
                                _kvi += 1
                            for s_ in range(2):
                                lh = 2 * lhp + s_
                                kvmap[(g, lh)] = _kvi - 1
                                qmap[(g, lh)] = 2 * (g * 16 + lhp) + s_

                    ptpar = {}
                    _c = 0
                    for p in range(NP):
                        g, qt, lhs2 = subs_of(p)
                        j, ext, n512, nkb, nbat = geom(g, qt)
                        for lh in lhs2:
                            for bat in range(nbat):
                                ptpar[(g, lh, qt, bat)] = _c % 2
                                _c += 1

                    def tile_readers(g, lh, qt, sub):
                        j, ext, n512, nkb, nbat = geom(g, qt)
                        out = []
                        for b in range(n512):
                            rd = []
                            if not (j % 2 == 0 and b == n512 - 1):
                                rd.append(("sev", g, lh, qt, b))
                            if b == n512 - 1:
                                rd.append(("dstt", g, lh, qt))
                            out.append(rd)
                        return out

                    def prev_pair(p):
                        return None if p == 0 else PAIRS[p - 1]

                    def p4_sync(e):
                        P.wait_min(e, "wkt", 16 * 64)
                        P.wait_min(e, "wvs0", 16 * 128)
                        P.wait_min(e, "wvs1", 16 * 128)
                        P.wait_min(e, "wqt", 16 * 128)
                        LHS = [(g, lhp) for g in range(NGRP) for lhp in range(16)]

                        def emit_loads(li):
                            g, lhp = LHS[li]
                            ext_g = 1024 * (g + 1)
                            if lhp % 2 == 0:
                                kv = lhp // 2
                                kvi = kvmap[(g, 2 * lhp)]
                                slot = kvi % 3
                                if kvi >= 3:
                                    og, okv = divmod(kvi - 3, 8)
                                    olh = okv * 4 + 3
                                    oj, oext, on512, onkb, onbat = geom(og, 3)
                                    P.wait(e, ("qk", og, olh, 3, on512 - 1))
                                    P.wait(e, ("avb", og, olh, 3, onbat - 1))
                                i = e.dma_start(out=kbuf[slot][:, 0:ext_g], in_=kT_d[kv, :, 0:ext_g]) if E else None
                                P.inc(i, ["dk0", "dk1", "dk2"][slot], ("p4k", kvi), 16)
                                if E:
                                    i = e.dma_start(out=vbuf[slot][:, 0:8 * (g + 1), :],
                                                    in_=vS_d[kv].rearrange("(n p) d -> p n d", p=128)[:, 0:8 * (g + 1), :])
                                else:
                                    i = None
                                P.inc(i, ["dv0", "dv1", "dv2"][slot], ("p4v", kvi), 16)
                            for s_ in range(2):
                                lh = 2 * lhp + s_
                                qi = qmap[(g, lh)]
                                qslot = qi % 8
                                if qi >= 8:
                                    oli = li - 4
                                    og, olhp = LHS[oli]
                                    olh = 2 * olhp + s_
                                    oj, oext, on512, onkb, onbat = geom(og, 3)
                                    P.wait(e, ("qk", og, olh, 3, on512 - 1))
                                i = e.dma_start(out=qbuf[qslot][:], in_=qT_d[lh, :, g * 512:(g + 1) * 512]) if E else None
                                P.inc(i, ["dq0", "dq1", "dq2", "dq3", "dq4", "dq5", "dq6", "dq7"][qslot], ("p4q", qi), 16)

                        LOOK = 2
                        for li in range(LOOK + 1):
                            emit_loads(li)
                        oti = 0
                        for p in range(NP):
                            g, qt, lhs2 = PAIRS[p][0], PAIRS[p][2], None
                            g, lhp, qt = PAIRS[p]
                            if qt == 0:
                                li = g * 16 + lhp
                                if li + LOOK + 1 < len(LHS):
                                    emit_loads(li + LOOK + 1)
                            for s_ in range(2):
                                lh = 2 * lhp + s_
                                P.wait(e, ("oev", g, lh, qt))
                                wsem = "wot0" if oti % 2 == 0 else "wot1"
                                vt = 4 * g + qt
                                i2 = e.dma_start(out=oT_d[vt, lh, :, :], in_=ob[oti % 2][:]) if E else None
                                P.inc(i2, wsem, ("otst", g, lh, qt), 16)
                                oti += 1

                    def p4_tensor(e):
                        def emit_qk(p):
                            g, qt, lhs2 = subs_of(p)
                            j, ext, n512, nkb, nbat = geom(g, qt)
                            if qt == 0:
                                P.wait(e, ("p4k", kvmap[(g, lhs2[0])]))
                                P.wait(e, ("p4v", kvmap[(g, lhs2[0])]))
                                P.wait(e, ("p4q", qmap[(g, lhs2[0])]))
                                P.wait(e, ("p4q", qmap[(g, lhs2[1])]))
                            for s_, lh in enumerate(lhs2):
                                rds = tile_readers(g, lh, qt, s_)
                                for b in range(n512):
                                    bank = 2 * s_ + (b % 2)
                                    P.psum_claim(e, ("S", bank), rds[b])
                                    N = blkN(j, b, n512, ext)
                                    i2 = e.matmul(Sps[bank][:, 0:N],
                                                  qbuf[qmap[(g, lh)] % 8][:, qt * 128:(qt + 1) * 128],
                                                  kbuf[kvmap[(g, lh)] % 3][:, b * 512:b * 512 + N],
                                                  start=True, stop=True) if E else None
                                    P.inc(i2, "t", ("qk", g, lh, qt, b))

                        def emit_ptav(p):
                            g, qt, lhs2 = subs_of(p)
                            j, ext, n512, nkb, nbat = geom(g, qt)
                            par = p % 2
                            P.psum_claim(e, ("AV",), [("oev", g, lhs2[0], qt), ("oev", g, lhs2[1], qt)])
                            P.wait(e, ("rcp", g, lhs2[1], qt))
                            P.psum_claim(e, ("RB",), [("rrowev", g, lhs2[1], qt), ("rbev", g, lhs2[1], qt)])
                            for s_ in range(2):
                                i2 = e.transpose(RBp[0:1, 256 + s_ * 128:256 + (s_ + 1) * 128],
                                                 rcp[par][:, s_:s_ + 1], c_idf[:]) if E else None
                            P.inc(i2, "t", ("rbt", g, lhs2[1], qt))
                            BB = [(s_, bat) for s_ in range(2) for bat in range(nbat)]

                            def do_pt(s_, bat):
                                lh = lhs2[s_]
                                lo = bat * 8
                                hi = min(nkb, lo + 8)
                                pslot = ptpar[(g, lh, qt, bat)]
                                P.psum_claim(e, ("PT", pslot), [("ptev", g, lh, qt, bat)])
                                for k in range(lo, hi):
                                    i2 = e.transpose(PTp[pslot][:, (k - lo) * 128:(k - lo + 1) * 128],
                                                     Pm[s_][:, k * 128:(k + 1) * 128], c_id[:]) if E else None
                                P.inc(i2, "t", ("pt", g, lh, qt, bat))

                            def do_av(s_, bat):
                                lh = lhs2[s_]
                                lo = bat * 8
                                hi = min(nkb, lo + 8)
                                pslot = ptpar[(g, lh, qt, bat)]
                                P.wait(e, ("ptev", g, lh, qt, bat))
                                for k in range(lo, hi):
                                    i2 = e.matmul(AVp[:, s_ * 128:(s_ + 1) * 128],
                                                  vbuf[kvmap[(g, lh)] % 3][:, k, :],
                                                  PTs[pslot][:, (k - lo) * 128:(k - lo + 1) * 128],
                                                  start=(k == 0), stop=(k == nkb - 1)) if E else None
                                P.inc(i2, "t", ("avb", g, lh, qt, bat))

                            P.wait(e, ("pmask", g, lhs2[1], qt))
                            for bi, (s_, bat) in enumerate(BB):
                                do_pt(s_, bat)
                                if bi >= 1:
                                    do_av(*BB[bi - 1])
                            P.wait(e, ("rrowev", g, lhs2[1], qt))
                            for s_ in range(2):
                                i2 = e.matmul(RBp[:, s_ * 128:(s_ + 1) * 128], c_ones[:],
                                              rrow[par][0:1, s_ * 128:(s_ + 1) * 128],
                                              start=True, stop=True) if E else None
                            P.inc(i2, "t", ("rbmm", g, lhs2[1], qt))
                            do_av(*BB[-1])

                        emit_qk(0)
                        for p in range(NP):
                            if p + 1 < NP:
                                emit_qk(p + 1)
                            emit_ptav(p)

                    def p4_scalar(e):
                        def emit_sev(p):
                            g, qt, lhs2 = subs_of(p)
                            j, ext, n512, nkb, nbat = geom(g, qt)
                            pp = prev_pair(p)
                            first = True
                            for s_, lh in enumerate(lhs2):
                                for b in range(n512):
                                    if j % 2 == 0 and b == n512 - 1:
                                        continue
                                    P.wait(e, ("qk", g, lh, qt, b))
                                    if first and pp is not None:
                                        pg, plhp, pqt = pp
                                        P.wait(e, ("cmax", pg, 2 * plhp + 1, pqt))
                                        P.wait(e, ("exp", pg, 2 * plhp + 1, pqt))
                                    first = False
                                    bank = 2 * s_ + (b % 2)
                                    N = blkN(j, b, n512, ext)
                                    Nc = N - 256 if b == n512 - 1 else N
                                    i2 = e.copy(Ssb[s_][:, b * 512:b * 512 + Nc], Sps[bank][:, 0:Nc]) if E else None
                                    P.inc(i2, "a", ("sev", g, lh, qt, b))

                        def emit_exp(p):
                            g, qt, lhs2 = subs_of(p)
                            j, ext, n512, nkb, nbat = geom(g, qt)
                            par = p % 2
                            pp = prev_pair(p)
                            P.wait(e, ("nm1", g, lhs2[1], qt))
                            if pp is not None:
                                pg, plhp, pqt = pp
                                P.wait(e, ("pmask", pg, 2 * plhp + 1, pqt))
                            for s_, lh in enumerate(lhs2):
                                i2 = e.activation(praw[s_][:, 0:ext], Ssb[s_][:, 0:ext], AF.Exp,
                                                  bias=nm1[par][:, s_:s_ + 1], scale=1.0) if E else None
                                P.inc(i2, "a", ("exp", g, lh, qt))

                        def emit_aux(p):
                            g, qt, lhs2 = subs_of(p)
                            j, ext, n512, nkb, nbat = geom(g, qt)
                            par = p % 2
                            P.wait(e, ("rbt", g, lhs2[1], qt))
                            i2 = e.copy(rrow[par][:], RBp[0:1, 256:512]) if E else None
                            P.inc(i2, "a", ("rrowev", g, lhs2[1], qt))
                            for s_, lh in enumerate(lhs2):
                                for bat in range(nbat):
                                    lo = bat * 8
                                    hi = min(nkb, lo + 8)
                                    pslot = ptpar[(g, lh, qt, bat)]
                                    P.wait(e, ("pt", g, lh, qt, bat))
                                    P.buf_claim(e, ("PTs", pslot), ("avb", g, lh, qt, bat))
                                    i2 = e.copy(PTs[pslot][:, 0:(hi - lo) * 128],
                                                PTp[pslot][:, 0:(hi - lo) * 128]) if E else None
                                    P.inc(i2, "a", ("ptev", g, lh, qt, bat))

                        emit_sev(0)
                        for p in range(NP):
                            emit_exp(p)
                            if p + 1 < NP:
                                emit_sev(p + 1)
                            emit_aux(p)

                    def p4_pool(e):
                        for p in range(NP):
                            g, qt, lhs2 = subs_of(p)
                            j, ext, n512, nkb, nbat = geom(g, qt)
                            par = p % 2
                            P.wait(e, ("max8", g, lhs2[1], qt))
                            for s_, lh in enumerate(lhs2):
                                i2 = e.memset(K16[par][:, s_, 0:8], 1.0) if E else None
                                i2 = e.tensor_scalar(K16[par][:, s_, 8:256], cmax[par][:, s_, 8:256],
                                                     m8[par][:, s_, 7:8], None, OP.is_ge) if E else None
                                P.inc(i2, "p", ("k16", g, lh, qt))
                                i2 = e.tensor_scalar(nm1[par][:, s_:s_ + 1], m8[par][:, s_, 0:1],
                                                     -1.0, None, OP.mult) if E else None
                            P.inc(i2, "p", ("nm1", g, lhs2[1], qt))


                    def p4_vector(e):
                        def emit_s3(p):
                            g, qt, lhs2 = subs_of(p)
                            j, ext, n512, nkb, nbat = geom(g, qt)
                            par = p % 2
                            pp = prev_pair(p)
                            b = n512 - 1
                            N = blkN(j, b, n512, ext)
                            off = N - 256
                            for s_, lh in enumerate(lhs2):
                                bank = 2 * s_ + (b % 2)
                                P.wait(e, ("qk", g, lh, qt, b))
                                if s_ == 0 and pp is not None:
                                    pg, plhp, pqt = pp
                                    P.wait(e, ("exp", pg, 2 * plhp + 1, pqt))
                                if j % 2 == 1:
                                    P.wait(e, ("sev", g, lh, qt, b))
                                i2 = e.scalar_tensor_tensor(Ssb[s_][:, ext - 256:ext], Sps[bank][:, off:off + 256],
                                                            1.0, c_tri[:], OP.mult, OP.add) if E else None
                                P.inc(i2, "v", ("dstt", g, lh, qt))
                            for s_, lh in enumerate(lhs2):
                                if n512 >= 2:
                                    lastsev = n512 - 2 if (j % 2 == 0) else n512 - 1
                                    P.wait(e, ("sev", g, lh, qt, lastsev))
                                P.wait(e, ("dstt", g, lh, qt))
                                i2 = e.tensor_reduce(cmax[par][:, s_, 0:16 * (j + 1)],
                                                     Ssb[s_][:, 0:ext].rearrange("p (c s) -> p c s", s=16),
                                                     AX.X, OP.max) if E else None
                                if j < 15:
                                    i2 = e.memset(cmax[par][:, s_, 16 * (j + 1):256], NEG) if E else None
                                P.inc(i2, "v", ("cmax", g, lh, qt))
                            P.wait(e, ("cmax", g, lhs2[1], qt))
                            for s_, lh in enumerate(lhs2):
                                i2 = e.max(m8[par][:, s_, :], cmax[par][:, s_, :]) if E else None
                                P.inc(i2, "v", ("max8", g, lh, qt))

                        def emit_s5(p):
                            g, qt, lhs2 = subs_of(p)
                            j, ext, n512, nkb, nbat = geom(g, qt)
                            par = p % 2
                            P.wait(e, ("k16", g, lhs2[1], qt))
                            for s_, lh in enumerate(lhs2):
                                i2 = e.tensor_tensor(KbD[par][:, s_, :], bc16(K16[par][:, s_, 16 * j:16 * j + 16]),
                                                     c_rec[:].rearrange("p (c s) -> p c s", s=16), OP.max) if E else None
                                P.inc(i2, "v", ("kbd", g, lh, qt))
                                if j >= 1:
                                    i2 = e.tensor_tensor(KbP[par][:, s_, :], bc16(K16[par][:, s_, 16 * j - 8:16 * j]),
                                                         c_utp[:].rearrange("p (c s) -> p c s", s=16), OP.max) if E else None
                                    P.inc(i2, "v", ("kbp", g, lh, qt))
                            P.wait(e, ("exp", g, lhs2[1], qt))
                            P.wait(e, ("kbd", g, lhs2[1], qt))
                            if j >= 1:
                                P.wait(e, ("kbp", g, lhs2[1], qt))
                            for s_, lh in enumerate(lhs2):
                                P.buf_claim(e, ("Pm", s_), ("pt", g, lh, qt, nbat - 1))
                                if j >= 1:
                                    i2 = e.scalar_tensor_tensor(
                                        Pm[s_][:, 0:256 * j - 128].rearrange("p (c s) -> p c s", s=16),
                                        praw[s_][:, 0:256 * j - 128].rearrange("p (c s) -> p c s", s=16),
                                        1.0, bc16(K16[par][:, s_, 0:16 * j - 8]),
                                        OP.mult, OP.mult, accum_out=racc[par][:, s_, 0:1]) if E else None
                                    i2 = e.scalar_tensor_tensor(
                                        Pm[s_][:, 256 * j - 128:256 * j], praw[s_][:, 256 * j - 128:256 * j],
                                        1.0, KbP[par][:, s_, :], OP.mult, OP.mult,
                                        accum_out=racc[par][:, s_, 1:2]) if E else None
                                i2 = e.scalar_tensor_tensor(
                                    Pm[s_][:, 256 * j:ext], praw[s_][:, 256 * j:ext],
                                    1.0, KbD[par][:, s_, :], OP.mult, OP.mult,
                                    accum_out=racc[par][:, s_, 2:3]) if E else None
                                P.inc(i2, "v", ("pmask", g, lh, qt))
                            P.wait(e, ("pmask", g, lhs2[1], qt))
                            for s_, lh in enumerate(lhs2):
                                if j >= 1:
                                    i2 = e.tensor_reduce(rr[par][:, s_:s_ + 1], racc[par][:, s_, :], AX.X, OP.add) if E else None
                                else:
                                    i2 = e.tensor_copy(rr[par][:, s_:s_ + 1], racc[par][:, s_, 2:3]) if E else None
                                P.inc(i2, "v", ("rsum", g, lh, qt))
                            P.wait(e, ("rsum", g, lhs2[1], qt))
                            i2 = e.reciprocal(rcp[par][:], rr[par][:]) if E else None
                            P.inc(i2, "v", ("rcp", g, lhs2[1], qt))

                        def emit_fin(p):
                            g, qt, lhs2 = subs_of(p)
                            j, ext, n512, nkb, nbat = geom(g, qt)
                            par = p % 2
                            P.wait(e, ("rbmm", g, lhs2[1], qt))
                            i2 = e.tensor_copy(Rbs[par][:], RBp[:, 0:256].rearrange("p (s d) -> p s d", d=128)) if E else None
                            P.inc(i2, "v", ("rbev", g, lhs2[1], qt))
                            P.wait(e, ("rbev", g, lhs2[1], qt))
                            P.wait(e, ("avb", g, lhs2[0], qt, nbat - 1))
                            P.wait(e, ("avb", g, lhs2[1], qt, nbat - 1))
                            for s_, lh in enumerate(lhs2):
                                oti = 2 * p + s_
                                wsem = "wot0" if oti % 2 == 0 else "wot1"
                                P.wait_min(e, wsem, 16 * (oti // 2))
                                i2 = e.tensor_tensor(ob[oti % 2][:], AVp[:, s_ * 128:(s_ + 1) * 128],
                                                     Rbs[par][:, s_, :], OP.mult) if E else None
                                P.inc(i2, "v", ("oev", g, lh, qt))

                        emit_s3(0)
                        for p in range(NP):
                            if p + 1 < NP:
                                emit_s3(p + 1)
                            emit_s5(p)
                            if p >= 1:
                                emit_fin(p - 1)
                        emit_fin(NP - 1)

                    eng_sections(dict(sync=p4_sync, tensor=p4_tensor, gpsimd=p4_pool,
                                        scalar=p4_scalar, vector=p4_vector))
                    barrier()


            if upto >= 6:
                with contextlib.ExitStack() as ph:
                    wob = [(ph.enter_context(nc.sbuf_tensor(f"wob{i}", [128, HC, 512], BF16)) if E else None) for i in range(2)]
                    wub = [(ph.enter_context(nc.sbuf_tensor(f"wub{i}", [128, HC, 512], BF16)) if E else None) for i in range(2)]
                    Ab = (ph.enter_context(nc.sbuf_tensor("Ab", [128, HC, 128], BF16)) if E else None)
                    Bb = (ph.enter_context(nc.sbuf_tensor("Bb", [128, HID], BF16)) if E else None)
                    otb = [(ph.enter_context(nc.sbuf_tensor(f"otb{i}", [128, 32, 128], BF16)) if E else None) for i in range(2)]
                    xnl = [(ph.enter_context(nc.sbuf_tensor(f"xnl{i}", [128, HC, 128], BF16)) if E else None) for i in range(2)]
                    xsl = [(ph.enter_context(nc.sbuf_tensor(f"xsl{i}", [128, 512], F32)) if E else None) for i in range(2)]
                    zT = [(ph.enter_context(nc.sbuf_tensor(f"zT{i}", [128, 128], BF16)) if E else None) for i in range(NVT)]
                    lg = (ph.enter_context(nc.sbuf_tensor("lg", [128, 4], F32)) if E else None)
                    mmx = (ph.enter_context(nc.sbuf_tensor("mmx", [128, 1], F32)) if E else None)
                    ge4 = (ph.enter_context(nc.sbuf_tensor("ge4", [128, 4], F32)) if E else None)
                    oh4 = (ph.enter_context(nc.sbuf_tensor("oh4", [128, 4], F32)) if E else None)
                    sc1 = (ph.enter_context(nc.sbuf_tensor("sc1", [128, 1], F32)) if E else None)
                    sc2 = (ph.enter_context(nc.sbuf_tensor("sc2", [128, 1], F32)) if E else None)
                    zsel = (ph.enter_context(nc.sbuf_tensor("zsel", [128, 128], BF16)) if E else None)
                    gsl = (ph.enter_context(nc.sbuf_tensor("gsl", [128, 512], F32)) if E else None)
                    ssl2 = (ph.enter_context(nc.sbuf_tensor("ssl2", [128, 512], F32)) if E else None)
                    sg = (ph.enter_context(nc.sbuf_tensor("sg", [128, 512], F32)) if E else None)
                    yb = (ph.enter_context(nc.sbuf_tensor("yb", [128, 512], F32)) if E else None)
                    spsP = (ph.enter_context(nc.psum_tensor("spsP", [128, 512], F32)) if E else None)
                    fpsP = (ph.enter_context(nc.psum_tensor("fpsP", [128, 512], F32)) if E else None)
                    zpsP = (ph.enter_context(nc.psum_tensor("zpsP", [128, 512], F32)) if E else None)
                    ztpP = (ph.enter_context(nc.psum_tensor("ztpP", [128, 512], BF16)) if E else None)

                    def p6_gp(e):
                        i = e.dma_start(out=Ab[:], in_=A_padT.rearrange("(c p) n -> p c n", p=128)) if E else None
                        P.inc(i, "gab", ("gA",), 16)
                        i = e.dma_start(out=Bb[:], in_=B_padT) if E else None
                        P.inc(i, "gab", ("gB",), 16)
                        for mb in range(8):
                            for lt in range(NVT):
                                idx = mb * NVT + lt
                                P.wait(e, ("sev6", mb, lt))
                                P.wait(e, ("silu", mb, lt))
                                i = e.tensor_tensor(sg[:], ssl2[:], gsl[:], OP.mult) if E else None
                                P.inc(i, "p", ("sg", mb, lt))
                                P.wait(e, ("sg", mb, lt))
                                P.wait_min(e, "wy", 16 * idx)
                                P.wait_min(e, "dxs0" if idx % 2 == 0 else "dxs1", 16 * (idx // 2 + 1))
                                i = e.tensor_tensor(yb[:], sg[:], xsl[idx % 2][:], OP.add) if E else None
                                P.inc(i, "p", ("yb", mb, lt))

                    def p6_sync(e):
                        P.wait_min(e, "wot0", 16 * 256)
                        P.wait_min(e, "wot1", 16 * 256)
                        P.wait_min(e, "wxno", 16 * 4)
                        P.wait_min(e, "gcwo", 16 * 8)
                        P.wait_min(e, "gcwup", 16 * 8)

                        def wload(mb):
                            if E:
                                i = e.dma_start(out=wob[mb % 2][:],
                                                in_=wo_bf_d.rearrange("(c p) m -> p c m", p=128)[:, :, mb * 512:(mb + 1) * 512])
                            else:
                                i = None
                            P.inc(i, "gwo" if mb % 2 == 0 else "gwo1", ("gwo", mb), 16)
                            if E:
                                i = e.dma_start(out=wub[mb % 2][:],
                                                in_=wup_bf_d.rearrange("(c p) m -> p c m", p=128)[:, :, mb * 512:(mb + 1) * 512])
                            else:
                                i = None
                            P.inc(i, "gwup" if mb % 2 == 0 else "gwup1", ("gwup", mb), 16)

                        wload(0)
                        wload(1)
                        n = 0
                        for mb in range(8):
                            if mb >= 2:
                                P.wait(e, ("op", mb - 2, NVT - 1))
                                P.wait(e, ("fp", mb - 2, NVT - 1))
                                wload(mb)
                            for lt in range(NVT):
                                slot = n % 2
                                if n >= 2:
                                    pmb, plt = divmod(n - 2, NVT)
                                    P.wait(e, ("op", pmb, plt))
                                    P.wait(e, ("fp", pmb, plt))
                                    P.wait(e, ("yb", pmb, plt))
                                if E:
                                    i = e.dma_start(out=otb[slot][:], in_=oT_d[lt].rearrange("h d q -> d h q"))
                                else:
                                    i = None
                                P.inc(i, "dot0" if slot == 0 else "dot1", ("p6o", mb, lt), 16)
                                if E:
                                    dd = xnoT_d.rearrange("(c p) s -> p c s", p=128)
                                    i = e.dma_start(out=xnl[slot][:], in_=dd[:, :, lt * 128:(lt + 1) * 128])
                                else:
                                    i = None
                                P.inc(i, "dxl0" if slot == 0 else "dxl1", ("p6n", mb, lt), 16)
                                i = e.dma_start(out=xsl[slot][:],
                                                in_=x_own[lt * 128:(lt + 1) * 128, mb * 512:(mb + 1) * 512]) if E else None
                                P.inc(i, "dxs0" if slot == 0 else "dxs1", ("p6x", mb, lt), 16)
                                n += 1

                    def p6_tensor(e):
                        P.wait_min(e, "gab", 32)
                        n = 0
                        for mb in range(8):
                            P.wait_min(e, "gwo" if mb % 2 == 0 else "gwo1", 16 * (mb // 2 + 1))
                            P.wait_min(e, "gwup" if mb % 2 == 0 else "gwup1", 16 * (mb // 2 + 1))
                            for lt in range(NVT):
                                slot = n % 2
                                sname_o = "dot0" if slot == 0 else "dot1"
                                sname_n = "dxl0" if slot == 0 else "dxl1"
                                P.wait_min(e, sname_o, 16 * (n // 2 + 1))
                                P.wait_min(e, sname_n, 16 * (n // 2 + 1))
                                if mb == 0:
                                    P.psum_claim(e, ("z",), [("zsel", lt)])
                                    for hc in range(HC):
                                        i = e.matmul(zpsP[:, 0:128], xnl[slot][:, hc, :], Ab[:, hc, :],
                                                     start=(hc == 0), stop=(hc == HC - 1)) if E else None
                                    P.inc(i, "t", ("zmm", lt))
                                    P.wait(e, ("zsel", lt))
                                    P.psum_claim(e, ("zt",), [("ztev", lt)])
                                    i = e.transpose(ztpP[:, 0:128], zsel[:], c_id[:]) if E else None
                                    P.inc(i, "t", ("ztr", lt))
                                P.psum_claim(e, ("sps",), [("sev6", mb, lt)])
                                for hd in range(NQH):
                                    i = e.matmul(spsP[:], otb[slot][:, hd, :], wob[mb % 2][:, hd, :],
                                                 start=(hd == 0), stop=(hd == NQH - 1)) if E else None
                                P.inc(i, "t", ("op", mb, lt))
                                P.psum_claim(e, ("fps",), [("silu", mb, lt)])
                                P.wait(e, ("ztev", lt))
                                for hc in range(HC):
                                    i = e.matmul(fpsP[:], xnl[slot][:, hc, :], wub[mb % 2][:, hc, :],
                                                 start=(hc == 0), stop=False) if E else None
                                i = e.matmul(fpsP[:], zT[lt][:], Bb[:, mb * 512:(mb + 1) * 512],
                                             start=False, stop=True) if E else None
                                P.inc(i, "t", ("fp", mb, lt))
                                n += 1

                    def p6_vector(e):
                        for lt in range(NVT):
                            P.wait(e, ("zmm", lt))
                            if E:
                                zz = zpsP[:, 0:128].rearrange("p (e r) -> p e r", r=32)[:, :, 31:32]
                                cb = bass.AP(c_bb[:].tensor, c_bb[:].offset, list(c_bb[:].ap) + [[0, 1]])
                                i = e.tensor_tensor(lg[:], zz, cb, OP.add)
                            else:
                                i = None
                            P.inc(i, "v", ("lg", lt))
                            P.wait(e, ("lg", lt))
                            i = e.tensor_reduce(mmx[:], lg[:], AX.X, OP.max) if E else None
                            P.inc(i, "v", ("lmx", lt))
                            P.wait(e, ("lmx", lt))
                            i = e.tensor_scalar(ge4[:], lg[:], mmx[:], None, OP.is_ge) if E else None
                            P.inc(i, "v", ("ge4", lt))
                            P.wait(e, ("ge4", lt))
                            i = e.tensor_copy(oh4[:, 0:1], ge4[:, 0:1]) if E else None
                            P.inc(i, "v", ("oh0", lt))
                            P.wait(e, ("oh0", lt))
                            i = e.tensor_scalar(sc1[:], oh4[:, 0:1], -1.0, 1.0, OP.mult, OP.add) if E else None
                            P.inc(i, "v", ("nh0", lt))
                            P.wait(e, ("nh0", lt))
                            i = e.tensor_tensor(oh4[:, 1:2], ge4[:, 1:2], sc1[:], OP.mult) if E else None
                            P.inc(i, "v", ("oh1", lt))
                            P.wait(e, ("oh1", lt))
                            i = e.tensor_scalar(sc2[:], oh4[:, 1:2], -1.0, 1.0, OP.mult, OP.add) if E else None
                            P.inc(i, "v", ("nh1", lt))
                            P.wait(e, ("nh1", lt))
                            i = e.tensor_tensor(sc1[:], sc1[:], sc2[:], OP.mult) if E else None
                            P.inc(i, "v", ("n01", lt))
                            P.wait(e, ("n01", lt))
                            i = e.tensor_tensor(oh4[:, 2:3], ge4[:, 2:3], sc1[:], OP.mult) if E else None
                            P.inc(i, "v", ("oh2", lt))
                            P.wait(e, ("oh2", lt))
                            i = e.tensor_scalar(sc2[:], oh4[:, 2:3], -1.0, 1.0, OP.mult, OP.add) if E else None
                            P.inc(i, "v", ("nh2", lt))
                            P.wait(e, ("nh2", lt))
                            i = e.tensor_tensor(sc1[:], sc1[:], sc2[:], OP.mult) if E else None
                            P.inc(i, "v", ("n012", lt))
                            P.wait(e, ("n012", lt))
                            i = e.tensor_tensor(oh4[:, 3:4], ge4[:, 3:4], sc1[:], OP.mult) if E else None
                            P.inc(i, "v", ("oh3", lt))
                            P.wait(e, ("oh3", lt))
                            for ex in range(4):
                                i = e.tensor_scalar(zsel[:, ex * 32:(ex + 1) * 32], zpsP[:, ex * 32:(ex + 1) * 32],
                                                    oh4[:, ex:ex + 1], None, OP.mult) if E else None
                            P.inc(i, "v", ("zsel", lt))
                            P.wait(e, ("ztr", lt))
                            i = e.tensor_copy(zT[lt][:], ztpP[:, 0:128]) if E else None
                            P.inc(i, "v", ("ztev", lt))

                    def p6_scalar(e):
                        for mb in range(8):
                            for lt in range(NVT):
                                idx = mb * NVT + lt
                                P.wait(e, ("op", mb, lt))
                                if idx >= 1:
                                    pmb, plt = divmod(idx - 1, NVT)
                                    P.wait(e, ("sg", pmb, plt))
                                i = e.copy(ssl2[:], spsP[:]) if E else None
                                P.inc(i, "a", ("sev6", mb, lt))
                                P.wait(e, ("fp", mb, lt))
                                i = e.activation(gsl[:], fpsP[:], AF.Silu) if E else None
                                P.inc(i, "a", ("silu", mb, lt))
                                P.wait(e, ("yb", mb, lt))
                                i = e.dma_start(out=y_out[lt * 128:(lt + 1) * 128, mb * 512:(mb + 1) * 512],
                                                in_=yb[:]) if E else None
                                P.inc(i, "wy", ("yst", mb, lt), 16)

                    eng_sections(dict(gpsimd=p6_gp, sync=p6_sync, tensor=p6_tensor,
                                        vector=p6_vector, scalar=p6_scalar))

                    def fin_sync(e):
                        P.wait_min(e, "wy", 16 * 128)
                    eng_sections(dict(sync=fin_sync))

            if upto < 6:
                def dbg_fin(e):
                    for s in ["wxnt", "wxno", "wkt", "wvs0", "wvs1", "wqt",
                              "wot0", "wot1"]:
                        if P.cnt.get(s):
                            P.wait_min(e, s, P.cnt[s])
                eng_sections(dict(sync=dbg_fin))

        P.begin(False)
        run()
        P.begin(True)
        run()

    return nc


_CACHED = {}


def kernel(L0_attn_hidden_states, L0_attn_wq, L0_attn_wk, L0_attn_wv, L0_attn_wo,
           L0_ffn_W_up, L0_ffn_gate_w, L0_ffn_expert_bias, L0_ffn_lora_A, L0_ffn_lora_B):
    x = np.ascontiguousarray(np.asarray(L0_attn_hidden_states, dtype=np.float32))
    wq = np.asarray(L0_attn_wq, np.float32)
    wk = np.asarray(L0_attn_wk, np.float32)
    wv = np.asarray(L0_attn_wv, np.float32)
    wo = np.asarray(L0_attn_wo, np.float32)
    wup = np.asarray(L0_ffn_W_up, np.float32)
    gw = np.asarray(L0_ffn_gate_w, np.float32)
    eb = np.asarray(L0_ffn_expert_bias, np.float32)
    lA = np.asarray(L0_ffn_lora_A, np.float32)
    lB = np.asarray(L0_ffn_lora_B, np.float32)

    inv = 1.0 / (10000.0 ** (np.arange(0, DH, 2, dtype=np.float32) / DH))
    ang = np.arange(SEQ, dtype=np.float32)[:, None] * inv[None, :]
    emb = np.concatenate([ang, ang], axis=1)
    cosf = np.cos(emb).astype(np.float32)
    sinf = np.sin(emb).astype(np.float32)
    scale = 1.0 / math.sqrt(DH)
    identb = np.eye(128, dtype=ml_dtypes.bfloat16)
    R = np.zeros((128, 128), np.float32)
    R[0:64, 64:128] = -np.eye(64)
    R[64:128, 0:64] = np.eye(64)
    rotT = np.ascontiguousarray(R.T).astype(ml_dtypes.bfloat16)
    ones1 = np.ones((1, 128), np.float32)

    A_padT = np.zeros((HID, 128), np.float32)
    for ex in range(4):
        A_padT[:, ex * 32:ex * 32 + 8] = lA[ex].T
        A_padT[:, ex * 32 + 31] = gw[ex]
    B_padT = np.zeros((128, HID), np.float32)
    for ex in range(4):
        B_padT[ex * 32:ex * 32 + 8, :] = lB[ex].T
    ebias = eb.reshape(1, 4).astype(np.float32)

    pcol = np.arange(128)[:, None]
    j1 = np.arange(256)[None, :]
    j2 = np.arange(128)[None, :]

    wqT = np.ascontiguousarray(wq.T)
    wkT = np.ascontiguousarray(wk.T)
    wvT = np.ascontiguousarray(wv.T)
    woT = np.ascontiguousarray(wo.T)
    wupT = np.ascontiguousarray(wup.T)

    in_maps = []
    for c in range(8):
        b, h = divmod(c, 2)
        own = np.arange(OWN) * 2 + h
        trim = np.where(j1 <= 2 * pcol + h, 0.0, NEG).astype(np.float32)
        rec = (j1 > 2 * pcol + h - 128).astype(np.float32)
        utp = (j2 > 2 * pcol + h).astype(np.float32)
        in_maps.append(dict(
            x_full=x[b],
            x_own=np.ascontiguousarray(x[b, own]),
            wqT=wqT, wkT=wkT, wvT=wvT, woT=woT, wupT=wupT,
            A_padT=A_padT, B_padT=B_padT, ebias=ebias,
            cosq=np.ascontiguousarray((cosf[own] * scale).T),
            sinq=np.ascontiguousarray((sinf[own] * scale).T),
            cosk=np.ascontiguousarray(cosf.T),
            sink=np.ascontiguousarray(sinf.T),
            identb=identb, identf=np.eye(128, dtype=np.float32), rotT=rotT,
            trimask2=trim, recent2=rec, ut2prev=utp,
            ones1=ones1,
        ))

    if "prog" not in _CACHED:
        _CACHED["prog"] = build_program()
    nc = _CACHED["prog"]

    res = run_bass_kernel_spmd(nc, in_maps, core_ids=list(range(8)))

    out = np.empty((NB, SEQ, HID), np.float32)
    for c in range(8):
        b, h = divmod(c, 2)
        out[b, h::2] = np.asarray(res.results[c]["y"], np.float32)
    return out


# version 07
# speedup vs baseline: 1.0325x; 1.0233x over previous
import contextlib
import math

import numpy as np
import ml_dtypes

import concourse.bass as bass
import concourse.mybir as mybir
from concourse.alu_op_type import AluOpType as OP
from concourse.bass_utils import run_bass_kernel_spmd

F32 = mybir.dt.float32
BF16 = mybir.dt.bfloat16
AF = mybir.ActivationFunctionType
AX = mybir.AxisListType

NEG = float(np.finfo(np.float32).min)
HID = 4096
SEQ = 4096
NB = 4
NQH = 32
NKV = 8
DH = 128
OWN = 2048
NVT = 16
NGRP = 4
HC = 32


def bc16(ap, reps=16):
    return bass.AP(ap.tensor, ap.offset, list(ap.ap) + [[0, reps]])


class Prog:
    def __init__(self):
        self.plan = {}
        self.emitting = False
        self.cnt = {}
        self.sem = {}

    def begin(self, emitting):
        self.emitting = emitting
        self.cnt = {}
        self.psum_readers = {}
        self.buf_consumer = {}

    def inc(self, inst, semname, key, n=1):
        c = self.cnt.get(semname, 0) + n
        self.cnt[semname] = c
        if self.emitting:
            assert self.plan[key] == (semname, c), (key, semname, c, self.plan.get(key))
            inst.then_inc(self.sem[semname], n)
        else:
            assert key not in self.plan, key
            self.plan[key] = (semname, c)

    def wait(self, eng, key):
        if self.emitting:
            semname, c = self.plan[key]
            eng.wait_ge(self.sem[semname], c)

    def wait_min(self, eng, semname, c):
        if self.emitting and c > 0:
            eng.wait_ge(self.sem[semname], c)

    def psum_claim(self, eng, bank, reader_keys):
        prev = self.psum_readers.get(bank)
        if prev:
            for k in prev:
                self.wait(eng, k)
        self.psum_readers[bank] = list(reader_keys)

    def buf_claim(self, eng, buf, consumer_key):
        prev = self.buf_consumer.get(buf)
        if prev is not None:
            self.wait(eng, prev)
        self.buf_consumer[buf] = consumer_key


def geom(g, qt):
    j = 4 * g + qt
    ext = 256 * (j + 1)
    n512 = (j + 2) // 2
    nkb = 2 * (j + 1)
    nbat = (nkb + 3) // 4
    return j, ext, n512, nkb, nbat


def blkN(j, b, n512, ext):
    return 512 if b < n512 - 1 else ext - 512 * (n512 - 1)


def build_program(upto=99, debug=False):
    nc = bass.Bass("TRN2", target_bir_lowering=False)

    def din(name, shape, dt=F32):
        return nc.dram_tensor(name, list(shape), dt, kind="ExternalInput").ap()

    x_full = din("x_full", (SEQ, HID))
    x_own = din("x_own", (OWN, HID))
    wqT = din("wqT", (HID, NQH * DH))
    wkT = din("wkT", (HID, NKV * DH))
    wvT = din("wvT", (HID, NKV * DH))
    woT = din("woT", (HID, HID))
    wupT = din("wupT", (HID, HID))
    A_padT = din("A_padT", (HID, 128))
    B_padT = din("B_padT", (128, HID))
    ebias = din("ebias", (1, 4))
    cosq = din("cosq", (DH, OWN))
    sinq = din("sinq", (DH, OWN))
    cosk = din("cosk", (DH, SEQ))
    sink = din("sink", (DH, SEQ))
    identb = din("identb", (128, 128), BF16)
    identf = din("identf", (128, 128))
    rotT = din("rotT", (128, 128), BF16)
    trimask2 = din("trimask2", (128, 256))
    recent2 = din("recent2", (128, 256))
    ut2prev = din("ut2prev", (128, 128))
    ones1 = din("ones1", (1, 128))

    y_out = nc.dram_tensor("y", [OWN, HID], F32, kind="ExternalOutput").ap()

    KIND = "ExternalOutput" if debug else "Internal"
    xnT_d = nc.dram_tensor("xnT_d", [HID, SEQ], BF16, kind=KIND).ap()
    xnoT_d = nc.dram_tensor("xnoT_d", [HID, OWN], BF16, kind=KIND).ap()
    qT_d = nc.dram_tensor("qT_d", [NQH, DH, OWN], BF16, kind=KIND).ap()
    kT_d = nc.dram_tensor("kT_d", [NKV, DH, SEQ], BF16, kind=KIND).ap()
    vS_d = nc.dram_tensor("vS_d", [NKV, SEQ, DH], BF16, kind=KIND).ap()
    oT_d = nc.dram_tensor("oT_d", [NVT, NQH, DH, 128], BF16, kind=KIND).ap()
    wo_bf_d = nc.dram_tensor("wo_bf_d", [HID, HID], BF16, kind="Internal").ap()
    wup_bf_d = nc.dram_tensor("wup_bf_d", [HID, HID], BF16, kind="Internal").ap()

    P = Prog()

    semnames = [
        "t", "v", "a", "p", "bar", "dcst",
        "dx0", "dx1", "dxo0", "dxo1", "wxnt", "wxno",
        "gwk", "gwv", "gwq", "gwo", "gwup", "gab", "gcwo", "gcwup", "gwo1", "gwup1",
        "dnb0", "dnb1",
        "wkt", "wvs0", "wvs1", "wqt",
        "dk0", "dk1", "dk2", "dv0", "dv1", "dv2", "dq0", "dq1", "dq2", "dq3", "dq4", "dq5", "dq6", "dq7",
        "wot0", "wot1",
        "dot0", "dot1", "dxl0", "dxl1", "dxs0", "dxs1",
        "wy",
    ]

    with contextlib.ExitStack() as top:
        for s in semnames:
            P.sem[s] = top.enter_context(nc.semaphore(s))
        block = top.enter_context(nc.Block())

        c_id = top.enter_context(nc.sbuf_tensor("c_id", [128, 128], BF16))
        c_idf = top.enter_context(nc.sbuf_tensor("c_idf", [128, 128], F32))
        c_rt = top.enter_context(nc.sbuf_tensor("c_rt", [128, 128], BF16))
        c_tri = top.enter_context(nc.sbuf_tensor("c_tri", [128, 256], F32))
        c_rec = top.enter_context(nc.sbuf_tensor("c_rec", [128, 256], F32))
        c_utp = top.enter_context(nc.sbuf_tensor("c_utp", [128, 128], F32))
        c_ones = top.enter_context(nc.sbuf_tensor("c_ones", [1, 128], F32))
        c_eb = top.enter_context(nc.sbuf_tensor("c_eb", [1, 4], F32))
        c_bb = top.enter_context(nc.sbuf_tensor("c_bb", [128, 4], F32))
        c_eps = top.enter_context(nc.sbuf_tensor("c_eps", [128, 1], F32))
        c_dmv = top.enter_context(nc.sbuf_tensor("c_dmv", [128, 1], F32))
        c_dma_ = top.enter_context(nc.sbuf_tensor("c_dma_", [128, 1], F32))
        c_dmg = top.enter_context(nc.sbuf_tensor("c_dmg", [128, 1], F32))

        barcnt = [0]

        def run():
            E = P.emitting
            barcnt[0] = 0

            def eng_sections(fns):
                engines = {}
                if E:
                    for nm in ["sync", "gpsimd", "tensor", "scalar", "vector"]:
                        f = fns.get(nm)
                        dec = getattr(block, nm)

                        def wrapper(eng, _f=f, _nm=nm):
                            engines[_nm] = eng
                            if _f:
                                _f(eng)
                        dec(wrapper)
                else:
                    for nm in ["sync", "gpsimd", "tensor", "scalar", "vector"]:
                        engines[nm] = None
                        f = fns.get(nm)
                        if f:
                            f(None)
                return engines

            def barrier():
                barcnt[0] += 1
                tgt = 5 * barcnt[0]
                pfx = {"sync": "d", "scalar": "w", "gpsimd": "g"}

                def mk(nm, op):
                    def f(e):
                        pf = pfx.get(nm)
                        if pf:
                            for s in P.sem:
                                if s.startswith(pf) and P.cnt.get(s):
                                    P.wait_min(e, s, P.cnt[s])
                        i = op(e) if P.emitting else None
                        P.inc(i, "bar", ("bar", barcnt[0], nm), 1)
                        P.wait_min(e, "bar", tgt)
                    return f
                eng_sections(dict(
                    sync=mk("sync", lambda e: e.nop()),
                    gpsimd=mk("gpsimd", lambda e: e.memset(c_dmg[:], 0.0)),
                    tensor=mk("tensor", lambda e: e.nop()),
                    scalar=mk("scalar", lambda e: e.memzero(c_dma_[:])),
                    vector=mk("vector", lambda e: e.memset(c_dmv[:], 0.0)),
                ))

            with contextlib.ExitStack() as ph:
                bbp = (ph.enter_context(nc.psum_tensor("bbp", [128, 512], F32)) if E else None)

                def p0_sync(e):
                    loads = [
                        (c_id, identb), (c_rt, rotT), (c_tri, trimask2),
                        (c_rec, recent2), (c_utp, ut2prev), (c_ones, ones1),
                        (c_eb, ebias), (c_idf, identf),
                    ]
                    for i2, (dst, src) in enumerate(loads):
                        inst = e.dma_start(out=dst[:], in_=src) if E else None
                        P.inc(inst, "dcst", ("dcst", i2), 16)

                def p0_gp(e):
                    for w_src, w_dst, tag in [(woT, wo_bf_d, "cwo"), (wupT, wup_bf_d, "cwup")]:
                        for ch in range(8):
                            i = e.dma_start(out=w_dst[ch * 512:(ch + 1) * 512, :],
                                            in_=w_src[ch * 512:(ch + 1) * 512, :],
                                            max_dma_last_dim=8192) if E else None
                            P.inc(i, "gcwo" if tag == "cwo" else "gcwup", (tag, ch), 16)

                def p0_tensor(e):
                    P.wait_min(e, "dcst", 16 * 8)
                    i = e.matmul(bbp[:, 0:4], c_ones[:], c_eb[:], start=True, stop=True) if E else None
                    P.inc(i, "t", ("bbmm",))

                def p0_vector(e):
                    P.wait(e, ("bbmm",))
                    i = e.tensor_copy(c_bb[:], bbp[:, 0:4]) if E else None
                    P.inc(i, "v", ("bbev",))
                    i = e.memset(c_eps[:], 1e-5) if E else None
                    P.inc(i, "v", ("epss",))

                eng_sections(dict(sync=p0_sync, gpsimd=p0_gp, tensor=p0_tensor, vector=p0_vector))
                barrier()

            if upto >= 1:
                with contextlib.ExitStack() as ph:
                    xb = [(ph.enter_context(nc.sbuf_tensor(f"xb{i}", [128, HID], F32)) if E else None) for i in range(2)]
                    stats = (ph.enter_context(nc.sbuf_tensor("stats", [128, 8, 6], F32)) if E else None)
                    mv = (ph.enter_context(nc.sbuf_tensor("mv", [128, 2], F32)) if E else None)
                    nmn = (ph.enter_context(nc.sbuf_tensor("nmn", [128, 1], F32)) if E else None)
                    lnv = (ph.enter_context(nc.sbuf_tensor("lnv", [128, 1], F32)) if E else None)
                    rstd = (ph.enter_context(nc.sbuf_tensor("rstd", [128, 1], F32)) if E else None)
                    xnb = (ph.enter_context(nc.sbuf_tensor("xnb", [128, HID], BF16)) if E else None)
                    col = (ph.enter_context(nc.sbuf_tensor("col", [128, HC, 4, 128], BF16)) if E else None)
                    tp = [(ph.enter_context(nc.psum_tensor(f"p1tp{i}", [128, 1024], BF16)) if E else None) for i in range(2)]

                    jobs = [
                        ("full", x_full, 32, xnT_d, "dx0", "dx1", "wxnt"),
                        ("own", x_own, 16, xnoT_d, "dxo0", "dxo1", "wxno"),
                    ]

                    def p1_sync(e):
                        for jn, src, nt, dst, s0, s1, ws in jobs:
                            for t in range(nt):
                                slot = t % 2
                                sname = s0 if slot == 0 else s1
                                P.buf_claim(e, ("xb", slot), ("p1xn", jn, t))
                                i = e.dma_start(out=xb[slot][:], in_=src[t * 128:(t + 1) * 128, :]) if E else None
                                P.inc(i, sname, ("p1x", jn, t), 16)

                    def p1_vector(e):
                        for jn, src, nt, dst, s0, s1, ws in jobs:
                            for t in range(nt):
                                slot = t % 2
                                sname = s0 if slot == 0 else s1
                                P.wait_min(e, sname, 16 * (t // 2 + 1))
                                for i8 in range(8):
                                    i = e.bn_stats(stats[:, i8, :], xb[slot][:, i8 * 512:(i8 + 1) * 512]) if E else None
                                P.inc(i, "v", ("p1bn", jn, t))
                                P.wait(e, ("p1bn", jn, t))
                                i = e.bn_aggr(mv[:], stats[:]) if E else None
                                P.inc(i, "v", ("p1ag", jn, t))
                                P.wait(e, ("p1ag", jn, t))
                                i = e.tensor_scalar(nmn[:], mv[:, 0:1], -1.0, None, OP.mult) if E else None
                                P.inc(i, "v", ("p1nm", jn, t))
                                P.wait(e, ("p1nm", jn, t))
                                P.wait(e, ("p1rs", jn, t))
                                P.buf_claim(e, "xnb", ("p1tp", jn, t, 3))
                                i = e.tensor_scalar(xnb[:], xb[slot][:], nmn[:], rstd[:], OP.add, OP.mult) if E else None
                                P.inc(i, "v", ("p1xn", jn, t))

                    def p1_scalar(e):
                        for jn, src, nt, dst, s0, s1, ws in jobs:
                            for t in range(nt):
                                P.wait(e, ("p1ag", jn, t))
                                i = e.activation(lnv[:], mv[:, 1:2], AF.Ln, bias=c_eps[:], scale=1.0) if E else None
                                P.inc(i, "a", ("p1ln", jn, t))
                                P.wait(e, ("p1ln", jn, t))
                                i = e.activation(rstd[:], lnv[:], AF.Exp, scale=-0.5) if E else None
                                P.inc(i, "a", ("p1rs", jn, t))
                                for b in range(4):
                                    if t % 4 == 0 and b == 0:
                                        if jn == "own" and t == 0:
                                            P.wait_min(e, "wxnt", 16 * 8)
                                        g = t // 4
                                        P.wait_min(e, ws, 16 * g)
                                    P.wait(e, ("p1tp", jn, t, b))
                                    i = e.copy(col[:, b * 8:(b + 1) * 8, t % 4, :],
                                               tp[b % 2][:].rearrange("p (c n) -> p c n", n=128)) if E else None
                                    P.inc(i, "a", ("p1ev", jn, t, b))
                                if t % 4 == 3:
                                    g = t // 4
                                    P.wait(e, ("p1ev", jn, t, 3))
                                    for _tt in range(t - 3, t):
                                        P.wait(e, ("p1ev", jn, _tt, 3))
                                    if E:
                                        dd = dst.rearrange("(c p) s -> p c s", p=128)
                                        i = e.dma_start(out=dd[:, :, g * 512:(g + 1) * 512],
                                                        in_=col[:].rearrange("p c f n -> p c (f n)"))
                                    else:
                                        i = None
                                    P.inc(i, ws, ("p1st", jn, g), 16)

                    def p1_tensor(e):
                        for jn, src, nt, dst, s0, s1, ws in jobs:
                            for t in range(nt):
                                P.wait(e, ("p1xn", jn, t))
                                for b in range(4):
                                    P.psum_claim(e, ("p1", b % 2), [("p1ev", jn, t, b)])
                                    for i8 in range(8):
                                        hcix = b * 8 + i8
                                        i = e.transpose(tp[b % 2][:, i8 * 128:(i8 + 1) * 128],
                                                        xnb[:, hcix * 128:(hcix + 1) * 128], c_id[:]) if E else None
                                    P.inc(i, "t", ("p1tp", jn, t, b))

                    eng_sections(dict(sync=p1_sync, vector=p1_vector,
                                        scalar=p1_scalar, tensor=p1_tensor))
                    barrier()

            if upto >= 2:
                with contextlib.ExitStack() as ph:
                    wkb = (ph.enter_context(nc.sbuf_tensor("wkb", [128, HC, NKV * DH], BF16)) if E else None)
                    xnbk = (ph.enter_context(nc.sbuf_tensor("xnbk", [128, HC, 512], BF16)) if E else None)
                    ksb = (ph.enter_context(nc.sbuf_tensor("ksb", [128, 512], BF16)) if E else None)
                    kt1 = (ph.enter_context(nc.sbuf_tensor("kt1", [128, 512], F32)) if E else None)
                    kt2 = (ph.enter_context(nc.sbuf_tensor("kt2", [128, 512], F32)) if E else None)
                    krsb = (ph.enter_context(nc.sbuf_tensor("krsb", [128, 512], BF16)) if E else None)
                    kout = (ph.enter_context(nc.sbuf_tensor("kout", [128, 512], BF16)) if E else None)
                    cskl = (ph.enter_context(nc.sbuf_tensor("cskl", [128, SEQ], F32)) if E else None)
                    sskl = (ph.enter_context(nc.sbuf_tensor("sskl", [128, SEQ], F32)) if E else None)
                    kpr = [(ph.enter_context(nc.psum_tensor(f"kpr{i}", [128, 512], F32)) if E else None) for i in range(2)]
                    rpr = [(ph.enter_context(nc.psum_tensor(f"rpr{i}", [128, 512], F32)) if E else None) for i in range(2)]

                    def p2k_gp(e):
                        i = e.dma_start(out=wkb[:], in_=wkT.rearrange("(c p) n -> p c n", p=128)) if E else None
                        P.inc(i, "gwk", ("gwk",), 16)
                        P.wait_min(e, "dcst", 16 * 10)
                        for tb in range(8):
                            for kh in range(NKV):
                                n = tb * NKV + kh
                                P.wait(e, ("p2kev", tb, kh))
                                i = e.tensor_tensor(kt1[:], ksb[:], cskl[:, tb * 512:(tb + 1) * 512], OP.mult) if E else None
                                P.inc(i, "p", ("p2kt1", tb, kh))
                                P.wait(e, ("p2krev", tb, kh))
                                i = e.tensor_tensor(kt2[:], krsb[:], sskl[:, tb * 512:(tb + 1) * 512], OP.mult) if E else None
                                P.inc(i, "p", ("p2kt2", tb, kh))
                                P.wait(e, ("p2kt2", tb, kh))
                                P.wait_min(e, "wkt", 16 * n)
                                i = e.tensor_tensor(kout[:], kt1[:], kt2[:], OP.add) if E else None
                                P.inc(i, "p", ("p2kr3", tb, kh))

                    def p2k_sync(e):
                        i = e.dma_start(out=cskl[:], in_=cosk) if E else None
                        P.inc(i, "dcst", ("csk",), 16)
                        i = e.dma_start(out=sskl[:], in_=sink) if E else None
                        P.inc(i, "dcst", ("ssk",), 16)
                        for tb in range(8):
                            if tb >= 1:
                                P.wait(e, ("p2kmm", tb - 1, NKV - 1))
                            if E:
                                dd = xnT_d.rearrange("(c p) s -> p c s", p=128)
                                i = e.dma_start(out=xnbk[:], in_=dd[:, :, tb * 512:(tb + 1) * 512])
                            else:
                                i = None
                            P.inc(i, "dnb0", ("p2kx", tb), 16)

                    def p2k_tensor(e):
                        P.wait_min(e, "gwk", 16)
                        for tb in range(8):
                            P.wait_min(e, "dnb0", 16 * (tb + 1))
                            for kh in range(NKV):
                                s = kh % 2
                                P.psum_claim(e, ("p2kk", s), [("p2kev", tb, kh)])
                                for hc in range(HC):
                                    i = e.matmul(kpr[s][:], wkb[:, hc, kh * 128:(kh + 1) * 128],
                                                 xnbk[:, hc, :], start=(hc == 0), stop=(hc == HC - 1)) if E else None
                                P.inc(i, "t", ("p2kmm", tb, kh))
                                P.wait(e, ("p2kev", tb, kh))
                                P.psum_claim(e, ("p2kr", s), [("p2krev", tb, kh)])
                                i = e.matmul(rpr[s][:], c_rt[:], ksb[:], start=True, stop=True) if E else None
                                P.inc(i, "t", ("p2krk", tb, kh))

                    def p2k_scalar(e):
                        for tb in range(8):
                            for kh in range(NKV):
                                n = tb * NKV + kh
                                s = kh % 2
                                P.wait(e, ("p2kmm", tb, kh))
                                if n >= 1:
                                    ptb, pkh = divmod(n - 1, NKV)
                                    P.wait(e, ("p2kt1", ptb, pkh))
                                    P.wait(e, ("p2krk", ptb, pkh))
                                i = e.copy(ksb[:], kpr[s][:]) if E else None
                                P.inc(i, "a", ("p2kev", tb, kh))
                                P.wait(e, ("p2krk", tb, kh))
                                if n >= 1:
                                    ptb, pkh = divmod(n - 1, NKV)
                                    P.wait(e, ("p2kt2", ptb, pkh))
                                i = e.copy(krsb[:], rpr[s][:]) if E else None
                                P.inc(i, "a", ("p2krev", tb, kh))
                                P.wait(e, ("p2kr3", tb, kh))
                                i = e.dma_start(out=kT_d[kh, :, tb * 512:(tb + 1) * 512], in_=kout[:]) if E else None
                                P.inc(i, "wkt", ("p2kst", tb, kh), 16)

                    eng_sections(dict(gpsimd=p2k_gp, sync=p2k_sync,
                                        tensor=p2k_tensor, scalar=p2k_scalar))
                    barrier()

            if upto >= 3:
                with contextlib.ExitStack() as ph:
                    wvb = (ph.enter_context(nc.sbuf_tensor("wvb", [128, HC, NKV * DH], BF16)) if E else None)
                    xnbv = (ph.enter_context(nc.sbuf_tensor("xnbv", [128, HC, 512], BF16)) if E else None)
                    vsb = [(ph.enter_context(nc.sbuf_tensor(f"vsb{i}", [128, 512], BF16)) if E else None) for i in range(2)]
                    vpr = [(ph.enter_context(nc.psum_tensor(f"vpr{i}", [128, 512], F32)) if E else None) for i in range(2)]

                    def p2v_gp(e):
                        i = e.dma_start(out=wvb[:], in_=wvT.rearrange("(c p) n -> p c n", p=128)) if E else None
                        P.inc(i, "gwv", ("gwv",), 16)

                    def p2v_sync(e):
                        for tb in range(8):
                            if tb >= 1:
                                P.wait(e, ("p2vmm", tb - 1, 3, 1))
                            if E:
                                dd = xnT_d.rearrange("(c p) s -> p c s", p=128)
                                i = e.dma_start(out=xnbv[:], in_=dd[:, :, tb * 512:(tb + 1) * 512])
                            else:
                                i = None
                            P.inc(i, "dnb1", ("p2vx", tb), 16)

                    def p2v_tensor(e):
                        P.wait_min(e, "gwv", 16)
                        for tb in range(8):
                            P.wait_min(e, "dnb1", 16 * (tb + 1))
                            for tt in range(4):
                                for vbk in range(2):
                                    n = tb * 8 + tt * 2 + vbk
                                    s = n % 2
                                    P.psum_claim(e, ("p2v", s), [("p2vev", tb, tt, vbk)])
                                    for hc in range(HC):
                                        i = e.matmul(vpr[s][:], xnbv[:, hc, tt * 128:(tt + 1) * 128],
                                                     wvb[:, hc, vbk * 512:(vbk + 1) * 512],
                                                     start=(hc == 0), stop=(hc == HC - 1)) if E else None
                                    P.inc(i, "t", ("p2vmm", tb, tt, vbk))

                    def p2v_scalar(e):
                        for tb in range(8):
                            for tt in range(4):
                                for vbk in range(2):
                                    n = tb * 8 + tt * 2 + vbk
                                    s = n % 2
                                    wsem = "wvs0" if s == 0 else "wvs1"
                                    P.wait(e, ("p2vmm", tb, tt, vbk))
                                    P.wait_min(e, wsem, 16 * 4 * (n // 2))
                                    i = e.copy(vsb[s][:], vpr[s][:]) if E else None
                                    P.inc(i, "a", ("p2vev", tb, tt, vbk))
                                    P.wait(e, ("p2vev", tb, tt, vbk))
                                    trow = tb * 512 + tt * 128
                                    for hh in range(4):
                                        kh = vbk * 4 + hh
                                        i = e.dma_start(out=vS_d[kh, trow:trow + 128, :],
                                                        in_=vsb[s][:, hh * 128:(hh + 1) * 128]) if E else None
                                        P.inc(i, wsem, ("p2vst", tb, tt, vbk, hh), 16)

                    eng_sections(dict(gpsimd=p2v_gp, sync=p2v_sync,
                                        tensor=p2v_tensor, scalar=p2v_scalar))
                    barrier()

            if upto >= 4:
                with contextlib.ExitStack() as ph:
                    wqb = (ph.enter_context(nc.sbuf_tensor("wqb", [128, HC, 8 * DH], BF16)) if E else None)
                    xnbq = (ph.enter_context(nc.sbuf_tensor("xnbq", [128, HC, 512], BF16)) if E else None)
                    qsb = (ph.enter_context(nc.sbuf_tensor("qsb", [128, 512], BF16)) if E else None)
                    qt1 = (ph.enter_context(nc.sbuf_tensor("qt1", [128, 512], F32)) if E else None)
                    qt2 = (ph.enter_context(nc.sbuf_tensor("qt2", [128, 512], F32)) if E else None)
                    qrsb = (ph.enter_context(nc.sbuf_tensor("qrsb", [128, 512], BF16)) if E else None)
                    qout = (ph.enter_context(nc.sbuf_tensor("qout", [128, 512], BF16)) if E else None)
                    cql = (ph.enter_context(nc.sbuf_tensor("cql", [128, OWN], F32)) if E else None)
                    sql = (ph.enter_context(nc.sbuf_tensor("sql", [128, OWN], F32)) if E else None)
                    qpr = [(ph.enter_context(nc.psum_tensor(f"qpr{i}", [128, 512], F32)) if E else None) for i in range(2)]
                    qrp = [(ph.enter_context(nc.psum_tensor(f"qrp{i}", [128, 512], F32)) if E else None) for i in range(2)]

                    def p3_gp(e):
                        for qq in range(4):
                            if qq >= 1:
                                P.wait(e, ("p3mm", qq - 1, 3, 7))
                            if E:
                                i = e.dma_start(out=wqb[:],
                                                in_=wqT.rearrange("(c p) n -> p c n", p=128)[:, :, qq * 1024:(qq + 1) * 1024])
                            else:
                                i = None
                            P.inc(i, "gwq", ("gwq", qq), 16)
                            for sp in range(4):
                                for q8 in range(8):
                                    P.wait(e, ("p3ev", qq, sp, q8))
                                    i = e.tensor_tensor(qt1[:], qsb[:], cql[:, sp * 512:(sp + 1) * 512], OP.mult) if E else None
                                    P.inc(i, "p", ("p3t1", qq, sp, q8))
                                    P.wait(e, ("p3rev", qq, sp, q8))
                                    i = e.tensor_tensor(qt2[:], qrsb[:], sql[:, sp * 512:(sp + 1) * 512], OP.mult) if E else None
                                    P.inc(i, "p", ("p3t2", qq, sp, q8))
                                    P.wait(e, ("p3t2", qq, sp, q8))
                                    idx = (qq * 4 + sp) * 8 + q8
                                    P.wait_min(e, "wqt", 16 * idx)
                                    i = e.tensor_tensor(qout[:], qt1[:], qt2[:], OP.add) if E else None
                                    P.inc(i, "p", ("p3r3", qq, sp, q8))

                    def p3_sync(e):
                        i = e.dma_start(out=cql[:], in_=cosq) if E else None
                        P.inc(i, "dcst", ("csq",), 16)
                        i = e.dma_start(out=sql[:], in_=sinq) if E else None
                        P.inc(i, "dcst", ("ssq",), 16)
                        n = 0
                        for qq in range(4):
                            for sp in range(4):
                                if n >= 1:
                                    pq, psp = divmod(n - 1, 4)
                                    P.wait(e, ("p3mm", pq, psp, 7))
                                if E:
                                    dd = xnoT_d.rearrange("(c p) s -> p c s", p=128)
                                    i = e.dma_start(out=xnbq[:], in_=dd[:, :, sp * 512:(sp + 1) * 512])
                                else:
                                    i = None
                                P.inc(i, "dnb0", ("p3x", qq, sp), 16)
                                n += 1

                    def p3_tensor(e):
                        P.wait_min(e, "dcst", 16 * 12)
                        n = 0
                        for qq in range(4):
                            P.wait_min(e, "gwq", 16 * (qq + 1))
                            for sp in range(4):
                                n += 1
                                P.wait_min(e, "dnb0", 16 * (8 + n))
                                for q8 in range(8):
                                    s = q8 % 2
                                    P.psum_claim(e, ("p3q", s), [("p3ev", qq, sp, q8)])
                                    for hc in range(HC):
                                        i = e.matmul(qpr[s][:], wqb[:, hc, q8 * 128:(q8 + 1) * 128],
                                                     xnbq[:, hc, :], start=(hc == 0), stop=(hc == HC - 1)) if E else None
                                    P.inc(i, "t", ("p3mm", qq, sp, q8))
                                    P.wait(e, ("p3ev", qq, sp, q8))
                                    P.psum_claim(e, ("p3r", s), [("p3rev", qq, sp, q8)])
                                    i = e.matmul(qrp[s][:], c_rt[:], qsb[:], start=True, stop=True) if E else None
                                    P.inc(i, "t", ("p3rk", qq, sp, q8))

                    def p3_scalar(e):
                        for qq in range(4):
                            for sp in range(4):
                                for q8 in range(8):
                                    n = (qq * 4 + sp) * 8 + q8
                                    s = q8 % 2
                                    P.wait(e, ("p3mm", qq, sp, q8))
                                    if n >= 1:
                                        pq, prem = divmod(n - 1, 32)
                                        psp, pq8 = divmod(prem, 8)
                                        P.wait(e, ("p3t1", pq, psp, pq8))
                                        P.wait(e, ("p3rk", pq, psp, pq8))
                                    i = e.copy(qsb[:], qpr[s][:]) if E else None
                                    P.inc(i, "a", ("p3ev", qq, sp, q8))
                                    P.wait(e, ("p3rk", qq, sp, q8))
                                    if n >= 1:
                                        P.wait(e, ("p3t2", pq, psp, pq8))
                                    i = e.copy(qrsb[:], qrp[s][:]) if E else None
                                    P.inc(i, "a", ("p3rev", qq, sp, q8))
                                    P.wait(e, ("p3r3", qq, sp, q8))
                                    qh = qq * 8 + q8
                                    i = e.dma_start(out=qT_d[qh, :, sp * 512:(sp + 1) * 512], in_=qout[:]) if E else None
                                    P.inc(i, "wqt", ("p3st", qq, sp, q8), 16)

                    eng_sections(dict(gpsimd=p3_gp, sync=p3_sync,
                                        tensor=p3_tensor, scalar=p3_scalar))
                    barrier()

            if upto >= 5:
                with contextlib.ExitStack() as ph:
                    kbuf = [(ph.enter_context(nc.sbuf_tensor(f"kbuf{i}", [128, SEQ], BF16)) if E else None) for i in range(3)]
                    vbuf = [(ph.enter_context(nc.sbuf_tensor(f"vbuf{i}", [128, 32, 128], BF16)) if E else None) for i in range(3)]
                    qbuf = [(ph.enter_context(nc.sbuf_tensor(f"qbuf{i}", [128, 512], BF16)) if E else None) for i in range(8)]
                    Ssb = [(ph.enter_context(nc.sbuf_tensor(f"Ssb{i}", [128, SEQ], F32)) if E else None) for i in range(2)]
                    praw = [(ph.enter_context(nc.sbuf_tensor(f"praw{i}", [128, SEQ], BF16)) if E else None) for i in range(2)]
                    Pm = [(ph.enter_context(nc.sbuf_tensor(f"Pm{i}", [128, SEQ], BF16)) if E else None) for i in range(2)]
                    cmax = [(ph.enter_context(nc.sbuf_tensor(f"cmax{i}", [128, 2, 256], F32)) if E else None) for i in range(2)]
                    m8 = [(ph.enter_context(nc.sbuf_tensor(f"m8_{i}", [128, 2, 8], F32)) if E else None) for i in range(2)]
                    nm1 = [(ph.enter_context(nc.sbuf_tensor(f"nm1_{i}", [128, 2], F32)) if E else None) for i in range(2)]
                    K16 = [(ph.enter_context(nc.sbuf_tensor(f"K16_{i}", [128, 2, 256], F32)) if E else None) for i in range(2)]
                    KbD = [(ph.enter_context(nc.sbuf_tensor(f"KbD{i}", [128, 2, 256], F32)) if E else None) for i in range(2)]
                    KbP = [(ph.enter_context(nc.sbuf_tensor(f"KbP{i}", [128, 2, 128], F32)) if E else None) for i in range(2)]
                    racc = [(ph.enter_context(nc.sbuf_tensor(f"racc{i}", [128, 2, 3], F32)) if E else None) for i in range(2)]
                    rr = [(ph.enter_context(nc.sbuf_tensor(f"rr{i}", [128, 2], F32)) if E else None) for i in range(2)]
                    rcp = [(ph.enter_context(nc.sbuf_tensor(f"rcp{i}", [128, 2], F32)) if E else None) for i in range(2)]
                    PTs = [(ph.enter_context(nc.sbuf_tensor(f"PTs{i}", [128, 1024], BF16)) if E else None) for i in range(2)]
                    ob = [(ph.enter_context(nc.sbuf_tensor(f"ob{i}", [128, 128], BF16)) if E else None) for i in range(2)]
                    Rbs = [(ph.enter_context(nc.sbuf_tensor(f"Rbs{i}", [128, 2, 128], F32)) if E else None) for i in range(2)]
                    rrow = [(ph.enter_context(nc.sbuf_tensor(f"rrow{i}", [1, 256], F32)) if E else None) for i in range(2)]
                    SpsA = (ph.enter_context(nc.psum_tensor("SpsA", [128, 2048], F32)) if E else None)
                    PTp = [(ph.enter_context(nc.psum_tensor(f"PTp{i}", [128, 1024], BF16)) if E else None) for i in range(2)]
                    AVp = (ph.enter_context(nc.psum_tensor("AVp", [128, 512], F32)) if E else None)
                    RBp = (ph.enter_context(nc.psum_tensor("RBp", [128, 512], F32)) if E else None)

                    PAIRS = [(g, lhp, qt) for g in range(NGRP) for lhp in range(16) for qt in range(4)]
                    NP = len(PAIRS)

                    def subs_of(p):
                        g, lhp, qt = PAIRS[p]
                        return g, qt, (2 * lhp, 2 * lhp + 1)

                    kvmap = {}
                    qmap = {}
                    _kvi = 0
                    for g in range(NGRP):
                        for lhp in range(16):
                            if lhp % 2 == 0:
                                _kvi += 1
                            for s_ in range(2):
                                lh = 2 * lhp + s_
                                kvmap[(g, lh)] = _kvi - 1
                                qmap[(g, lh)] = 2 * (g * 16 + lhp) + s_

                    ptpar = {}
                    _c = 0
                    for p in range(NP):
                        g, qt, lhs2 = subs_of(p)
                        j, ext, n512, nkb, nbat = geom(g, qt)
                        for bat in range(nbat):
                            ptpar[(p, bat)] = _c % 2
                            _c += 1

                    def sev_groups(j, ext, n512):
                        tot = ext - 256
                        out = []
                        c = 0
                        while c < tot:
                            w = min(1024, tot - c)
                            out.append((c, w))
                            c += w
                        return out

                    def tile_readers(g, lh, qt, sub):
                        j, ext, n512, nkb, nbat = geom(g, qt)
                        grps = sev_groups(j, ext, n512)
                        out = []
                        for b in range(n512):
                            rd = []
                            for gi, (c, w) in enumerate(grps):
                                if c < (b + 1) * 512 and c + w > b * 512:
                                    rd.append(("sev", g, lh, qt, gi))
                            if b == n512 - 1:
                                rd.append(("dstt", g, lh, qt))
                            out.append(list(dict.fromkeys(rd)))
                        return out

                    def prev_pair(p):
                        return None if p == 0 else PAIRS[p - 1]

                    def pair_idx(g, lhp, qt):
                        return (g * 16 + lhp) * 4 + qt

                    def p4_sync(e):
                        P.wait_min(e, "wkt", 16 * 64)
                        P.wait_min(e, "wvs0", 16 * 128)
                        P.wait_min(e, "wvs1", 16 * 128)
                        P.wait_min(e, "wqt", 16 * 128)
                        LHS = [(g, lhp) for g in range(NGRP) for lhp in range(16)]

                        def emit_loads(li):
                            g, lhp = LHS[li]
                            ext_g = 1024 * (g + 1)
                            if lhp % 2 == 0:
                                kv = lhp // 2
                                kvi = kvmap[(g, 2 * lhp)]
                                slot = kvi % 3
                                if kvi >= 3:
                                    og, okv = divmod(kvi - 3, 8)
                                    olh = okv * 4 + 3
                                    oj, oext, on512, onkb, onbat = geom(og, 3)
                                    P.wait(e, ("qk", og, olh, 3, on512 - 1))
                                    P.wait(e, ("avb2", pair_idx(og, okv * 2 + 1, 3), onbat - 1))
                                i = e.dma_start(out=kbuf[slot][:, 0:ext_g], in_=kT_d[kv, :, 0:ext_g]) if E else None
                                P.inc(i, ["dk0", "dk1", "dk2"][slot], ("p4k", kvi), 16)
                                if E:
                                    i = e.dma_start(out=vbuf[slot][:, 0:8 * (g + 1), :],
                                                    in_=vS_d[kv].rearrange("(n p) d -> p n d", p=128)[:, 0:8 * (g + 1), :])
                                else:
                                    i = None
                                P.inc(i, ["dv0", "dv1", "dv2"][slot], ("p4v", kvi), 16)
                            for s_ in range(2):
                                lh = 2 * lhp + s_
                                qi = qmap[(g, lh)]
                                qslot = qi % 8
                                if qi >= 8:
                                    oli = li - 4
                                    og, olhp = LHS[oli]
                                    olh = 2 * olhp + s_
                                    oj, oext, on512, onkb, onbat = geom(og, 3)
                                    P.wait(e, ("qk", og, olh, 3, on512 - 1))
                                i = e.dma_start(out=qbuf[qslot][:], in_=qT_d[lh, :, g * 512:(g + 1) * 512]) if E else None
                                P.inc(i, ["dq0", "dq1", "dq2", "dq3", "dq4", "dq5", "dq6", "dq7"][qslot], ("p4q", qi), 16)

                        LOOK = 2
                        for li in range(LOOK + 1):
                            emit_loads(li)
                        oti = 0
                        for p in range(NP):
                            g, qt, lhs2 = PAIRS[p][0], PAIRS[p][2], None
                            g, lhp, qt = PAIRS[p]
                            if qt == 0:
                                li = g * 16 + lhp
                                if li + LOOK + 1 < len(LHS):
                                    emit_loads(li + LOOK + 1)
                            for s_ in range(2):
                                lh = 2 * lhp + s_
                                P.wait(e, ("oev", g, lh, qt))
                                wsem = "wot0" if oti % 2 == 0 else "wot1"
                                vt = 4 * g + qt
                                i2 = e.dma_start(out=oT_d[vt, lh, :, :], in_=ob[oti % 2][:]) if E else None
                                P.inc(i2, wsem, ("otst", g, lh, qt), 16)
                                oti += 1

                    def p4_tensor(e):
                        def emit_qk(p):
                            g, qt, lhs2 = subs_of(p)
                            j, ext, n512, nkb, nbat = geom(g, qt)
                            if qt == 0:
                                P.wait(e, ("p4k", kvmap[(g, lhs2[0])]))
                                P.wait(e, ("p4v", kvmap[(g, lhs2[0])]))
                                P.wait(e, ("p4q", qmap[(g, lhs2[0])]))
                                P.wait(e, ("p4q", qmap[(g, lhs2[1])]))
                            for s_, lh in enumerate(lhs2):
                                rds = tile_readers(g, lh, qt, s_)
                                for b in range(n512):
                                    bank = 2 * s_ + (b % 2)
                                    P.psum_claim(e, ("S", bank), rds[b])
                                    N = blkN(j, b, n512, ext)
                                    col = bank * 512
                                    i2 = e.matmul(SpsA[:, col:col + N],
                                                  qbuf[qmap[(g, lh)] % 8][:, qt * 128:(qt + 1) * 128],
                                                  kbuf[kvmap[(g, lh)] % 3][:, b * 512:b * 512 + N],
                                                  start=True, stop=True) if E else None
                                    P.inc(i2, "t", ("qk", g, lh, qt, b))

                        def emit_ptav(p):
                            g, qt, lhs2 = subs_of(p)
                            j, ext, n512, nkb, nbat = geom(g, qt)
                            par = p % 2
                            P.psum_claim(e, ("AV",), [("oev", g, lhs2[0], qt), ("oev", g, lhs2[1], qt)])
                            P.wait(e, ("rcp", g, lhs2[1], qt))
                            P.psum_claim(e, ("RB",), [("rrowev", g, lhs2[1], qt), ("rbev", g, lhs2[1], qt)])
                            for s_ in range(2):
                                i2 = e.transpose(RBp[0:1, 256 + s_ * 128:256 + (s_ + 1) * 128],
                                                 rcp[par][:, s_:s_ + 1], c_idf[:]) if E else None
                            P.inc(i2, "t", ("rbt", g, lhs2[1], qt))
                            def do_pt(bat):
                                lo = bat * 4
                                hi = min(nkb, lo + 4)
                                pslot = ptpar[(p, bat)]
                                P.psum_claim(e, ("PT", pslot), [("ptev2", p, bat)])
                                for k in range(lo, hi):
                                    for s_ in range(2):
                                        i2 = e.transpose(PTp[pslot][:, ((k - lo) * 2 + s_) * 128:((k - lo) * 2 + s_ + 1) * 128],
                                                         Pm[s_][:, k * 128:(k + 1) * 128], c_id[:]) if E else None
                                P.inc(i2, "t", ("pt2", p, bat))

                            def do_av(bat):
                                lo = bat * 4
                                hi = min(nkb, lo + 4)
                                pslot = ptpar[(p, bat)]
                                P.wait(e, ("ptev2", p, bat))
                                for k in range(lo, hi):
                                    i2 = e.matmul(AVp[:, 0:256],
                                                  vbuf[kvmap[(g, lhs2[0])] % 3][:, k, :],
                                                  PTs[pslot][:, (k - lo) * 256:(k - lo + 1) * 256],
                                                  start=(k == 0 and bat == 0), stop=(k == nkb - 1)) if E else None
                                P.inc(i2, "t", ("avb2", p, bat))

                            P.wait(e, ("pmask", g, lhs2[1], qt))
                            for bat in range(nbat):
                                do_pt(bat)
                                if bat >= 1:
                                    do_av(bat - 1)
                            P.wait(e, ("rrowev", g, lhs2[1], qt))
                            for s_ in range(2):
                                i2 = e.matmul(RBp[:, s_ * 128:(s_ + 1) * 128], c_ones[:],
                                              rrow[par][0:1, s_ * 128:(s_ + 1) * 128],
                                              start=True, stop=True) if E else None
                            P.inc(i2, "t", ("rbmm", g, lhs2[1], qt))
                            do_av(nbat - 1)

                        emit_qk(0)
                        for p in range(NP):
                            if p + 1 < NP:
                                emit_qk(p + 1)
                            emit_ptav(p)

                    def p4_scalar(e):
                        def emit_sev(p):
                            g, qt, lhs2 = subs_of(p)
                            j, ext, n512, nkb, nbat = geom(g, qt)
                            pp = prev_pair(p)
                            grps = sev_groups(j, ext, n512)
                            first = True
                            for s_, lh in enumerate(lhs2):
                                for gi, (c, w) in enumerate(grps):
                                    blo = c // 512
                                    bhi = (c + w - 1) // 512
                                    for b in range(blo, bhi + 1):
                                        P.wait(e, ("qk", g, lh, qt, b))
                                    if first and pp is not None:
                                        pg, plhp, pqt = pp
                                        P.wait(e, ("cmax", pg, 2 * plhp + 1, pqt))
                                        P.wait(e, ("exp", pg, 2 * plhp + 1, pqt))
                                    first = False
                                    i2 = e.copy(Ssb[s_][:, c:c + w],
                                                SpsA[:, s_ * 1024 + (c % 1024):s_ * 1024 + (c % 1024) + w]) if E else None
                                    P.inc(i2, "a", ("sev", g, lh, qt, gi))

                        def emit_exp(p):
                            g, qt, lhs2 = subs_of(p)
                            j, ext, n512, nkb, nbat = geom(g, qt)
                            par = p % 2
                            pp = prev_pair(p)
                            P.wait(e, ("nm1", g, lhs2[1], qt))
                            if pp is not None:
                                pg, plhp, pqt = pp
                                P.wait(e, ("pmask", pg, 2 * plhp + 1, pqt))
                            for s_, lh in enumerate(lhs2):
                                i2 = e.activation(praw[s_][:, 0:ext], Ssb[s_][:, 0:ext], AF.Exp,
                                                  bias=nm1[par][:, s_:s_ + 1], scale=1.0) if E else None
                                P.inc(i2, "a", ("exp", g, lh, qt))

                        def emit_aux(p):
                            g, qt, lhs2 = subs_of(p)
                            j, ext, n512, nkb, nbat = geom(g, qt)
                            par = p % 2
                            P.wait(e, ("rbt", g, lhs2[1], qt))
                            i2 = e.copy(rrow[par][:], RBp[0:1, 256:512]) if E else None
                            P.inc(i2, "a", ("rrowev", g, lhs2[1], qt))
                            for bat in range(nbat):
                                lo = bat * 4
                                hi = min(nkb, lo + 4)
                                pslot = ptpar[(p, bat)]
                                P.wait(e, ("pt2", p, bat))
                                P.buf_claim(e, ("PTs", pslot), ("avb2", p, bat))
                                i2 = e.copy(PTs[pslot][:, 0:(hi - lo) * 256],
                                            PTp[pslot][:, 0:(hi - lo) * 256]) if E else None
                                P.inc(i2, "a", ("ptev2", p, bat))

                        emit_sev(0)
                        for p in range(NP):
                            emit_exp(p)
                            if p + 1 < NP:
                                emit_sev(p + 1)
                            emit_aux(p)

                    def p4_vector(e):
                        def emit_s3(p):
                            g, qt, lhs2 = subs_of(p)
                            j, ext, n512, nkb, nbat = geom(g, qt)
                            par = p % 2
                            pp = prev_pair(p)
                            b = n512 - 1
                            N = blkN(j, b, n512, ext)
                            off = N - 256
                            ngr = len(sev_groups(j, ext, n512))
                            for s_, lh in enumerate(lhs2):
                                bank = 2 * s_ + (b % 2)
                                P.wait(e, ("qk", g, lh, qt, b))
                                if s_ == 0 and pp is not None:
                                    pg, plhp, pqt = pp
                                    P.wait(e, ("exp", pg, 2 * plhp + 1, pqt))
                                if j % 2 == 1 and ngr >= 1:
                                    P.wait(e, ("sev", g, lh, qt, ngr - 1))
                                i2 = e.scalar_tensor_tensor(Ssb[s_][:, ext - 256:ext],
                                                            SpsA[:, bank * 512 + off:bank * 512 + off + 256],
                                                            1.0, c_tri[:], OP.mult, OP.add) if E else None
                                P.inc(i2, "v", ("dstt", g, lh, qt))
                            for s_, lh in enumerate(lhs2):
                                if ngr >= 1:
                                    P.wait(e, ("sev", g, lh, qt, ngr - 1))
                                P.wait(e, ("dstt", g, lh, qt))
                                i2 = e.tensor_reduce(cmax[par][:, s_, 0:16 * (j + 1)],
                                                     Ssb[s_][:, 0:ext].rearrange("p (c s) -> p c s", s=16),
                                                     AX.X, OP.max) if E else None
                                if j < 15:
                                    i2 = e.memset(cmax[par][:, s_, 16 * (j + 1):256], NEG) if E else None
                                P.inc(i2, "v", ("cmax", g, lh, qt))
                            P.wait(e, ("cmax", g, lhs2[1], qt))
                            for s_, lh in enumerate(lhs2):
                                i2 = e.max(m8[par][:, s_, :], cmax[par][:, s_, :]) if E else None
                                P.inc(i2, "v", ("max8", g, lh, qt))
                            P.wait(e, ("max8", g, lhs2[1], qt))
                            for s_, lh in enumerate(lhs2):
                                i2 = e.memset(K16[par][:, s_, 0:8], 1.0) if E else None
                                i2 = e.tensor_scalar(K16[par][:, s_, 8:256], cmax[par][:, s_, 8:256],
                                                     m8[par][:, s_, 7:8], None, OP.is_ge) if E else None
                                P.inc(i2, "v", ("k16", g, lh, qt))
                                i2 = e.tensor_scalar(nm1[par][:, s_:s_ + 1], m8[par][:, s_, 0:1],
                                                     -1.0, None, OP.mult) if E else None
                            P.inc(i2, "v", ("nm1", g, lhs2[1], qt))

                        def emit_s5(p):
                            g, qt, lhs2 = subs_of(p)
                            j, ext, n512, nkb, nbat = geom(g, qt)
                            par = p % 2
                            P.wait(e, ("k16", g, lhs2[1], qt))
                            for s_, lh in enumerate(lhs2):
                                i2 = e.tensor_tensor(KbD[par][:, s_, :], bc16(K16[par][:, s_, 16 * j:16 * j + 16]),
                                                     c_rec[:].rearrange("p (c s) -> p c s", s=16), OP.max) if E else None
                                P.inc(i2, "v", ("kbd", g, lh, qt))
                                if j >= 1:
                                    i2 = e.tensor_tensor(KbP[par][:, s_, :], bc16(K16[par][:, s_, 16 * j - 8:16 * j]),
                                                         c_utp[:].rearrange("p (c s) -> p c s", s=16), OP.max) if E else None
                                    P.inc(i2, "v", ("kbp", g, lh, qt))
                            P.wait(e, ("exp", g, lhs2[1], qt))
                            P.wait(e, ("kbd", g, lhs2[1], qt))
                            if j >= 1:
                                P.wait(e, ("kbp", g, lhs2[1], qt))
                            for s_, lh in enumerate(lhs2):
                                P.buf_claim(e, ("Pm", s_), ("pt2", p, nbat - 1))
                                if j >= 1:
                                    i2 = e.scalar_tensor_tensor(
                                        Pm[s_][:, 0:256 * j - 128].rearrange("p (c s) -> p c s", s=16),
                                        praw[s_][:, 0:256 * j - 128].rearrange("p (c s) -> p c s", s=16),
                                        1.0, bc16(K16[par][:, s_, 0:16 * j - 8]),
                                        OP.mult, OP.mult, accum_out=racc[par][:, s_, 0:1]) if E else None
                                    i2 = e.scalar_tensor_tensor(
                                        Pm[s_][:, 256 * j - 128:256 * j], praw[s_][:, 256 * j - 128:256 * j],
                                        1.0, KbP[par][:, s_, :], OP.mult, OP.mult,
                                        accum_out=racc[par][:, s_, 1:2]) if E else None
                                i2 = e.scalar_tensor_tensor(
                                    Pm[s_][:, 256 * j:ext], praw[s_][:, 256 * j:ext],
                                    1.0, KbD[par][:, s_, :], OP.mult, OP.mult,
                                    accum_out=racc[par][:, s_, 2:3]) if E else None
                                P.inc(i2, "v", ("pmask", g, lh, qt))
                            P.wait(e, ("pmask", g, lhs2[1], qt))
                            for s_, lh in enumerate(lhs2):
                                if j >= 1:
                                    i2 = e.tensor_reduce(rr[par][:, s_:s_ + 1], racc[par][:, s_, :], AX.X, OP.add) if E else None
                                else:
                                    i2 = e.tensor_copy(rr[par][:, s_:s_ + 1], racc[par][:, s_, 2:3]) if E else None
                                P.inc(i2, "v", ("rsum", g, lh, qt))
                            P.wait(e, ("rsum", g, lhs2[1], qt))
                            i2 = e.reciprocal(rcp[par][:], rr[par][:]) if E else None
                            P.inc(i2, "v", ("rcp", g, lhs2[1], qt))

                        def emit_fin(p):
                            g, qt, lhs2 = subs_of(p)
                            j, ext, n512, nkb, nbat = geom(g, qt)
                            par = p % 2
                            P.wait(e, ("rbmm", g, lhs2[1], qt))
                            i2 = e.tensor_copy(Rbs[par][:], RBp[:, 0:256].rearrange("p (s d) -> p s d", d=128)) if E else None
                            P.inc(i2, "v", ("rbev", g, lhs2[1], qt))
                            P.wait(e, ("rbev", g, lhs2[1], qt))
                            P.wait(e, ("avb2", p, nbat - 1))
                            for s_, lh in enumerate(lhs2):
                                oti = 2 * p + s_
                                wsem = "wot0" if oti % 2 == 0 else "wot1"
                                P.wait_min(e, wsem, 16 * (oti // 2))
                                i2 = e.tensor_tensor(ob[oti % 2][:], AVp[:, s_ * 128:(s_ + 1) * 128],
                                                     Rbs[par][:, s_, :], OP.mult) if E else None
                                P.inc(i2, "v", ("oev", g, lh, qt))

                        emit_s3(0)
                        for p in range(NP):
                            if p + 1 < NP:
                                emit_s3(p + 1)
                            emit_s5(p)
                            if p >= 1:
                                emit_fin(p - 1)
                        emit_fin(NP - 1)

                    eng_sections(dict(sync=p4_sync, tensor=p4_tensor,
                                        scalar=p4_scalar, vector=p4_vector))
                    barrier()


            if upto >= 6:
                with contextlib.ExitStack() as ph:
                    wob = [(ph.enter_context(nc.sbuf_tensor(f"wob{i}", [128, HC, 512], BF16)) if E else None) for i in range(2)]
                    wub = [(ph.enter_context(nc.sbuf_tensor(f"wub{i}", [128, HC, 512], BF16)) if E else None) for i in range(2)]
                    Ab = (ph.enter_context(nc.sbuf_tensor("Ab", [128, HC, 128], BF16)) if E else None)
                    Bb = (ph.enter_context(nc.sbuf_tensor("Bb", [128, HID], BF16)) if E else None)
                    otb = [(ph.enter_context(nc.sbuf_tensor(f"otb{i}", [128, 32, 128], BF16)) if E else None) for i in range(2)]
                    xnl = [(ph.enter_context(nc.sbuf_tensor(f"xnl{i}", [128, HC, 128], BF16)) if E else None) for i in range(2)]
                    xsl = [(ph.enter_context(nc.sbuf_tensor(f"xsl{i}", [128, 512], F32)) if E else None) for i in range(2)]
                    zT = [(ph.enter_context(nc.sbuf_tensor(f"zT{i}", [128, 128], BF16)) if E else None) for i in range(NVT)]
                    lg = (ph.enter_context(nc.sbuf_tensor("lg", [128, 4], F32)) if E else None)
                    mmx = (ph.enter_context(nc.sbuf_tensor("mmx", [128, 1], F32)) if E else None)
                    ge4 = (ph.enter_context(nc.sbuf_tensor("ge4", [128, 4], F32)) if E else None)
                    oh4 = (ph.enter_context(nc.sbuf_tensor("oh4", [128, 4], F32)) if E else None)
                    sc1 = (ph.enter_context(nc.sbuf_tensor("sc1", [128, 1], F32)) if E else None)
                    sc2 = (ph.enter_context(nc.sbuf_tensor("sc2", [128, 1], F32)) if E else None)
                    zsel = (ph.enter_context(nc.sbuf_tensor("zsel", [128, 128], BF16)) if E else None)
                    gsl = (ph.enter_context(nc.sbuf_tensor("gsl", [128, 512], F32)) if E else None)
                    ssl2 = (ph.enter_context(nc.sbuf_tensor("ssl2", [128, 512], F32)) if E else None)
                    sg = (ph.enter_context(nc.sbuf_tensor("sg", [128, 512], F32)) if E else None)
                    yb = (ph.enter_context(nc.sbuf_tensor("yb", [128, 512], F32)) if E else None)
                    spsP = (ph.enter_context(nc.psum_tensor("spsP", [128, 512], F32)) if E else None)
                    fpsP = (ph.enter_context(nc.psum_tensor("fpsP", [128, 512], F32)) if E else None)
                    zpsP = (ph.enter_context(nc.psum_tensor("zpsP", [128, 512], F32)) if E else None)
                    ztpP = (ph.enter_context(nc.psum_tensor("ztpP", [128, 512], BF16)) if E else None)

                    def p6_gp(e):
                        i = e.dma_start(out=Ab[:], in_=A_padT.rearrange("(c p) n -> p c n", p=128)) if E else None
                        P.inc(i, "gab", ("gA",), 16)
                        i = e.dma_start(out=Bb[:], in_=B_padT) if E else None
                        P.inc(i, "gab", ("gB",), 16)
                        for mb in range(8):
                            for lt in range(NVT):
                                idx = mb * NVT + lt
                                P.wait(e, ("sev6", mb, lt))
                                P.wait(e, ("silu", mb, lt))
                                i = e.tensor_tensor(sg[:], ssl2[:], gsl[:], OP.mult) if E else None
                                P.inc(i, "p", ("sg", mb, lt))
                                P.wait(e, ("sg", mb, lt))
                                P.wait_min(e, "wy", 16 * idx)
                                P.wait_min(e, "dxs0" if idx % 2 == 0 else "dxs1", 16 * (idx // 2 + 1))
                                i = e.tensor_tensor(yb[:], sg[:], xsl[idx % 2][:], OP.add) if E else None
                                P.inc(i, "p", ("yb", mb, lt))

                    def p6_sync(e):
                        P.wait_min(e, "wot0", 16 * 256)
                        P.wait_min(e, "wot1", 16 * 256)
                        P.wait_min(e, "wxno", 16 * 4)
                        P.wait_min(e, "gcwo", 16 * 8)
                        P.wait_min(e, "gcwup", 16 * 8)

                        def wload(mb):
                            if E:
                                i = e.dma_start(out=wob[mb % 2][:],
                                                in_=wo_bf_d.rearrange("(c p) m -> p c m", p=128)[:, :, mb * 512:(mb + 1) * 512])
                            else:
                                i = None
                            P.inc(i, "gwo" if mb % 2 == 0 else "gwo1", ("gwo", mb), 16)
                            if E:
                                i = e.dma_start(out=wub[mb % 2][:],
                                                in_=wup_bf_d.rearrange("(c p) m -> p c m", p=128)[:, :, mb * 512:(mb + 1) * 512])
                            else:
                                i = None
                            P.inc(i, "gwup" if mb % 2 == 0 else "gwup1", ("gwup", mb), 16)

                        wload(0)
                        wload(1)
                        n = 0
                        for mb in range(8):
                            if mb >= 2:
                                P.wait(e, ("op", mb - 2, NVT - 1))
                                P.wait(e, ("fp", mb - 2, NVT - 1))
                                wload(mb)
                            for lt in range(NVT):
                                slot = n % 2
                                if n >= 2:
                                    pmb, plt = divmod(n - 2, NVT)
                                    P.wait(e, ("op", pmb, plt))
                                    P.wait(e, ("fp", pmb, plt))
                                    P.wait(e, ("yb", pmb, plt))
                                if E:
                                    i = e.dma_start(out=otb[slot][:], in_=oT_d[lt].rearrange("h d q -> d h q"))
                                else:
                                    i = None
                                P.inc(i, "dot0" if slot == 0 else "dot1", ("p6o", mb, lt), 16)
                                if E:
                                    dd = xnoT_d.rearrange("(c p) s -> p c s", p=128)
                                    i = e.dma_start(out=xnl[slot][:], in_=dd[:, :, lt * 128:(lt + 1) * 128])
                                else:
                                    i = None
                                P.inc(i, "dxl0" if slot == 0 else "dxl1", ("p6n", mb, lt), 16)
                                i = e.dma_start(out=xsl[slot][:],
                                                in_=x_own[lt * 128:(lt + 1) * 128, mb * 512:(mb + 1) * 512]) if E else None
                                P.inc(i, "dxs0" if slot == 0 else "dxs1", ("p6x", mb, lt), 16)
                                n += 1

                    def p6_tensor(e):
                        P.wait_min(e, "gab", 32)
                        n = 0
                        for mb in range(8):
                            P.wait_min(e, "gwo" if mb % 2 == 0 else "gwo1", 16 * (mb // 2 + 1))
                            P.wait_min(e, "gwup" if mb % 2 == 0 else "gwup1", 16 * (mb // 2 + 1))
                            for lt in range(NVT):
                                slot = n % 2
                                sname_o = "dot0" if slot == 0 else "dot1"
                                sname_n = "dxl0" if slot == 0 else "dxl1"
                                P.wait_min(e, sname_o, 16 * (n // 2 + 1))
                                P.wait_min(e, sname_n, 16 * (n // 2 + 1))
                                if mb == 0:
                                    P.psum_claim(e, ("z",), [("zsel", lt)])
                                    for hc in range(HC):
                                        i = e.matmul(zpsP[:, 0:128], xnl[slot][:, hc, :], Ab[:, hc, :],
                                                     start=(hc == 0), stop=(hc == HC - 1)) if E else None
                                    P.inc(i, "t", ("zmm", lt))
                                    P.wait(e, ("zsel", lt))
                                    P.psum_claim(e, ("zt",), [("ztev", lt)])
                                    i = e.transpose(ztpP[:, 0:128], zsel[:], c_id[:]) if E else None
                                    P.inc(i, "t", ("ztr", lt))
                                P.psum_claim(e, ("sps",), [("sev6", mb, lt)])
                                for hd in range(NQH):
                                    i = e.matmul(spsP[:], otb[slot][:, hd, :], wob[mb % 2][:, hd, :],
                                                 start=(hd == 0), stop=(hd == NQH - 1)) if E else None
                                P.inc(i, "t", ("op", mb, lt))
                                P.psum_claim(e, ("fps",), [("silu", mb, lt)])
                                P.wait(e, ("ztev", lt))
                                for hc in range(HC):
                                    i = e.matmul(fpsP[:], xnl[slot][:, hc, :], wub[mb % 2][:, hc, :],
                                                 start=(hc == 0), stop=False) if E else None
                                i = e.matmul(fpsP[:], zT[lt][:], Bb[:, mb * 512:(mb + 1) * 512],
                                             start=False, stop=True) if E else None
                                P.inc(i, "t", ("fp", mb, lt))
                                n += 1

                    def p6_vector(e):
                        for lt in range(NVT):
                            P.wait(e, ("zmm", lt))
                            if E:
                                zz = zpsP[:, 0:128].rearrange("p (e r) -> p e r", r=32)[:, :, 31:32]
                                cb = bass.AP(c_bb[:].tensor, c_bb[:].offset, list(c_bb[:].ap) + [[0, 1]])
                                i = e.tensor_tensor(lg[:], zz, cb, OP.add)
                            else:
                                i = None
                            P.inc(i, "v", ("lg", lt))
                            P.wait(e, ("lg", lt))
                            i = e.tensor_reduce(mmx[:], lg[:], AX.X, OP.max) if E else None
                            P.inc(i, "v", ("lmx", lt))
                            P.wait(e, ("lmx", lt))
                            i = e.tensor_scalar(ge4[:], lg[:], mmx[:], None, OP.is_ge) if E else None
                            P.inc(i, "v", ("ge4", lt))
                            P.wait(e, ("ge4", lt))
                            i = e.tensor_copy(oh4[:, 0:1], ge4[:, 0:1]) if E else None
                            P.inc(i, "v", ("oh0", lt))
                            P.wait(e, ("oh0", lt))
                            i = e.tensor_scalar(sc1[:], oh4[:, 0:1], -1.0, 1.0, OP.mult, OP.add) if E else None
                            P.inc(i, "v", ("nh0", lt))
                            P.wait(e, ("nh0", lt))
                            i = e.tensor_tensor(oh4[:, 1:2], ge4[:, 1:2], sc1[:], OP.mult) if E else None
                            P.inc(i, "v", ("oh1", lt))
                            P.wait(e, ("oh1", lt))
                            i = e.tensor_scalar(sc2[:], oh4[:, 1:2], -1.0, 1.0, OP.mult, OP.add) if E else None
                            P.inc(i, "v", ("nh1", lt))
                            P.wait(e, ("nh1", lt))
                            i = e.tensor_tensor(sc1[:], sc1[:], sc2[:], OP.mult) if E else None
                            P.inc(i, "v", ("n01", lt))
                            P.wait(e, ("n01", lt))
                            i = e.tensor_tensor(oh4[:, 2:3], ge4[:, 2:3], sc1[:], OP.mult) if E else None
                            P.inc(i, "v", ("oh2", lt))
                            P.wait(e, ("oh2", lt))
                            i = e.tensor_scalar(sc2[:], oh4[:, 2:3], -1.0, 1.0, OP.mult, OP.add) if E else None
                            P.inc(i, "v", ("nh2", lt))
                            P.wait(e, ("nh2", lt))
                            i = e.tensor_tensor(sc1[:], sc1[:], sc2[:], OP.mult) if E else None
                            P.inc(i, "v", ("n012", lt))
                            P.wait(e, ("n012", lt))
                            i = e.tensor_tensor(oh4[:, 3:4], ge4[:, 3:4], sc1[:], OP.mult) if E else None
                            P.inc(i, "v", ("oh3", lt))
                            P.wait(e, ("oh3", lt))
                            for ex in range(4):
                                i = e.tensor_scalar(zsel[:, ex * 32:(ex + 1) * 32], zpsP[:, ex * 32:(ex + 1) * 32],
                                                    oh4[:, ex:ex + 1], None, OP.mult) if E else None
                            P.inc(i, "v", ("zsel", lt))
                            P.wait(e, ("ztr", lt))
                            i = e.tensor_copy(zT[lt][:], ztpP[:, 0:128]) if E else None
                            P.inc(i, "v", ("ztev", lt))

                    def p6_scalar(e):
                        for mb in range(8):
                            for lt in range(NVT):
                                idx = mb * NVT + lt
                                P.wait(e, ("op", mb, lt))
                                if idx >= 1:
                                    pmb, plt = divmod(idx - 1, NVT)
                                    P.wait(e, ("sg", pmb, plt))
                                i = e.copy(ssl2[:], spsP[:]) if E else None
                                P.inc(i, "a", ("sev6", mb, lt))
                                P.wait(e, ("fp", mb, lt))
                                i = e.activation(gsl[:], fpsP[:], AF.Silu) if E else None
                                P.inc(i, "a", ("silu", mb, lt))
                                P.wait(e, ("yb", mb, lt))
                                i = e.dma_start(out=y_out[lt * 128:(lt + 1) * 128, mb * 512:(mb + 1) * 512],
                                                in_=yb[:]) if E else None
                                P.inc(i, "wy", ("yst", mb, lt), 16)

                    eng_sections(dict(gpsimd=p6_gp, sync=p6_sync, tensor=p6_tensor,
                                        vector=p6_vector, scalar=p6_scalar))

                    def fin_sync(e):
                        P.wait_min(e, "wy", 16 * 128)
                    eng_sections(dict(sync=fin_sync))

            if upto < 6:
                def dbg_fin(e):
                    for s in ["wxnt", "wxno", "wkt", "wvs0", "wvs1", "wqt",
                              "wot0", "wot1"]:
                        if P.cnt.get(s):
                            P.wait_min(e, s, P.cnt[s])
                eng_sections(dict(sync=dbg_fin))

        P.begin(False)
        run()
        P.begin(True)
        run()

    return nc


_CACHED = {}


def kernel(L0_attn_hidden_states, L0_attn_wq, L0_attn_wk, L0_attn_wv, L0_attn_wo,
           L0_ffn_W_up, L0_ffn_gate_w, L0_ffn_expert_bias, L0_ffn_lora_A, L0_ffn_lora_B):
    x = np.ascontiguousarray(np.asarray(L0_attn_hidden_states, dtype=np.float32))
    wq = np.asarray(L0_attn_wq, np.float32)
    wk = np.asarray(L0_attn_wk, np.float32)
    wv = np.asarray(L0_attn_wv, np.float32)
    wo = np.asarray(L0_attn_wo, np.float32)
    wup = np.asarray(L0_ffn_W_up, np.float32)
    gw = np.asarray(L0_ffn_gate_w, np.float32)
    eb = np.asarray(L0_ffn_expert_bias, np.float32)
    lA = np.asarray(L0_ffn_lora_A, np.float32)
    lB = np.asarray(L0_ffn_lora_B, np.float32)

    inv = 1.0 / (10000.0 ** (np.arange(0, DH, 2, dtype=np.float32) / DH))
    ang = np.arange(SEQ, dtype=np.float32)[:, None] * inv[None, :]
    emb = np.concatenate([ang, ang], axis=1)
    cosf = np.cos(emb).astype(np.float32)
    sinf = np.sin(emb).astype(np.float32)
    scale = 1.0 / math.sqrt(DH)
    identb = np.eye(128, dtype=ml_dtypes.bfloat16)
    R = np.zeros((128, 128), np.float32)
    R[0:64, 64:128] = -np.eye(64)
    R[64:128, 0:64] = np.eye(64)
    rotT = np.ascontiguousarray(R.T).astype(ml_dtypes.bfloat16)
    ones1 = np.ones((1, 128), np.float32)

    A_padT = np.zeros((HID, 128), np.float32)
    for ex in range(4):
        A_padT[:, ex * 32:ex * 32 + 8] = lA[ex].T
        A_padT[:, ex * 32 + 31] = gw[ex]
    B_padT = np.zeros((128, HID), np.float32)
    for ex in range(4):
        B_padT[ex * 32:ex * 32 + 8, :] = lB[ex].T
    ebias = eb.reshape(1, 4).astype(np.float32)

    pcol = np.arange(128)[:, None]
    j1 = np.arange(256)[None, :]
    j2 = np.arange(128)[None, :]

    wqT = np.ascontiguousarray(wq.T)
    wkT = np.ascontiguousarray(wk.T)
    wvT = np.ascontiguousarray(wv.T)
    woT = np.ascontiguousarray(wo.T)
    wupT = np.ascontiguousarray(wup.T)

    in_maps = []
    for c in range(8):
        b, h = divmod(c, 2)
        own = np.arange(OWN) * 2 + h
        trim = np.where(j1 <= 2 * pcol + h, 0.0, NEG).astype(np.float32)
        rec = (j1 > 2 * pcol + h - 128).astype(np.float32)
        utp = (j2 > 2 * pcol + h).astype(np.float32)
        in_maps.append(dict(
            x_full=x[b],
            x_own=np.ascontiguousarray(x[b, own]),
            wqT=wqT, wkT=wkT, wvT=wvT, woT=woT, wupT=wupT,
            A_padT=A_padT, B_padT=B_padT, ebias=ebias,
            cosq=np.ascontiguousarray((cosf[own] * scale).T),
            sinq=np.ascontiguousarray((sinf[own] * scale).T),
            cosk=np.ascontiguousarray(cosf.T),
            sink=np.ascontiguousarray(sinf.T),
            identb=identb, identf=np.eye(128, dtype=np.float32), rotT=rotT,
            trimask2=trim, recent2=rec, ut2prev=utp,
            ones1=ones1,
        ))

    if "prog" not in _CACHED:
        _CACHED["prog"] = build_program()
    nc = _CACHED["prog"]

    res = run_bass_kernel_spmd(nc, in_maps, core_ids=list(range(8)))

    out = np.empty((NB, SEQ, HID), np.float32)
    for c in range(8):
        b, h = divmod(c, 2)
        out[b, h::2] = np.asarray(res.results[c]["y"], np.float32)
    return out


# version 08
# speedup vs baseline: 1.0847x; 1.0506x over previous
import contextlib
import math

import numpy as np
import ml_dtypes

import concourse.bass as bass
import concourse.mybir as mybir
from concourse.alu_op_type import AluOpType as OP
from concourse.bass_utils import run_bass_kernel_spmd

F32 = mybir.dt.float32
BF16 = mybir.dt.bfloat16
AF = mybir.ActivationFunctionType
AX = mybir.AxisListType

NEG = float(np.finfo(np.float32).min)
HID = 4096
SEQ = 4096
NB = 4
NQH = 32
NKV = 8
DH = 128
OWN = 2048
NVT = 16
NGRP = 4
HC = 32


def bc16(ap, reps=16):
    return bass.AP(ap.tensor, ap.offset, list(ap.ap) + [[0, reps]])


class Prog:
    def __init__(self):
        self.plan = {}
        self.emitting = False
        self.cnt = {}
        self.sem = {}

    def begin(self, emitting):
        self.emitting = emitting
        self.cnt = {}
        self.psum_readers = {}
        self.buf_consumer = {}

    def inc(self, inst, semname, key, n=1):
        c = self.cnt.get(semname, 0) + n
        self.cnt[semname] = c
        if self.emitting:
            assert self.plan[key] == (semname, c), (key, semname, c, self.plan.get(key))
            inst.then_inc(self.sem[semname], n)
        else:
            assert key not in self.plan, key
            self.plan[key] = (semname, c)

    def wait(self, eng, key):
        if self.emitting:
            semname, c = self.plan[key]
            eng.wait_ge(self.sem[semname], c)

    def wait_min(self, eng, semname, c):
        if self.emitting and c > 0:
            eng.wait_ge(self.sem[semname], c)

    def psum_claim(self, eng, bank, reader_keys):
        prev = self.psum_readers.get(bank)
        if prev:
            for k in prev:
                self.wait(eng, k)
        self.psum_readers[bank] = list(reader_keys)

    def buf_claim(self, eng, buf, consumer_key):
        prev = self.buf_consumer.get(buf)
        if prev is not None:
            self.wait(eng, prev)
        self.buf_consumer[buf] = consumer_key


def geom(g, qt):
    j = 4 * g + qt
    ext = 256 * (j + 1)
    n512 = (j + 2) // 2
    nkb = 2 * (j + 1)
    nbat = (nkb + 3) // 4
    return j, ext, n512, nkb, nbat


def blkN(j, b, n512, ext):
    return 512 if b < n512 - 1 else ext - 512 * (n512 - 1)


def build_program(upto=99, debug=False):
    nc = bass.Bass("TRN2", target_bir_lowering=False)

    def din(name, shape, dt=F32):
        return nc.dram_tensor(name, list(shape), dt, kind="ExternalInput").ap()

    x_full = din("x_full", (SEQ, HID))
    x_own = din("x_own", (OWN, HID))
    wqT = din("wqT", (HID, NQH * DH))
    wkT = din("wkT", (HID, NKV * DH))
    wvT = din("wvT", (HID, NKV * DH))
    woT = din("woT", (HID, HID))
    wupT = din("wupT", (HID, HID))
    A_padT = din("A_padT", (HID, 128))
    B_padT = din("B_padT", (128, HID))
    ebias = din("ebias", (1, 4))
    cosq = din("cosq", (DH, OWN))
    sinq = din("sinq", (DH, OWN))
    cosk = din("cosk", (DH, SEQ))
    sink = din("sink", (DH, SEQ))
    identb = din("identb", (128, 128), BF16)
    identf = din("identf", (128, 128))
    rotT = din("rotT", (128, 128), BF16)
    trimask2 = din("trimask2", (128, 256))
    recent2 = din("recent2", (128, 256))
    ut2prev = din("ut2prev", (128, 128))
    ones1 = din("ones1", (1, 128))

    y_out = nc.dram_tensor("y", [OWN, HID], F32, kind="ExternalOutput").ap()

    KIND = "ExternalOutput" if debug else "Internal"
    xnT_d = nc.dram_tensor("xnT_d", [HID, SEQ], BF16, kind=KIND).ap()
    xnoT_d = nc.dram_tensor("xnoT_d", [HID, OWN], BF16, kind=KIND).ap()
    qT_d = nc.dram_tensor("qT_d", [NQH, DH, OWN], BF16, kind=KIND).ap()
    kT_d = nc.dram_tensor("kT_d", [NKV, DH, SEQ], BF16, kind=KIND).ap()
    vS_d = nc.dram_tensor("vS_d", [NKV, SEQ, DH], BF16, kind=KIND).ap()
    oT_d = nc.dram_tensor("oT_d", [NVT, NQH, DH, 128], BF16, kind=KIND).ap()
    wo_bf_d = nc.dram_tensor("wo_bf_d", [HID, HID], BF16, kind="Internal").ap()
    wup_bf_d = nc.dram_tensor("wup_bf_d", [HID, HID], BF16, kind="Internal").ap()

    P = Prog()

    semnames = [
        "t", "v", "a", "p", "bar", "dcst",
        "dx0", "dx1", "dxo0", "dxo1", "wxnt", "wxno",
        "gwk", "gwv", "gwq", "gwo", "gwup", "gab", "gcwo", "gcwup", "gwo1", "gwup1",
        "dnb0", "dnb1",
        "wkt", "wvs0", "wvs1", "wqt",
        "dk0", "dk1", "dk2", "dv0", "dv1", "dv2", "dq0", "dq1", "dq2", "dq3", "dq4", "dq5", "dq6", "dq7",
        "wot0", "wot1",
        "dot0", "dot1", "dxl0", "dxl1", "dxs0", "dxs1",
        "wy",
    ]

    with contextlib.ExitStack() as top:
        for s in semnames:
            P.sem[s] = top.enter_context(nc.semaphore(s))
        block = top.enter_context(nc.Block())

        c_id = top.enter_context(nc.sbuf_tensor("c_id", [128, 128], BF16))
        c_idf = top.enter_context(nc.sbuf_tensor("c_idf", [128, 128], F32))
        c_rt = top.enter_context(nc.sbuf_tensor("c_rt", [128, 128], BF16))
        c_tri = top.enter_context(nc.sbuf_tensor("c_tri", [128, 256], F32))
        c_rec = top.enter_context(nc.sbuf_tensor("c_rec", [128, 256], F32))
        c_utp = top.enter_context(nc.sbuf_tensor("c_utp", [128, 128], F32))
        c_ones = top.enter_context(nc.sbuf_tensor("c_ones", [1, 128], F32))
        c_eb = top.enter_context(nc.sbuf_tensor("c_eb", [1, 4], F32))
        c_bb = top.enter_context(nc.sbuf_tensor("c_bb", [128, 4], F32))
        c_eps = top.enter_context(nc.sbuf_tensor("c_eps", [128, 1], F32))
        c_dmv = top.enter_context(nc.sbuf_tensor("c_dmv", [128, 1], F32))
        c_dma_ = top.enter_context(nc.sbuf_tensor("c_dma_", [128, 1], F32))
        c_dmg = top.enter_context(nc.sbuf_tensor("c_dmg", [128, 1], F32))

        barcnt = [0]

        def run():
            E = P.emitting
            barcnt[0] = 0

            def eng_sections(fns):
                engines = {}
                if E:
                    for nm in ["sync", "gpsimd", "tensor", "scalar", "vector"]:
                        f = fns.get(nm)
                        dec = getattr(block, nm)

                        def wrapper(eng, _f=f, _nm=nm):
                            engines[_nm] = eng
                            if _f:
                                _f(eng)
                        dec(wrapper)
                else:
                    for nm in ["sync", "gpsimd", "tensor", "scalar", "vector"]:
                        engines[nm] = None
                        f = fns.get(nm)
                        if f:
                            f(None)
                return engines

            def barrier():
                barcnt[0] += 1
                tgt = 5 * barcnt[0]
                pfx = {"sync": "d", "scalar": "w", "gpsimd": "g"}

                def mk(nm, op):
                    def f(e):
                        pf = pfx.get(nm)
                        if pf:
                            for s in P.sem:
                                if s.startswith(pf) and P.cnt.get(s):
                                    P.wait_min(e, s, P.cnt[s])
                        i = op(e) if P.emitting else None
                        P.inc(i, "bar", ("bar", barcnt[0], nm), 1)
                        P.wait_min(e, "bar", tgt)
                    return f
                eng_sections(dict(
                    sync=mk("sync", lambda e: e.nop()),
                    gpsimd=mk("gpsimd", lambda e: e.memset(c_dmg[:], 0.0)),
                    tensor=mk("tensor", lambda e: e.nop()),
                    scalar=mk("scalar", lambda e: e.memzero(c_dma_[:])),
                    vector=mk("vector", lambda e: e.memset(c_dmv[:], 0.0)),
                ))

            with contextlib.ExitStack() as ph:
                bbp = (ph.enter_context(nc.psum_tensor("bbp", [128, 512], F32)) if E else None)

                def p0_sync(e):
                    loads = [
                        (c_id, identb), (c_rt, rotT), (c_tri, trimask2),
                        (c_rec, recent2), (c_utp, ut2prev), (c_ones, ones1),
                        (c_eb, ebias), (c_idf, identf),
                    ]
                    for i2, (dst, src) in enumerate(loads):
                        inst = e.dma_start(out=dst[:], in_=src) if E else None
                        P.inc(inst, "dcst", ("dcst", i2), 16)

                def p0_gp(e):
                    for w_src, w_dst, tag in [(woT, wo_bf_d, "cwo"), (wupT, wup_bf_d, "cwup")]:
                        for ch in range(8):
                            i = e.dma_start(out=w_dst[ch * 512:(ch + 1) * 512, :],
                                            in_=w_src[ch * 512:(ch + 1) * 512, :],
                                            max_dma_last_dim=8192) if E else None
                            P.inc(i, "gcwo" if tag == "cwo" else "gcwup", (tag, ch), 16)

                def p0_tensor(e):
                    P.wait_min(e, "dcst", 16 * 8)
                    i = e.matmul(bbp[:, 0:4], c_ones[:], c_eb[:], start=True, stop=True) if E else None
                    P.inc(i, "t", ("bbmm",))

                def p0_vector(e):
                    P.wait(e, ("bbmm",))
                    i = e.tensor_copy(c_bb[:], bbp[:, 0:4]) if E else None
                    P.inc(i, "v", ("bbev",))
                    i = e.memset(c_eps[:], 1e-5) if E else None
                    P.inc(i, "v", ("epss",))

                eng_sections(dict(sync=p0_sync, gpsimd=p0_gp, tensor=p0_tensor, vector=p0_vector))
                barrier()

            if upto >= 1:
                with contextlib.ExitStack() as ph:
                    xb = [(ph.enter_context(nc.sbuf_tensor(f"xb{i}", [128, HID], F32)) if E else None) for i in range(2)]
                    stats = (ph.enter_context(nc.sbuf_tensor("stats", [128, 8, 6], F32)) if E else None)
                    mv = (ph.enter_context(nc.sbuf_tensor("mv", [128, 2], F32)) if E else None)
                    nmn = (ph.enter_context(nc.sbuf_tensor("nmn", [128, 1], F32)) if E else None)
                    lnv = (ph.enter_context(nc.sbuf_tensor("lnv", [128, 1], F32)) if E else None)
                    rstd = (ph.enter_context(nc.sbuf_tensor("rstd", [128, 1], F32)) if E else None)
                    xnb = (ph.enter_context(nc.sbuf_tensor("xnb", [128, HID], BF16)) if E else None)
                    col = (ph.enter_context(nc.sbuf_tensor("col", [128, HC, 4, 128], BF16)) if E else None)
                    tp = [(ph.enter_context(nc.psum_tensor(f"p1tp{i}", [128, 1024], BF16)) if E else None) for i in range(2)]

                    jobs = [
                        ("full", x_full, 32, xnT_d, "dx0", "dx1", "wxnt"),
                        ("own", x_own, 16, xnoT_d, "dxo0", "dxo1", "wxno"),
                    ]

                    def p1_sync(e):
                        for jn, src, nt, dst, s0, s1, ws in jobs:
                            for t in range(nt):
                                slot = t % 2
                                sname = s0 if slot == 0 else s1
                                P.buf_claim(e, ("xb", slot), ("p1xn", jn, t))
                                i = e.dma_start(out=xb[slot][:], in_=src[t * 128:(t + 1) * 128, :]) if E else None
                                P.inc(i, sname, ("p1x", jn, t), 16)

                    def p1_vector(e):
                        for jn, src, nt, dst, s0, s1, ws in jobs:
                            for t in range(nt):
                                slot = t % 2
                                sname = s0 if slot == 0 else s1
                                P.wait_min(e, sname, 16 * (t // 2 + 1))
                                for i8 in range(8):
                                    i = e.bn_stats(stats[:, i8, :], xb[slot][:, i8 * 512:(i8 + 1) * 512]) if E else None
                                P.inc(i, "v", ("p1bn", jn, t))
                                P.wait(e, ("p1bn", jn, t))
                                i = e.bn_aggr(mv[:], stats[:]) if E else None
                                P.inc(i, "v", ("p1ag", jn, t))
                                P.wait(e, ("p1ag", jn, t))
                                i = e.tensor_scalar(nmn[:], mv[:, 0:1], -1.0, None, OP.mult) if E else None
                                P.inc(i, "v", ("p1nm", jn, t))
                                P.wait(e, ("p1nm", jn, t))
                                P.wait(e, ("p1rs", jn, t))
                                P.buf_claim(e, "xnb", ("p1tp", jn, t, 3))
                                i = e.tensor_scalar(xnb[:], xb[slot][:], nmn[:], rstd[:], OP.add, OP.mult) if E else None
                                P.inc(i, "v", ("p1xn", jn, t))

                    def p1_scalar(e):
                        for jn, src, nt, dst, s0, s1, ws in jobs:
                            for t in range(nt):
                                P.wait(e, ("p1ag", jn, t))
                                i = e.activation(lnv[:], mv[:, 1:2], AF.Ln, bias=c_eps[:], scale=1.0) if E else None
                                P.inc(i, "a", ("p1ln", jn, t))
                                P.wait(e, ("p1ln", jn, t))
                                i = e.activation(rstd[:], lnv[:], AF.Exp, scale=-0.5) if E else None
                                P.inc(i, "a", ("p1rs", jn, t))
                                for b in range(4):
                                    if t % 4 == 0 and b == 0:
                                        if jn == "own" and t == 0:
                                            P.wait_min(e, "wxnt", 16 * 8)
                                        g = t // 4
                                        P.wait_min(e, ws, 16 * g)
                                    P.wait(e, ("p1tp", jn, t, b))
                                    i = e.copy(col[:, b * 8:(b + 1) * 8, t % 4, :],
                                               tp[b % 2][:].rearrange("p (c n) -> p c n", n=128)) if E else None
                                    P.inc(i, "a", ("p1ev", jn, t, b))
                                if t % 4 == 3:
                                    g = t // 4
                                    P.wait(e, ("p1ev", jn, t, 3))
                                    for _tt in range(t - 3, t):
                                        P.wait(e, ("p1ev", jn, _tt, 3))
                                    if E:
                                        dd = dst.rearrange("(c p) s -> p c s", p=128)
                                        i = e.dma_start(out=dd[:, :, g * 512:(g + 1) * 512],
                                                        in_=col[:].rearrange("p c f n -> p c (f n)"))
                                    else:
                                        i = None
                                    P.inc(i, ws, ("p1st", jn, g), 16)

                    def p1_tensor(e):
                        for jn, src, nt, dst, s0, s1, ws in jobs:
                            for t in range(nt):
                                P.wait(e, ("p1xn", jn, t))
                                for b in range(4):
                                    P.psum_claim(e, ("p1", b % 2), [("p1ev", jn, t, b)])
                                    for i8 in range(8):
                                        hcix = b * 8 + i8
                                        i = e.transpose(tp[b % 2][:, i8 * 128:(i8 + 1) * 128],
                                                        xnb[:, hcix * 128:(hcix + 1) * 128], c_id[:]) if E else None
                                    P.inc(i, "t", ("p1tp", jn, t, b))

                    eng_sections(dict(sync=p1_sync, vector=p1_vector,
                                        scalar=p1_scalar, tensor=p1_tensor))
                    barrier()

            if upto >= 2:
                with contextlib.ExitStack() as ph:
                    wkb = (ph.enter_context(nc.sbuf_tensor("wkb", [128, HC, NKV * DH], BF16)) if E else None)
                    xnbk = (ph.enter_context(nc.sbuf_tensor("xnbk", [128, HC, 512], BF16)) if E else None)
                    ksb = (ph.enter_context(nc.sbuf_tensor("ksb", [128, 512], BF16)) if E else None)
                    kt1 = (ph.enter_context(nc.sbuf_tensor("kt1", [128, 512], F32)) if E else None)
                    kt2 = (ph.enter_context(nc.sbuf_tensor("kt2", [128, 512], F32)) if E else None)
                    krsb = (ph.enter_context(nc.sbuf_tensor("krsb", [128, 512], BF16)) if E else None)
                    kout = (ph.enter_context(nc.sbuf_tensor("kout", [128, 512], BF16)) if E else None)
                    cskl = (ph.enter_context(nc.sbuf_tensor("cskl", [128, SEQ], F32)) if E else None)
                    sskl = (ph.enter_context(nc.sbuf_tensor("sskl", [128, SEQ], F32)) if E else None)
                    kpr = [(ph.enter_context(nc.psum_tensor(f"kpr{i}", [128, 512], F32)) if E else None) for i in range(2)]
                    rpr = [(ph.enter_context(nc.psum_tensor(f"rpr{i}", [128, 512], F32)) if E else None) for i in range(2)]

                    def p2k_gp(e):
                        i = e.dma_start(out=wkb[:], in_=wkT.rearrange("(c p) n -> p c n", p=128)) if E else None
                        P.inc(i, "gwk", ("gwk",), 16)
                        P.wait_min(e, "dcst", 16 * 10)
                        for tb in range(8):
                            for kh in range(NKV):
                                n = tb * NKV + kh
                                P.wait(e, ("p2kev", tb, kh))
                                i = e.tensor_tensor(kt1[:], ksb[:], cskl[:, tb * 512:(tb + 1) * 512], OP.mult) if E else None
                                P.inc(i, "p", ("p2kt1", tb, kh))
                                P.wait(e, ("p2krev", tb, kh))
                                i = e.tensor_tensor(kt2[:], krsb[:], sskl[:, tb * 512:(tb + 1) * 512], OP.mult) if E else None
                                P.inc(i, "p", ("p2kt2", tb, kh))
                                P.wait(e, ("p2kt2", tb, kh))
                                P.wait_min(e, "wkt", 16 * n)
                                i = e.tensor_tensor(kout[:], kt1[:], kt2[:], OP.add) if E else None
                                P.inc(i, "p", ("p2kr3", tb, kh))

                    def p2k_sync(e):
                        i = e.dma_start(out=cskl[:], in_=cosk) if E else None
                        P.inc(i, "dcst", ("csk",), 16)
                        i = e.dma_start(out=sskl[:], in_=sink) if E else None
                        P.inc(i, "dcst", ("ssk",), 16)
                        for tb in range(8):
                            if tb >= 1:
                                P.wait(e, ("p2kmm", tb - 1, NKV - 1))
                            if E:
                                dd = xnT_d.rearrange("(c p) s -> p c s", p=128)
                                i = e.dma_start(out=xnbk[:], in_=dd[:, :, tb * 512:(tb + 1) * 512])
                            else:
                                i = None
                            P.inc(i, "dnb0", ("p2kx", tb), 16)

                    def p2k_tensor(e):
                        P.wait_min(e, "gwk", 16)
                        for tb in range(8):
                            P.wait_min(e, "dnb0", 16 * (tb + 1))
                            for kh in range(NKV):
                                s = kh % 2
                                P.psum_claim(e, ("p2kk", s), [("p2kev", tb, kh)])
                                for hc in range(HC):
                                    i = e.matmul(kpr[s][:], wkb[:, hc, kh * 128:(kh + 1) * 128],
                                                 xnbk[:, hc, :], start=(hc == 0), stop=(hc == HC - 1)) if E else None
                                P.inc(i, "t", ("p2kmm", tb, kh))
                                P.wait(e, ("p2kev", tb, kh))
                                P.psum_claim(e, ("p2kr", s), [("p2krev", tb, kh)])
                                i = e.matmul(rpr[s][:], c_rt[:], ksb[:], start=True, stop=True) if E else None
                                P.inc(i, "t", ("p2krk", tb, kh))

                    def p2k_scalar(e):
                        for tb in range(8):
                            for kh in range(NKV):
                                n = tb * NKV + kh
                                s = kh % 2
                                P.wait(e, ("p2kmm", tb, kh))
                                if n >= 1:
                                    ptb, pkh = divmod(n - 1, NKV)
                                    P.wait(e, ("p2kt1", ptb, pkh))
                                    P.wait(e, ("p2krk", ptb, pkh))
                                i = e.copy(ksb[:], kpr[s][:]) if E else None
                                P.inc(i, "a", ("p2kev", tb, kh))
                                P.wait(e, ("p2krk", tb, kh))
                                if n >= 1:
                                    ptb, pkh = divmod(n - 1, NKV)
                                    P.wait(e, ("p2kt2", ptb, pkh))
                                i = e.copy(krsb[:], rpr[s][:]) if E else None
                                P.inc(i, "a", ("p2krev", tb, kh))
                                P.wait(e, ("p2kr3", tb, kh))
                                i = e.dma_start(out=kT_d[kh, :, tb * 512:(tb + 1) * 512], in_=kout[:]) if E else None
                                P.inc(i, "wkt", ("p2kst", tb, kh), 16)

                    eng_sections(dict(gpsimd=p2k_gp, sync=p2k_sync,
                                        tensor=p2k_tensor, scalar=p2k_scalar))
                    barrier()

            if upto >= 3:
                with contextlib.ExitStack() as ph:
                    wvb = (ph.enter_context(nc.sbuf_tensor("wvb", [128, HC, NKV * DH], BF16)) if E else None)
                    xnbv = (ph.enter_context(nc.sbuf_tensor("xnbv", [128, HC, 512], BF16)) if E else None)
                    vsb = [(ph.enter_context(nc.sbuf_tensor(f"vsb{i}", [128, 512], BF16)) if E else None) for i in range(2)]
                    vpr = [(ph.enter_context(nc.psum_tensor(f"vpr{i}", [128, 512], F32)) if E else None) for i in range(2)]

                    def p2v_gp(e):
                        i = e.dma_start(out=wvb[:], in_=wvT.rearrange("(c p) n -> p c n", p=128)) if E else None
                        P.inc(i, "gwv", ("gwv",), 16)

                    def p2v_sync(e):
                        for tb in range(8):
                            if tb >= 1:
                                P.wait(e, ("p2vmm", tb - 1, 3, 1))
                            if E:
                                dd = xnT_d.rearrange("(c p) s -> p c s", p=128)
                                i = e.dma_start(out=xnbv[:], in_=dd[:, :, tb * 512:(tb + 1) * 512])
                            else:
                                i = None
                            P.inc(i, "dnb1", ("p2vx", tb), 16)

                    def p2v_tensor(e):
                        P.wait_min(e, "gwv", 16)
                        for tb in range(8):
                            P.wait_min(e, "dnb1", 16 * (tb + 1))
                            for tt in range(4):
                                for vbk in range(2):
                                    n = tb * 8 + tt * 2 + vbk
                                    s = n % 2
                                    P.psum_claim(e, ("p2v", s), [("p2vev", tb, tt, vbk)])
                                    for hc in range(HC):
                                        i = e.matmul(vpr[s][:], xnbv[:, hc, tt * 128:(tt + 1) * 128],
                                                     wvb[:, hc, vbk * 512:(vbk + 1) * 512],
                                                     start=(hc == 0), stop=(hc == HC - 1)) if E else None
                                    P.inc(i, "t", ("p2vmm", tb, tt, vbk))

                    def p2v_scalar(e):
                        for tb in range(8):
                            for tt in range(4):
                                for vbk in range(2):
                                    n = tb * 8 + tt * 2 + vbk
                                    s = n % 2
                                    wsem = "wvs0" if s == 0 else "wvs1"
                                    P.wait(e, ("p2vmm", tb, tt, vbk))
                                    P.wait_min(e, wsem, 16 * 4 * (n // 2))
                                    i = e.copy(vsb[s][:], vpr[s][:]) if E else None
                                    P.inc(i, "a", ("p2vev", tb, tt, vbk))
                                    P.wait(e, ("p2vev", tb, tt, vbk))
                                    trow = tb * 512 + tt * 128
                                    for hh in range(4):
                                        kh = vbk * 4 + hh
                                        i = e.dma_start(out=vS_d[kh, trow:trow + 128, :],
                                                        in_=vsb[s][:, hh * 128:(hh + 1) * 128]) if E else None
                                        P.inc(i, wsem, ("p2vst", tb, tt, vbk, hh), 16)

                    eng_sections(dict(gpsimd=p2v_gp, sync=p2v_sync,
                                        tensor=p2v_tensor, scalar=p2v_scalar))
                    barrier()

            if upto >= 4:
                with contextlib.ExitStack() as ph:
                    wqb = (ph.enter_context(nc.sbuf_tensor("wqb", [128, HC, 8 * DH], BF16)) if E else None)
                    xnbq = (ph.enter_context(nc.sbuf_tensor("xnbq", [128, HC, 512], BF16)) if E else None)
                    qsb = (ph.enter_context(nc.sbuf_tensor("qsb", [128, 512], BF16)) if E else None)
                    qt1 = (ph.enter_context(nc.sbuf_tensor("qt1", [128, 512], F32)) if E else None)
                    qt2 = (ph.enter_context(nc.sbuf_tensor("qt2", [128, 512], F32)) if E else None)
                    qrsb = (ph.enter_context(nc.sbuf_tensor("qrsb", [128, 512], BF16)) if E else None)
                    qout = (ph.enter_context(nc.sbuf_tensor("qout", [128, 512], BF16)) if E else None)
                    cql = (ph.enter_context(nc.sbuf_tensor("cql", [128, OWN], F32)) if E else None)
                    sql = (ph.enter_context(nc.sbuf_tensor("sql", [128, OWN], F32)) if E else None)
                    qpr = [(ph.enter_context(nc.psum_tensor(f"qpr{i}", [128, 512], F32)) if E else None) for i in range(2)]
                    qrp = [(ph.enter_context(nc.psum_tensor(f"qrp{i}", [128, 512], F32)) if E else None) for i in range(2)]

                    def p3_gp(e):
                        for qq in range(4):
                            if qq >= 1:
                                P.wait(e, ("p3mm", qq - 1, 3, 7))
                            if E:
                                i = e.dma_start(out=wqb[:],
                                                in_=wqT.rearrange("(c p) n -> p c n", p=128)[:, :, qq * 1024:(qq + 1) * 1024])
                            else:
                                i = None
                            P.inc(i, "gwq", ("gwq", qq), 16)
                            for sp in range(4):
                                for q8 in range(8):
                                    P.wait(e, ("p3ev", qq, sp, q8))
                                    i = e.tensor_tensor(qt1[:], qsb[:], cql[:, sp * 512:(sp + 1) * 512], OP.mult) if E else None
                                    P.inc(i, "p", ("p3t1", qq, sp, q8))
                                    P.wait(e, ("p3rev", qq, sp, q8))
                                    i = e.tensor_tensor(qt2[:], qrsb[:], sql[:, sp * 512:(sp + 1) * 512], OP.mult) if E else None
                                    P.inc(i, "p", ("p3t2", qq, sp, q8))
                                    P.wait(e, ("p3t2", qq, sp, q8))
                                    idx = (qq * 4 + sp) * 8 + q8
                                    P.wait_min(e, "wqt", 16 * idx)
                                    i = e.tensor_tensor(qout[:], qt1[:], qt2[:], OP.add) if E else None
                                    P.inc(i, "p", ("p3r3", qq, sp, q8))

                    def p3_sync(e):
                        i = e.dma_start(out=cql[:], in_=cosq) if E else None
                        P.inc(i, "dcst", ("csq",), 16)
                        i = e.dma_start(out=sql[:], in_=sinq) if E else None
                        P.inc(i, "dcst", ("ssq",), 16)
                        n = 0
                        for qq in range(4):
                            for sp in range(4):
                                if n >= 1:
                                    pq, psp = divmod(n - 1, 4)
                                    P.wait(e, ("p3mm", pq, psp, 7))
                                if E:
                                    dd = xnoT_d.rearrange("(c p) s -> p c s", p=128)
                                    i = e.dma_start(out=xnbq[:], in_=dd[:, :, sp * 512:(sp + 1) * 512])
                                else:
                                    i = None
                                P.inc(i, "dnb0", ("p3x", qq, sp), 16)
                                n += 1

                    def p3_tensor(e):
                        P.wait_min(e, "dcst", 16 * 12)
                        n = 0
                        for qq in range(4):
                            P.wait_min(e, "gwq", 16 * (qq + 1))
                            for sp in range(4):
                                n += 1
                                P.wait_min(e, "dnb0", 16 * (8 + n))
                                for q8 in range(8):
                                    s = q8 % 2
                                    P.psum_claim(e, ("p3q", s), [("p3ev", qq, sp, q8)])
                                    for hc in range(HC):
                                        i = e.matmul(qpr[s][:], wqb[:, hc, q8 * 128:(q8 + 1) * 128],
                                                     xnbq[:, hc, :], start=(hc == 0), stop=(hc == HC - 1)) if E else None
                                    P.inc(i, "t", ("p3mm", qq, sp, q8))
                                    P.wait(e, ("p3ev", qq, sp, q8))
                                    P.psum_claim(e, ("p3r", s), [("p3rev", qq, sp, q8)])
                                    i = e.matmul(qrp[s][:], c_rt[:], qsb[:], start=True, stop=True) if E else None
                                    P.inc(i, "t", ("p3rk", qq, sp, q8))

                    def p3_scalar(e):
                        for qq in range(4):
                            for sp in range(4):
                                for q8 in range(8):
                                    n = (qq * 4 + sp) * 8 + q8
                                    s = q8 % 2
                                    P.wait(e, ("p3mm", qq, sp, q8))
                                    if n >= 1:
                                        pq, prem = divmod(n - 1, 32)
                                        psp, pq8 = divmod(prem, 8)
                                        P.wait(e, ("p3t1", pq, psp, pq8))
                                        P.wait(e, ("p3rk", pq, psp, pq8))
                                    i = e.copy(qsb[:], qpr[s][:]) if E else None
                                    P.inc(i, "a", ("p3ev", qq, sp, q8))
                                    P.wait(e, ("p3rk", qq, sp, q8))
                                    if n >= 1:
                                        P.wait(e, ("p3t2", pq, psp, pq8))
                                    i = e.copy(qrsb[:], qrp[s][:]) if E else None
                                    P.inc(i, "a", ("p3rev", qq, sp, q8))
                                    P.wait(e, ("p3r3", qq, sp, q8))
                                    qh = qq * 8 + q8
                                    i = e.dma_start(out=qT_d[qh, :, sp * 512:(sp + 1) * 512], in_=qout[:]) if E else None
                                    P.inc(i, "wqt", ("p3st", qq, sp, q8), 16)

                    eng_sections(dict(gpsimd=p3_gp, sync=p3_sync,
                                        tensor=p3_tensor, scalar=p3_scalar))
                    barrier()

            if upto >= 5:
                with contextlib.ExitStack() as ph:
                    kbuf = [(ph.enter_context(nc.sbuf_tensor(f"kbuf{i}", [128, SEQ], BF16)) if E else None) for i in range(3)]
                    vbuf = [(ph.enter_context(nc.sbuf_tensor(f"vbuf{i}", [128, 32, 128], BF16)) if E else None) for i in range(3)]
                    qbuf = [(ph.enter_context(nc.sbuf_tensor(f"qbuf{i}", [128, 512], BF16)) if E else None) for i in range(8)]
                    Ssb = [[(ph.enter_context(nc.sbuf_tensor(f"Ssb{i}_{s}", [128, SEQ], F32)) if E else None) for s in range(2)] for i in range(2)]
                    praw = [(ph.enter_context(nc.sbuf_tensor(f"praw{i}", [128, SEQ], BF16)) if E else None) for i in range(2)]
                    Pm = [(ph.enter_context(nc.sbuf_tensor(f"Pm{i}", [128, SEQ], BF16)) if E else None) for i in range(2)]
                    cmax = [(ph.enter_context(nc.sbuf_tensor(f"cmax{i}", [128, 2, 256], F32)) if E else None) for i in range(2)]
                    m8 = [(ph.enter_context(nc.sbuf_tensor(f"m8_{i}", [128, 2, 8], F32)) if E else None) for i in range(2)]
                    nm1 = [(ph.enter_context(nc.sbuf_tensor(f"nm1_{i}", [128, 2], F32)) if E else None) for i in range(2)]
                    K16 = [(ph.enter_context(nc.sbuf_tensor(f"K16_{i}", [128, 2, 256], F32)) if E else None) for i in range(2)]
                    KbD = [(ph.enter_context(nc.sbuf_tensor(f"KbD{i}", [128, 2, 256], F32)) if E else None) for i in range(2)]
                    KbP = [(ph.enter_context(nc.sbuf_tensor(f"KbP{i}", [128, 2, 128], F32)) if E else None) for i in range(2)]
                    racc = [(ph.enter_context(nc.sbuf_tensor(f"racc{i}", [128, 2, 3], F32)) if E else None) for i in range(2)]
                    rr = [(ph.enter_context(nc.sbuf_tensor(f"rr{i}", [128, 2], F32)) if E else None) for i in range(2)]
                    rcp = [(ph.enter_context(nc.sbuf_tensor(f"rcp{i}", [128, 2], F32)) if E else None) for i in range(2)]
                    PTs = [(ph.enter_context(nc.sbuf_tensor(f"PTs{i}", [128, 1024], BF16)) if E else None) for i in range(2)]
                    ob = [(ph.enter_context(nc.sbuf_tensor(f"ob{i}", [128, 128], BF16)) if E else None) for i in range(2)]
                    Rbs = [(ph.enter_context(nc.sbuf_tensor(f"Rbs{i}", [128, 2, 128], F32)) if E else None) for i in range(2)]
                    rrow = [(ph.enter_context(nc.sbuf_tensor(f"rrow{i}", [1, 256], F32)) if E else None) for i in range(2)]
                    SpsA = (ph.enter_context(nc.psum_tensor("SpsA", [128, 2048], F32)) if E else None)
                    PTp = [(ph.enter_context(nc.psum_tensor(f"PTp{i}", [128, 1024], BF16)) if E else None) for i in range(2)]
                    AVp = (ph.enter_context(nc.psum_tensor("AVp", [128, 512], F32)) if E else None)
                    RBp = (ph.enter_context(nc.psum_tensor("RBp", [128, 512], F32)) if E else None)

                    PAIRS = [(g, lhp, qt) for g in range(NGRP) for lhp in range(16) for qt in range(4)]
                    NP = len(PAIRS)

                    def subs_of(p):
                        g, lhp, qt = PAIRS[p]
                        return g, qt, (2 * lhp, 2 * lhp + 1)

                    kvmap = {}
                    qmap = {}
                    _kvi = 0
                    for g in range(NGRP):
                        for lhp in range(16):
                            if lhp % 2 == 0:
                                _kvi += 1
                            for s_ in range(2):
                                lh = 2 * lhp + s_
                                kvmap[(g, lh)] = _kvi - 1
                                qmap[(g, lh)] = 2 * (g * 16 + lhp) + s_

                    ptpar = {}
                    _c = 0
                    for p in range(NP):
                        g, qt, lhs2 = subs_of(p)
                        j, ext, n512, nkb, nbat = geom(g, qt)
                        for bat in range(nbat):
                            ptpar[(p, bat)] = _c % 2
                            _c += 1

                    def sev_groups(j, ext, n512):
                        tot = ext - 256
                        out = []
                        c = 0
                        while c < tot:
                            w = min(1024, tot - c)
                            out.append((c, w))
                            c += w
                        return out

                    def tile_readers(g, lh, qt, sub):
                        j, ext, n512, nkb, nbat = geom(g, qt)
                        grps = sev_groups(j, ext, n512)
                        out = []
                        for b in range(n512):
                            rd = []
                            for gi, (c, w) in enumerate(grps):
                                if c < (b + 1) * 512 and c + w > b * 512:
                                    rd.append(("sev", g, lh, qt, gi))
                            if b == n512 - 1:
                                rd.append(("dstt", g, lh, qt))
                            out.append(list(dict.fromkeys(rd)))
                        return out

                    def prev_pair(p):
                        return None if p == 0 else PAIRS[p - 1]

                    def pair_idx(g, lhp, qt):
                        return (g * 16 + lhp) * 4 + qt

                    def p4_sync(e):
                        P.wait_min(e, "wkt", 16 * 64)
                        P.wait_min(e, "wvs0", 16 * 128)
                        P.wait_min(e, "wvs1", 16 * 128)
                        P.wait_min(e, "wqt", 16 * 128)
                        LHS = [(g, lhp) for g in range(NGRP) for lhp in range(16)]

                        def emit_loads(li):
                            g, lhp = LHS[li]
                            ext_g = 1024 * (g + 1)
                            if lhp % 2 == 0:
                                kv = lhp // 2
                                kvi = kvmap[(g, 2 * lhp)]
                                slot = kvi % 3
                                if kvi >= 3:
                                    og, okv = divmod(kvi - 3, 8)
                                    olh = okv * 4 + 3
                                    oj, oext, on512, onkb, onbat = geom(og, 3)
                                    P.wait(e, ("qk", og, olh, 3, on512 - 1))
                                    P.wait(e, ("avb2", pair_idx(og, okv * 2 + 1, 3), onbat - 1))
                                i = e.dma_start(out=kbuf[slot][:, 0:ext_g], in_=kT_d[kv, :, 0:ext_g]) if E else None
                                P.inc(i, ["dk0", "dk1", "dk2"][slot], ("p4k", kvi), 16)
                                if E:
                                    i = e.dma_start(out=vbuf[slot][:, 0:8 * (g + 1), :],
                                                    in_=vS_d[kv].rearrange("(n p) d -> p n d", p=128)[:, 0:8 * (g + 1), :])
                                else:
                                    i = None
                                P.inc(i, ["dv0", "dv1", "dv2"][slot], ("p4v", kvi), 16)
                            for s_ in range(2):
                                lh = 2 * lhp + s_
                                qi = qmap[(g, lh)]
                                qslot = qi % 8
                                if qi >= 8:
                                    oli = li - 4
                                    og, olhp = LHS[oli]
                                    olh = 2 * olhp + s_
                                    oj, oext, on512, onkb, onbat = geom(og, 3)
                                    P.wait(e, ("qk", og, olh, 3, on512 - 1))
                                i = e.dma_start(out=qbuf[qslot][:], in_=qT_d[lh, :, g * 512:(g + 1) * 512]) if E else None
                                P.inc(i, ["dq0", "dq1", "dq2", "dq3", "dq4", "dq5", "dq6", "dq7"][qslot], ("p4q", qi), 16)

                        LOOK = 2
                        for li in range(LOOK + 1):
                            emit_loads(li)
                        oti = 0
                        for p in range(NP):
                            g, qt, lhs2 = PAIRS[p][0], PAIRS[p][2], None
                            g, lhp, qt = PAIRS[p]
                            if qt == 0:
                                li = g * 16 + lhp
                                if li + LOOK + 1 < len(LHS):
                                    emit_loads(li + LOOK + 1)
                            for s_ in range(2):
                                lh = 2 * lhp + s_
                                P.wait(e, ("oev", g, lh, qt))
                                wsem = "wot0" if oti % 2 == 0 else "wot1"
                                vt = 4 * g + qt
                                i2 = e.dma_start(out=oT_d[vt, lh, :, :], in_=ob[oti % 2][:]) if E else None
                                P.inc(i2, wsem, ("otst", g, lh, qt), 16)
                                oti += 1

                    def p4_tensor(e):
                        def emit_qk(p):
                            g, qt, lhs2 = subs_of(p)
                            j, ext, n512, nkb, nbat = geom(g, qt)
                            if qt == 0:
                                P.wait(e, ("p4k", kvmap[(g, lhs2[0])]))
                                P.wait(e, ("p4v", kvmap[(g, lhs2[0])]))
                                P.wait(e, ("p4q", qmap[(g, lhs2[0])]))
                                P.wait(e, ("p4q", qmap[(g, lhs2[1])]))
                            for s_, lh in enumerate(lhs2):
                                rds = tile_readers(g, lh, qt, s_)
                                for b in range(n512):
                                    bank = 2 * s_ + (b % 2)
                                    P.psum_claim(e, ("S", bank), rds[b])
                                    N = blkN(j, b, n512, ext)
                                    col = bank * 512
                                    i2 = e.matmul(SpsA[:, col:col + N],
                                                  qbuf[qmap[(g, lh)] % 8][:, qt * 128:(qt + 1) * 128],
                                                  kbuf[kvmap[(g, lh)] % 3][:, b * 512:b * 512 + N],
                                                  start=True, stop=True) if E else None
                                    P.inc(i2, "t", ("qk", g, lh, qt, b))

                        def emit_ptav(p):
                            g, qt, lhs2 = subs_of(p)
                            j, ext, n512, nkb, nbat = geom(g, qt)
                            par = p % 2

                            def do_rbt(pq):
                                qg, qqt, qlhs = subs_of(pq)
                                P.wait(e, ("rcp", qg, qlhs[1], qqt))
                                P.psum_claim(e, ("RB",), [("rrowev", qg, qlhs[1], qqt), ("rbev", qg, qlhs[1], qqt)])
                                for s_ in range(2):
                                    i2 = e.transpose(RBp[0:1, 256 + s_ * 128:256 + (s_ + 1) * 128],
                                                     rcp[pq % 2][:, s_:s_ + 1], c_idf[:]) if E else None
                                P.inc(i2, "t", ("rbt", qg, qlhs[1], qqt))

                            def do_rbmm(pq):
                                qg, qqt, qlhs = subs_of(pq)
                                P.wait(e, ("rrowev", qg, qlhs[1], qqt))
                                for s_ in range(2):
                                    i2 = e.matmul(RBp[:, s_ * 128:(s_ + 1) * 128], c_ones[:],
                                                  rrow[pq % 2][0:1, s_ * 128:(s_ + 1) * 128],
                                                  start=True, stop=True) if E else None
                                P.inc(i2, "t", ("rbmm", qg, qlhs[1], qqt))

                            def do_pt(bat):
                                lo = bat * 4
                                hi = min(nkb, lo + 4)
                                pslot = ptpar[(p, bat)]
                                P.psum_claim(e, ("PT", pslot), [("ptev2", p, bat)])
                                for k in range(lo, hi):
                                    for s_ in range(2):
                                        i2 = e.transpose(PTp[pslot][:, ((k - lo) * 2 + s_) * 128:((k - lo) * 2 + s_ + 1) * 128],
                                                         Pm[s_][:, k * 128:(k + 1) * 128], c_id[:]) if E else None
                                P.inc(i2, "t", ("pt2", p, bat))

                            def do_av(bat):
                                if bat == 0:
                                    P.psum_claim(e, ("AV",), [("oev", g, lhs2[0], qt), ("oev", g, lhs2[1], qt)])
                                lo = bat * 4
                                hi = min(nkb, lo + 4)
                                pslot = ptpar[(p, bat)]
                                P.wait(e, ("ptev2", p, bat))
                                for k in range(lo, hi):
                                    i2 = e.matmul(AVp[:, 0:256],
                                                  vbuf[kvmap[(g, lhs2[0])] % 3][:, k, :],
                                                  PTs[pslot][:, (k - lo) * 256:(k - lo + 1) * 256],
                                                  start=(k == 0 and bat == 0), stop=(k == nkb - 1)) if E else None
                                P.inc(i2, "t", ("avb2", p, bat))

                            P.wait(e, ("pmask", g, lhs2[1], qt))
                            for bat in range(nbat):
                                do_pt(bat)
                                if bat >= 1:
                                    do_av(bat - 1)
                            do_rbt(p)
                            do_rbmm(p)
                            do_av(nbat - 1)

                        emit_qk(0)
                        for p in range(NP):
                            if p + 1 < NP:
                                emit_qk(p + 1)
                            emit_ptav(p)

                    def p4_scalar(e):
                        def emit_sev(p):
                            g, qt, lhs2 = subs_of(p)
                            j, ext, n512, nkb, nbat = geom(g, qt)
                            par = p % 2
                            pp2 = PAIRS[p - 2] if p >= 2 else None
                            grps = sev_groups(j, ext, n512)
                            first = True
                            for s_, lh in enumerate(lhs2):
                                for gi, (c, w) in enumerate(grps):
                                    blo = c // 512
                                    bhi = (c + w - 1) // 512
                                    for b in range(blo, bhi + 1):
                                        P.wait(e, ("qk", g, lh, qt, b))
                                    if first and pp2 is not None:
                                        pg, plhp, pqt = pp2
                                        P.wait(e, ("cmax", pg, 2 * plhp + 1, pqt))
                                        P.wait(e, ("exp", pg, 2 * plhp + 1, pqt))
                                    first = False
                                    i2 = e.copy(Ssb[par][s_][:, c:c + w],
                                                SpsA[:, s_ * 1024 + (c % 1024):s_ * 1024 + (c % 1024) + w]) if E else None
                                    P.inc(i2, "a", ("sev", g, lh, qt, gi))

                        def emit_exp(p):
                            g, qt, lhs2 = subs_of(p)
                            j, ext, n512, nkb, nbat = geom(g, qt)
                            par = p % 2
                            pp = prev_pair(p)
                            P.wait(e, ("nm1", g, lhs2[1], qt))
                            if pp is not None:
                                pg, plhp, pqt = pp
                                P.wait(e, ("pmask", pg, 2 * plhp + 1, pqt))
                            for s_, lh in enumerate(lhs2):
                                i2 = e.activation(praw[s_][:, 0:ext], Ssb[par][s_][:, 0:ext], AF.Exp,
                                                  bias=nm1[par][:, s_:s_ + 1], scale=1.0) if E else None
                                P.inc(i2, "a", ("exp", g, lh, qt))

                        def emit_aux(p):
                            g, qt, lhs2 = subs_of(p)
                            j, ext, n512, nkb, nbat = geom(g, qt)
                            par = p % 2
                            for bat in range(nbat):
                                lo = bat * 4
                                hi = min(nkb, lo + 4)
                                pslot = ptpar[(p, bat)]
                                P.wait(e, ("pt2", p, bat))
                                P.buf_claim(e, ("PTs", pslot), ("avb2", p, bat))
                                i2 = e.copy(PTs[pslot][:, 0:(hi - lo) * 256],
                                            PTp[pslot][:, 0:(hi - lo) * 256]) if E else None
                                P.inc(i2, "a", ("ptev2", p, bat))

                        emit_sev(0)
                        for p in range(NP):
                            emit_exp(p)
                            if p + 1 < NP:
                                emit_sev(p + 1)
                            emit_aux(p)

                    def p4_vector(e):
                        def emit_s3(p):
                            g, qt, lhs2 = subs_of(p)
                            j, ext, n512, nkb, nbat = geom(g, qt)
                            par = p % 2
                            pp = prev_pair(p)
                            b = n512 - 1
                            N = blkN(j, b, n512, ext)
                            off = N - 256
                            ngr = len(sev_groups(j, ext, n512))
                            pe2 = PAIRS[p - 2] if p >= 2 else None
                            for s_, lh in enumerate(lhs2):
                                bank = 2 * s_ + (b % 2)
                                P.wait(e, ("qk", g, lh, qt, b))
                                if s_ == 0 and pe2 is not None:
                                    pg, plhp, pqt = pe2
                                    P.wait(e, ("exp", pg, 2 * plhp + 1, pqt))
                                if j % 2 == 1 and ngr >= 1:
                                    P.wait(e, ("sev", g, lh, qt, ngr - 1))
                                i2 = e.scalar_tensor_tensor(Ssb[par][s_][:, ext - 256:ext],
                                                            SpsA[:, bank * 512 + off:bank * 512 + off + 256],
                                                            1.0, c_tri[:], OP.mult, OP.add) if E else None
                                P.inc(i2, "v", ("dstt", g, lh, qt))
                            for s_, lh in enumerate(lhs2):
                                if ngr >= 1:
                                    P.wait(e, ("sev", g, lh, qt, ngr - 1))
                                P.wait(e, ("dstt", g, lh, qt))
                                i2 = e.tensor_reduce(cmax[par][:, s_, 0:16 * (j + 1)],
                                                     Ssb[par][s_][:, 0:ext].rearrange("p (c s) -> p c s", s=16),
                                                     AX.X, OP.max) if E else None
                                if j < 15:
                                    i2 = e.memset(cmax[par][:, s_, 16 * (j + 1):256], NEG) if E else None
                                P.inc(i2, "v", ("cmax", g, lh, qt))
                            P.wait(e, ("cmax", g, lhs2[1], qt))
                            for s_, lh in enumerate(lhs2):
                                i2 = e.max(m8[par][:, s_, :], cmax[par][:, s_, :]) if E else None
                                P.inc(i2, "v", ("max8", g, lh, qt))
                            P.wait(e, ("max8", g, lhs2[1], qt))
                            for s_, lh in enumerate(lhs2):
                                i2 = e.memset(K16[par][:, s_, 0:8], 1.0) if E else None
                                i2 = e.tensor_scalar(K16[par][:, s_, 8:256], cmax[par][:, s_, 8:256],
                                                     m8[par][:, s_, 7:8], None, OP.is_ge) if E else None
                                P.inc(i2, "v", ("k16", g, lh, qt))
                                i2 = e.tensor_scalar(nm1[par][:, s_:s_ + 1], m8[par][:, s_, 0:1],
                                                     -1.0, None, OP.mult) if E else None
                            P.inc(i2, "v", ("nm1", g, lhs2[1], qt))

                        def emit_s5(p):
                            g, qt, lhs2 = subs_of(p)
                            j, ext, n512, nkb, nbat = geom(g, qt)
                            par = p % 2
                            P.wait(e, ("k16", g, lhs2[1], qt))
                            for s_, lh in enumerate(lhs2):
                                i2 = e.tensor_tensor(KbD[par][:, s_, :], bc16(K16[par][:, s_, 16 * j:16 * j + 16]),
                                                     c_rec[:].rearrange("p (c s) -> p c s", s=16), OP.max) if E else None
                                P.inc(i2, "v", ("kbd", g, lh, qt))
                                if j >= 1:
                                    i2 = e.tensor_tensor(KbP[par][:, s_, :], bc16(K16[par][:, s_, 16 * j - 8:16 * j]),
                                                         c_utp[:].rearrange("p (c s) -> p c s", s=16), OP.max) if E else None
                                    P.inc(i2, "v", ("kbp", g, lh, qt))
                            P.wait(e, ("exp", g, lhs2[1], qt))
                            P.wait(e, ("kbd", g, lhs2[1], qt))
                            if j >= 1:
                                P.wait(e, ("kbp", g, lhs2[1], qt))
                            for s_, lh in enumerate(lhs2):
                                P.buf_claim(e, ("Pm", s_), ("pt2", p, nbat - 1))
                                if j >= 1:
                                    i2 = e.scalar_tensor_tensor(
                                        Pm[s_][:, 0:256 * j - 128].rearrange("p (c s) -> p c s", s=16),
                                        praw[s_][:, 0:256 * j - 128].rearrange("p (c s) -> p c s", s=16),
                                        1.0, bc16(K16[par][:, s_, 0:16 * j - 8]),
                                        OP.mult, OP.mult, accum_out=racc[par][:, s_, 0:1]) if E else None
                                    i2 = e.scalar_tensor_tensor(
                                        Pm[s_][:, 256 * j - 128:256 * j], praw[s_][:, 256 * j - 128:256 * j],
                                        1.0, KbP[par][:, s_, :], OP.mult, OP.mult,
                                        accum_out=racc[par][:, s_, 1:2]) if E else None
                                i2 = e.scalar_tensor_tensor(
                                    Pm[s_][:, 256 * j:ext], praw[s_][:, 256 * j:ext],
                                    1.0, KbD[par][:, s_, :], OP.mult, OP.mult,
                                    accum_out=racc[par][:, s_, 2:3]) if E else None
                                P.inc(i2, "v", ("pmask", g, lh, qt))
                            P.wait(e, ("pmask", g, lhs2[1], qt))
                            for s_, lh in enumerate(lhs2):
                                if j >= 1:
                                    i2 = e.tensor_reduce(rr[par][:, s_:s_ + 1], racc[par][:, s_, :], AX.X, OP.add) if E else None
                                else:
                                    i2 = e.tensor_copy(rr[par][:, s_:s_ + 1], racc[par][:, s_, 2:3]) if E else None
                                P.inc(i2, "v", ("rsum", g, lh, qt))
                            P.wait(e, ("rsum", g, lhs2[1], qt))
                            i2 = e.reciprocal(rcp[par][:], rr[par][:]) if E else None
                            P.inc(i2, "v", ("rcp", g, lhs2[1], qt))

                        def emit_rrowv(p):
                            g, qt, lhs2 = subs_of(p)
                            par = p % 2
                            P.wait(e, ("rbt", g, lhs2[1], qt))
                            i2 = e.tensor_copy(rrow[par][:], RBp[0:1, 256:512]) if E else None
                            P.inc(i2, "v", ("rrowev", g, lhs2[1], qt))

                        def emit_fin(p):
                            g, qt, lhs2 = subs_of(p)
                            j, ext, n512, nkb, nbat = geom(g, qt)
                            par = p % 2
                            P.wait(e, ("rbmm", g, lhs2[1], qt))
                            i2 = e.tensor_copy(Rbs[par][:], RBp[:, 0:256].rearrange("p (s d) -> p s d", d=128)) if E else None
                            P.inc(i2, "v", ("rbev", g, lhs2[1], qt))
                            P.wait(e, ("rbev", g, lhs2[1], qt))
                            P.wait(e, ("avb2", p, nbat - 1))
                            for s_, lh in enumerate(lhs2):
                                oti = 2 * p + s_
                                wsem = "wot0" if oti % 2 == 0 else "wot1"
                                P.wait_min(e, wsem, 16 * (oti // 2))
                                i2 = e.tensor_tensor(ob[oti % 2][:], AVp[:, s_ * 128:(s_ + 1) * 128],
                                                     Rbs[par][:, s_, :], OP.mult) if E else None
                                P.inc(i2, "v", ("oev", g, lh, qt))

                        emit_s3(0)
                        for p in range(NP):
                            if p >= 1:
                                emit_fin(p - 1)
                            if p + 1 < NP:
                                emit_s3(p + 1)
                            emit_s5(p)
                            emit_rrowv(p)
                        emit_fin(NP - 1)

                    eng_sections(dict(sync=p4_sync, tensor=p4_tensor,
                                        scalar=p4_scalar, vector=p4_vector))
                    barrier()


            if upto >= 6:
                with contextlib.ExitStack() as ph:
                    wob = [(ph.enter_context(nc.sbuf_tensor(f"wob{i}", [128, HC, 512], BF16)) if E else None) for i in range(2)]
                    wub = [(ph.enter_context(nc.sbuf_tensor(f"wub{i}", [128, HC, 512], BF16)) if E else None) for i in range(2)]
                    Ab = (ph.enter_context(nc.sbuf_tensor("Ab", [128, HC, 128], BF16)) if E else None)
                    Bb = (ph.enter_context(nc.sbuf_tensor("Bb", [128, HID], BF16)) if E else None)
                    otb = [(ph.enter_context(nc.sbuf_tensor(f"otb{i}", [128, 32, 128], BF16)) if E else None) for i in range(2)]
                    xnl = [(ph.enter_context(nc.sbuf_tensor(f"xnl{i}", [128, HC, 128], BF16)) if E else None) for i in range(2)]
                    xsl = [(ph.enter_context(nc.sbuf_tensor(f"xsl{i}", [128, 512], F32)) if E else None) for i in range(2)]
                    zT = [(ph.enter_context(nc.sbuf_tensor(f"zT{i}", [128, 128], BF16)) if E else None) for i in range(NVT)]
                    lg = (ph.enter_context(nc.sbuf_tensor("lg", [128, 4], F32)) if E else None)
                    mmx = (ph.enter_context(nc.sbuf_tensor("mmx", [128, 1], F32)) if E else None)
                    ge4 = (ph.enter_context(nc.sbuf_tensor("ge4", [128, 4], F32)) if E else None)
                    oh4 = (ph.enter_context(nc.sbuf_tensor("oh4", [128, 4], F32)) if E else None)
                    sc1 = (ph.enter_context(nc.sbuf_tensor("sc1", [128, 1], F32)) if E else None)
                    sc2 = (ph.enter_context(nc.sbuf_tensor("sc2", [128, 1], F32)) if E else None)
                    zsel = (ph.enter_context(nc.sbuf_tensor("zsel", [128, 128], BF16)) if E else None)
                    gsl = (ph.enter_context(nc.sbuf_tensor("gsl", [128, 512], F32)) if E else None)
                    ssl2 = (ph.enter_context(nc.sbuf_tensor("ssl2", [128, 512], F32)) if E else None)
                    sg = (ph.enter_context(nc.sbuf_tensor("sg", [128, 512], F32)) if E else None)
                    yb = (ph.enter_context(nc.sbuf_tensor("yb", [128, 512], F32)) if E else None)
                    spsP = (ph.enter_context(nc.psum_tensor("spsP", [128, 512], F32)) if E else None)
                    fpsP = (ph.enter_context(nc.psum_tensor("fpsP", [128, 512], F32)) if E else None)
                    zpsP = (ph.enter_context(nc.psum_tensor("zpsP", [128, 512], F32)) if E else None)
                    ztpP = (ph.enter_context(nc.psum_tensor("ztpP", [128, 512], BF16)) if E else None)

                    def p6_gp(e):
                        i = e.dma_start(out=Ab[:], in_=A_padT.rearrange("(c p) n -> p c n", p=128)) if E else None
                        P.inc(i, "gab", ("gA",), 16)
                        i = e.dma_start(out=Bb[:], in_=B_padT) if E else None
                        P.inc(i, "gab", ("gB",), 16)
                        for mb in range(8):
                            for lt in range(NVT):
                                idx = mb * NVT + lt
                                P.wait(e, ("sev6", mb, lt))
                                P.wait(e, ("silu", mb, lt))
                                i = e.tensor_tensor(sg[:], ssl2[:], gsl[:], OP.mult) if E else None
                                P.inc(i, "p", ("sg", mb, lt))
                                P.wait(e, ("sg", mb, lt))
                                P.wait_min(e, "wy", 16 * idx)
                                P.wait_min(e, "dxs0" if idx % 2 == 0 else "dxs1", 16 * (idx // 2 + 1))
                                i = e.tensor_tensor(yb[:], sg[:], xsl[idx % 2][:], OP.add) if E else None
                                P.inc(i, "p", ("yb", mb, lt))

                    def p6_sync(e):
                        P.wait_min(e, "wot0", 16 * 256)
                        P.wait_min(e, "wot1", 16 * 256)
                        P.wait_min(e, "wxno", 16 * 4)
                        P.wait_min(e, "gcwo", 16 * 8)
                        P.wait_min(e, "gcwup", 16 * 8)

                        def wload(mb):
                            if E:
                                i = e.dma_start(out=wob[mb % 2][:],
                                                in_=wo_bf_d.rearrange("(c p) m -> p c m", p=128)[:, :, mb * 512:(mb + 1) * 512])
                            else:
                                i = None
                            P.inc(i, "gwo" if mb % 2 == 0 else "gwo1", ("gwo", mb), 16)
                            if E:
                                i = e.dma_start(out=wub[mb % 2][:],
                                                in_=wup_bf_d.rearrange("(c p) m -> p c m", p=128)[:, :, mb * 512:(mb + 1) * 512])
                            else:
                                i = None
                            P.inc(i, "gwup" if mb % 2 == 0 else "gwup1", ("gwup", mb), 16)

                        wload(0)
                        wload(1)
                        n = 0
                        for mb in range(8):
                            if mb >= 2:
                                P.wait(e, ("op", mb - 2, NVT - 1))
                                P.wait(e, ("fp", mb - 2, NVT - 1))
                                wload(mb)
                            for lt in range(NVT):
                                slot = n % 2
                                if n >= 2:
                                    pmb, plt = divmod(n - 2, NVT)
                                    P.wait(e, ("op", pmb, plt))
                                    P.wait(e, ("fp", pmb, plt))
                                    P.wait(e, ("yb", pmb, plt))
                                if E:
                                    i = e.dma_start(out=otb[slot][:], in_=oT_d[lt].rearrange("h d q -> d h q"))
                                else:
                                    i = None
                                P.inc(i, "dot0" if slot == 0 else "dot1", ("p6o", mb, lt), 16)
                                if E:
                                    dd = xnoT_d.rearrange("(c p) s -> p c s", p=128)
                                    i = e.dma_start(out=xnl[slot][:], in_=dd[:, :, lt * 128:(lt + 1) * 128])
                                else:
                                    i = None
                                P.inc(i, "dxl0" if slot == 0 else "dxl1", ("p6n", mb, lt), 16)
                                i = e.dma_start(out=xsl[slot][:],
                                                in_=x_own[lt * 128:(lt + 1) * 128, mb * 512:(mb + 1) * 512]) if E else None
                                P.inc(i, "dxs0" if slot == 0 else "dxs1", ("p6x", mb, lt), 16)
                                n += 1

                    def p6_tensor(e):
                        P.wait_min(e, "gab", 32)
                        n = 0
                        for mb in range(8):
                            P.wait_min(e, "gwo" if mb % 2 == 0 else "gwo1", 16 * (mb // 2 + 1))
                            P.wait_min(e, "gwup" if mb % 2 == 0 else "gwup1", 16 * (mb // 2 + 1))
                            for lt in range(NVT):
                                slot = n % 2
                                sname_o = "dot0" if slot == 0 else "dot1"
                                sname_n = "dxl0" if slot == 0 else "dxl1"
                                P.wait_min(e, sname_o, 16 * (n // 2 + 1))
                                P.wait_min(e, sname_n, 16 * (n // 2 + 1))
                                if mb == 0:
                                    P.psum_claim(e, ("z",), [("zsel", lt)])
                                    for hc in range(HC):
                                        i = e.matmul(zpsP[:, 0:128], xnl[slot][:, hc, :], Ab[:, hc, :],
                                                     start=(hc == 0), stop=(hc == HC - 1)) if E else None
                                    P.inc(i, "t", ("zmm", lt))
                                    P.wait(e, ("zsel", lt))
                                    P.psum_claim(e, ("zt",), [("ztev", lt)])
                                    i = e.transpose(ztpP[:, 0:128], zsel[:], c_id[:]) if E else None
                                    P.inc(i, "t", ("ztr", lt))
                                P.psum_claim(e, ("sps",), [("sev6", mb, lt)])
                                for hd in range(NQH):
                                    i = e.matmul(spsP[:], otb[slot][:, hd, :], wob[mb % 2][:, hd, :],
                                                 start=(hd == 0), stop=(hd == NQH - 1)) if E else None
                                P.inc(i, "t", ("op", mb, lt))
                                P.psum_claim(e, ("fps",), [("silu", mb, lt)])
                                P.wait(e, ("ztev", lt))
                                for hc in range(HC):
                                    i = e.matmul(fpsP[:], xnl[slot][:, hc, :], wub[mb % 2][:, hc, :],
                                                 start=(hc == 0), stop=False) if E else None
                                i = e.matmul(fpsP[:], zT[lt][:], Bb[:, mb * 512:(mb + 1) * 512],
                                             start=False, stop=True) if E else None
                                P.inc(i, "t", ("fp", mb, lt))
                                n += 1

                    def p6_vector(e):
                        for lt in range(NVT):
                            P.wait(e, ("zmm", lt))
                            if E:
                                zz = zpsP[:, 0:128].rearrange("p (e r) -> p e r", r=32)[:, :, 31:32]
                                cb = bass.AP(c_bb[:].tensor, c_bb[:].offset, list(c_bb[:].ap) + [[0, 1]])
                                i = e.tensor_tensor(lg[:], zz, cb, OP.add)
                            else:
                                i = None
                            P.inc(i, "v", ("lg", lt))
                            P.wait(e, ("lg", lt))
                            i = e.tensor_reduce(mmx[:], lg[:], AX.X, OP.max) if E else None
                            P.inc(i, "v", ("lmx", lt))
                            P.wait(e, ("lmx", lt))
                            i = e.tensor_scalar(ge4[:], lg[:], mmx[:], None, OP.is_ge) if E else None
                            P.inc(i, "v", ("ge4", lt))
                            P.wait(e, ("ge4", lt))
                            i = e.tensor_copy(oh4[:, 0:1], ge4[:, 0:1]) if E else None
                            P.inc(i, "v", ("oh0", lt))
                            P.wait(e, ("oh0", lt))
                            i = e.tensor_scalar(sc1[:], oh4[:, 0:1], -1.0, 1.0, OP.mult, OP.add) if E else None
                            P.inc(i, "v", ("nh0", lt))
                            P.wait(e, ("nh0", lt))
                            i = e.tensor_tensor(oh4[:, 1:2], ge4[:, 1:2], sc1[:], OP.mult) if E else None
                            P.inc(i, "v", ("oh1", lt))
                            P.wait(e, ("oh1", lt))
                            i = e.tensor_scalar(sc2[:], oh4[:, 1:2], -1.0, 1.0, OP.mult, OP.add) if E else None
                            P.inc(i, "v", ("nh1", lt))
                            P.wait(e, ("nh1", lt))
                            i = e.tensor_tensor(sc1[:], sc1[:], sc2[:], OP.mult) if E else None
                            P.inc(i, "v", ("n01", lt))
                            P.wait(e, ("n01", lt))
                            i = e.tensor_tensor(oh4[:, 2:3], ge4[:, 2:3], sc1[:], OP.mult) if E else None
                            P.inc(i, "v", ("oh2", lt))
                            P.wait(e, ("oh2", lt))
                            i = e.tensor_scalar(sc2[:], oh4[:, 2:3], -1.0, 1.0, OP.mult, OP.add) if E else None
                            P.inc(i, "v", ("nh2", lt))
                            P.wait(e, ("nh2", lt))
                            i = e.tensor_tensor(sc1[:], sc1[:], sc2[:], OP.mult) if E else None
                            P.inc(i, "v", ("n012", lt))
                            P.wait(e, ("n012", lt))
                            i = e.tensor_tensor(oh4[:, 3:4], ge4[:, 3:4], sc1[:], OP.mult) if E else None
                            P.inc(i, "v", ("oh3", lt))
                            P.wait(e, ("oh3", lt))
                            for ex in range(4):
                                i = e.tensor_scalar(zsel[:, ex * 32:(ex + 1) * 32], zpsP[:, ex * 32:(ex + 1) * 32],
                                                    oh4[:, ex:ex + 1], None, OP.mult) if E else None
                            P.inc(i, "v", ("zsel", lt))
                            P.wait(e, ("ztr", lt))
                            i = e.tensor_copy(zT[lt][:], ztpP[:, 0:128]) if E else None
                            P.inc(i, "v", ("ztev", lt))

                    def p6_scalar(e):
                        for mb in range(8):
                            for lt in range(NVT):
                                idx = mb * NVT + lt
                                P.wait(e, ("op", mb, lt))
                                if idx >= 1:
                                    pmb, plt = divmod(idx - 1, NVT)
                                    P.wait(e, ("sg", pmb, plt))
                                i = e.copy(ssl2[:], spsP[:]) if E else None
                                P.inc(i, "a", ("sev6", mb, lt))
                                P.wait(e, ("fp", mb, lt))
                                i = e.activation(gsl[:], fpsP[:], AF.Silu) if E else None
                                P.inc(i, "a", ("silu", mb, lt))
                                P.wait(e, ("yb", mb, lt))
                                i = e.dma_start(out=y_out[lt * 128:(lt + 1) * 128, mb * 512:(mb + 1) * 512],
                                                in_=yb[:]) if E else None
                                P.inc(i, "wy", ("yst", mb, lt), 16)

                    eng_sections(dict(gpsimd=p6_gp, sync=p6_sync, tensor=p6_tensor,
                                        vector=p6_vector, scalar=p6_scalar))

                    def fin_sync(e):
                        P.wait_min(e, "wy", 16 * 128)
                    eng_sections(dict(sync=fin_sync))

            if upto < 6:
                def dbg_fin(e):
                    for s in ["wxnt", "wxno", "wkt", "wvs0", "wvs1", "wqt",
                              "wot0", "wot1"]:
                        if P.cnt.get(s):
                            P.wait_min(e, s, P.cnt[s])
                eng_sections(dict(sync=dbg_fin))

        P.begin(False)
        run()
        P.begin(True)
        run()

    return nc


_CACHED = {}


def kernel(L0_attn_hidden_states, L0_attn_wq, L0_attn_wk, L0_attn_wv, L0_attn_wo,
           L0_ffn_W_up, L0_ffn_gate_w, L0_ffn_expert_bias, L0_ffn_lora_A, L0_ffn_lora_B):
    x = np.ascontiguousarray(np.asarray(L0_attn_hidden_states, dtype=np.float32))
    wq = np.asarray(L0_attn_wq, np.float32)
    wk = np.asarray(L0_attn_wk, np.float32)
    wv = np.asarray(L0_attn_wv, np.float32)
    wo = np.asarray(L0_attn_wo, np.float32)
    wup = np.asarray(L0_ffn_W_up, np.float32)
    gw = np.asarray(L0_ffn_gate_w, np.float32)
    eb = np.asarray(L0_ffn_expert_bias, np.float32)
    lA = np.asarray(L0_ffn_lora_A, np.float32)
    lB = np.asarray(L0_ffn_lora_B, np.float32)

    inv = 1.0 / (10000.0 ** (np.arange(0, DH, 2, dtype=np.float32) / DH))
    ang = np.arange(SEQ, dtype=np.float32)[:, None] * inv[None, :]
    emb = np.concatenate([ang, ang], axis=1)
    cosf = np.cos(emb).astype(np.float32)
    sinf = np.sin(emb).astype(np.float32)
    scale = 1.0 / math.sqrt(DH)
    identb = np.eye(128, dtype=ml_dtypes.bfloat16)
    R = np.zeros((128, 128), np.float32)
    R[0:64, 64:128] = -np.eye(64)
    R[64:128, 0:64] = np.eye(64)
    rotT = np.ascontiguousarray(R.T).astype(ml_dtypes.bfloat16)
    ones1 = np.ones((1, 128), np.float32)

    A_padT = np.zeros((HID, 128), np.float32)
    for ex in range(4):
        A_padT[:, ex * 32:ex * 32 + 8] = lA[ex].T
        A_padT[:, ex * 32 + 31] = gw[ex]
    B_padT = np.zeros((128, HID), np.float32)
    for ex in range(4):
        B_padT[ex * 32:ex * 32 + 8, :] = lB[ex].T
    ebias = eb.reshape(1, 4).astype(np.float32)

    pcol = np.arange(128)[:, None]
    j1 = np.arange(256)[None, :]
    j2 = np.arange(128)[None, :]

    wqT = np.ascontiguousarray(wq.T)
    wkT = np.ascontiguousarray(wk.T)
    wvT = np.ascontiguousarray(wv.T)
    woT = np.ascontiguousarray(wo.T)
    wupT = np.ascontiguousarray(wup.T)

    in_maps = []
    for c in range(8):
        b, h = divmod(c, 2)
        own = np.arange(OWN) * 2 + h
        trim = np.where(j1 <= 2 * pcol + h, 0.0, NEG).astype(np.float32)
        rec = (j1 > 2 * pcol + h - 128).astype(np.float32)
        utp = (j2 > 2 * pcol + h).astype(np.float32)
        in_maps.append(dict(
            x_full=x[b],
            x_own=np.ascontiguousarray(x[b, own]),
            wqT=wqT, wkT=wkT, wvT=wvT, woT=woT, wupT=wupT,
            A_padT=A_padT, B_padT=B_padT, ebias=ebias,
            cosq=np.ascontiguousarray((cosf[own] * scale).T),
            sinq=np.ascontiguousarray((sinf[own] * scale).T),
            cosk=np.ascontiguousarray(cosf.T),
            sink=np.ascontiguousarray(sinf.T),
            identb=identb, identf=np.eye(128, dtype=np.float32), rotT=rotT,
            trimask2=trim, recent2=rec, ut2prev=utp,
            ones1=ones1,
        ))

    if "prog" not in _CACHED:
        _CACHED["prog"] = build_program()
    nc = _CACHED["prog"]

    res = run_bass_kernel_spmd(nc, in_maps, core_ids=list(range(8)))

    out = np.empty((NB, SEQ, HID), np.float32)
    for c in range(8):
        b, h = divmod(c, 2)
        out[b, h::2] = np.asarray(res.results[c]["y"], np.float32)
    return out


# version 09
# speedup vs baseline: 1.0999x; 1.0140x over previous
import contextlib
import math

import numpy as np
import ml_dtypes

import concourse.bass as bass
import concourse.mybir as mybir
from concourse.alu_op_type import AluOpType as OP
from concourse.bass_utils import run_bass_kernel_spmd

F32 = mybir.dt.float32
BF16 = mybir.dt.bfloat16
AF = mybir.ActivationFunctionType
AX = mybir.AxisListType

NEG = float(np.finfo(np.float32).min)
HID = 4096
SEQ = 4096
NB = 4
NQH = 32
NKV = 8
DH = 128
OWN = 2048
NVT = 16
NGRP = 4
HC = 32


def bc16(ap, reps=16):
    return bass.AP(ap.tensor, ap.offset, list(ap.ap) + [[0, reps]])


class Prog:
    def __init__(self):
        self.plan = {}
        self.emitting = False
        self.cnt = {}
        self.sem = {}

    def begin(self, emitting):
        self.emitting = emitting
        self.cnt = {}
        self.psum_readers = {}
        self.buf_consumer = {}

    def inc(self, inst, semname, key, n=1):
        c = self.cnt.get(semname, 0) + n
        self.cnt[semname] = c
        if self.emitting:
            assert self.plan[key] == (semname, c), (key, semname, c, self.plan.get(key))
            inst.then_inc(self.sem[semname], n)
        else:
            assert key not in self.plan, key
            self.plan[key] = (semname, c)

    def wait(self, eng, key):
        if self.emitting:
            semname, c = self.plan[key]
            eng.wait_ge(self.sem[semname], c)

    def wait_min(self, eng, semname, c):
        if self.emitting and c > 0:
            eng.wait_ge(self.sem[semname], c)

    def psum_claim(self, eng, bank, reader_keys):
        prev = self.psum_readers.get(bank)
        if prev:
            for k in prev:
                self.wait(eng, k)
        self.psum_readers[bank] = list(reader_keys)

    def buf_claim(self, eng, buf, consumer_key):
        prev = self.buf_consumer.get(buf)
        if prev is not None:
            self.wait(eng, prev)
        self.buf_consumer[buf] = consumer_key


def geom(g, qt):
    j = 4 * g + qt
    ext = 256 * (j + 1)
    n512 = (j + 2) // 2
    nkb = 2 * (j + 1)
    nbat = (nkb + 3) // 4
    return j, ext, n512, nkb, nbat


def blkN(j, b, n512, ext):
    return 512 if b < n512 - 1 else ext - 512 * (n512 - 1)


def build_program(upto=99, debug=False):
    nc = bass.Bass("TRN2", target_bir_lowering=False)

    def din(name, shape, dt=F32):
        return nc.dram_tensor(name, list(shape), dt, kind="ExternalInput").ap()

    x_full = din("x_full", (SEQ, HID))
    x_own = din("x_own", (OWN, HID))
    wqT = din("wqT", (HID, NQH * DH))
    wkT = din("wkT", (HID, NKV * DH))
    wvT = din("wvT", (HID, NKV * DH))
    woT = din("woT", (HID, HID))
    wupT = din("wupT", (HID, HID))
    A_padT = din("A_padT", (HID, 128))
    B_padT = din("B_padT", (128, HID))
    ebias = din("ebias", (1, 4))
    cosq = din("cosq", (DH, OWN))
    sinq = din("sinq", (DH, OWN))
    cosk = din("cosk", (DH, SEQ))
    sink = din("sink", (DH, SEQ))
    identb = din("identb", (128, 128), BF16)
    identf = din("identf", (128, 128))
    rotT = din("rotT", (128, 128), BF16)
    trimask2 = din("trimask2", (128, 256))
    recent2 = din("recent2", (128, 256))
    ut2prev = din("ut2prev", (128, 128))
    ones1 = din("ones1", (1, 128))

    y_out = nc.dram_tensor("y", [OWN, HID], F32, kind="ExternalOutput").ap()

    KIND = "ExternalOutput" if debug else "Internal"
    xnT_d = nc.dram_tensor("xnT_d", [HID, SEQ], BF16, kind=KIND).ap()
    xnoT_d = nc.dram_tensor("xnoT_d", [HID, OWN], BF16, kind=KIND).ap()
    qT_d = nc.dram_tensor("qT_d", [NQH, DH, OWN], BF16, kind=KIND).ap()
    kT_d = nc.dram_tensor("kT_d", [NKV, DH, SEQ], BF16, kind=KIND).ap()
    vS_d = nc.dram_tensor("vS_d", [NKV, SEQ, DH], BF16, kind=KIND).ap()
    oT_d = nc.dram_tensor("oT_d", [NVT, NQH, DH, 128], BF16, kind=KIND).ap()
    wo_bf_d = nc.dram_tensor("wo_bf_d", [HID, HID], BF16, kind="Internal").ap()
    wup_bf_d = nc.dram_tensor("wup_bf_d", [HID, HID], BF16, kind="Internal").ap()

    P = Prog()

    semnames = [
        "t", "v", "a", "p", "bar", "dcst",
        "dx0", "dx1", "dxo0", "dxo1", "wxnt", "wxno",
        "gwk", "gwv", "gwq", "gwo", "gwup", "gab", "gcwo", "gcwup", "gwo1", "gwup1",
        "dnb0", "dnb1",
        "wkt", "wvs0", "wvs1", "wqt",
        "dk0", "dk1", "dk2", "dv0", "dv1", "dv2", "dq0", "dq1", "dq2", "dq3", "dq4", "dq5", "dq6", "dq7",
        "wot0", "wot1",
        "dot0", "dot1", "dxl0", "dxl1", "dxs0", "dxs1",
        "wy",
    ]

    with contextlib.ExitStack() as top:
        for s in semnames:
            P.sem[s] = top.enter_context(nc.semaphore(s))
        block = top.enter_context(nc.Block())

        c_id = top.enter_context(nc.sbuf_tensor("c_id", [128, 128], BF16))
        c_idf = top.enter_context(nc.sbuf_tensor("c_idf", [128, 128], F32))
        c_rt = top.enter_context(nc.sbuf_tensor("c_rt", [128, 128], BF16))
        c_tri = top.enter_context(nc.sbuf_tensor("c_tri", [128, 256], F32))
        c_rec = top.enter_context(nc.sbuf_tensor("c_rec", [128, 256], F32))
        c_utp = top.enter_context(nc.sbuf_tensor("c_utp", [128, 128], F32))
        c_ones = top.enter_context(nc.sbuf_tensor("c_ones", [1, 128], F32))
        c_eb = top.enter_context(nc.sbuf_tensor("c_eb", [1, 4], F32))
        c_bb = top.enter_context(nc.sbuf_tensor("c_bb", [128, 4], F32))
        c_eps = top.enter_context(nc.sbuf_tensor("c_eps", [128, 1], F32))
        c_dmv = top.enter_context(nc.sbuf_tensor("c_dmv", [128, 1], F32))
        c_dma_ = top.enter_context(nc.sbuf_tensor("c_dma_", [128, 1], F32))
        c_dmg = top.enter_context(nc.sbuf_tensor("c_dmg", [128, 1], F32))

        barcnt = [0]

        def run():
            E = P.emitting
            barcnt[0] = 0

            def eng_sections(fns):
                engines = {}
                if E:
                    for nm in ["sync", "gpsimd", "tensor", "scalar", "vector"]:
                        f = fns.get(nm)
                        dec = getattr(block, nm)

                        def wrapper(eng, _f=f, _nm=nm):
                            engines[_nm] = eng
                            if _f:
                                _f(eng)
                        dec(wrapper)
                else:
                    for nm in ["sync", "gpsimd", "tensor", "scalar", "vector"]:
                        engines[nm] = None
                        f = fns.get(nm)
                        if f:
                            f(None)
                return engines

            def barrier():
                barcnt[0] += 1
                tgt = 5 * barcnt[0]
                pfx = {"sync": "d", "scalar": "w", "gpsimd": "g"}

                def mk(nm, op):
                    def f(e):
                        pf = pfx.get(nm)
                        if pf:
                            for s in P.sem:
                                if s.startswith(pf) and P.cnt.get(s):
                                    P.wait_min(e, s, P.cnt[s])
                        i = op(e) if P.emitting else None
                        P.inc(i, "bar", ("bar", barcnt[0], nm), 1)
                        P.wait_min(e, "bar", tgt)
                    return f
                eng_sections(dict(
                    sync=mk("sync", lambda e: e.nop()),
                    gpsimd=mk("gpsimd", lambda e: e.memset(c_dmg[:], 0.0)),
                    tensor=mk("tensor", lambda e: e.nop()),
                    scalar=mk("scalar", lambda e: e.memzero(c_dma_[:])),
                    vector=mk("vector", lambda e: e.memset(c_dmv[:], 0.0)),
                ))

            with contextlib.ExitStack() as ph:
                bbp = (ph.enter_context(nc.psum_tensor("bbp", [128, 512], F32)) if E else None)

                def p0_sync(e):
                    loads = [
                        (c_id, identb), (c_rt, rotT), (c_tri, trimask2),
                        (c_rec, recent2), (c_utp, ut2prev), (c_ones, ones1),
                        (c_eb, ebias), (c_idf, identf),
                    ]
                    for i2, (dst, src) in enumerate(loads):
                        inst = e.dma_start(out=dst[:], in_=src) if E else None
                        P.inc(inst, "dcst", ("dcst", i2), 16)

                def p0_gp(e):
                    for w_src, w_dst, tag in [(woT, wo_bf_d, "cwo"), (wupT, wup_bf_d, "cwup")]:
                        for ch in range(8):
                            i = e.dma_start(out=w_dst[ch * 512:(ch + 1) * 512, :],
                                            in_=w_src[ch * 512:(ch + 1) * 512, :],
                                            max_dma_last_dim=8192) if E else None
                            P.inc(i, "gcwo" if tag == "cwo" else "gcwup", (tag, ch), 16)

                def p0_tensor(e):
                    P.wait_min(e, "dcst", 16 * 8)
                    i = e.matmul(bbp[:, 0:4], c_ones[:], c_eb[:], start=True, stop=True) if E else None
                    P.inc(i, "t", ("bbmm",))

                def p0_vector(e):
                    P.wait(e, ("bbmm",))
                    i = e.tensor_copy(c_bb[:], bbp[:, 0:4]) if E else None
                    P.inc(i, "v", ("bbev",))
                    i = e.memset(c_eps[:], 1e-5) if E else None
                    P.inc(i, "v", ("epss",))

                eng_sections(dict(sync=p0_sync, gpsimd=p0_gp, tensor=p0_tensor, vector=p0_vector))
                barrier()

            if upto >= 1:
                with contextlib.ExitStack() as ph:
                    xb = [(ph.enter_context(nc.sbuf_tensor(f"xb{i}", [128, HID], F32)) if E else None) for i in range(2)]
                    stats = (ph.enter_context(nc.sbuf_tensor("stats", [128, 8, 6], F32)) if E else None)
                    mv = (ph.enter_context(nc.sbuf_tensor("mv", [128, 2], F32)) if E else None)
                    nmn = (ph.enter_context(nc.sbuf_tensor("nmn", [128, 1], F32)) if E else None)
                    lnv = (ph.enter_context(nc.sbuf_tensor("lnv", [128, 1], F32)) if E else None)
                    rstd = (ph.enter_context(nc.sbuf_tensor("rstd", [128, 1], F32)) if E else None)
                    xnb = (ph.enter_context(nc.sbuf_tensor("xnb", [128, HID], BF16)) if E else None)
                    col = (ph.enter_context(nc.sbuf_tensor("col", [128, HC, 4, 128], BF16)) if E else None)
                    tp = [(ph.enter_context(nc.psum_tensor(f"p1tp{i}", [128, 1024], BF16)) if E else None) for i in range(2)]

                    jobs = [
                        ("full", x_full, 32, xnT_d, "dx0", "dx1", "wxnt"),
                        ("own", x_own, 16, xnoT_d, "dxo0", "dxo1", "wxno"),
                    ]

                    def p1_sync(e):
                        for jn, src, nt, dst, s0, s1, ws in jobs:
                            for t in range(nt):
                                slot = t % 2
                                sname = s0 if slot == 0 else s1
                                P.buf_claim(e, ("xb", slot), ("p1xn", jn, t))
                                i = e.dma_start(out=xb[slot][:], in_=src[t * 128:(t + 1) * 128, :]) if E else None
                                P.inc(i, sname, ("p1x", jn, t), 16)

                    def p1_vector(e):
                        for jn, src, nt, dst, s0, s1, ws in jobs:
                            for t in range(nt):
                                slot = t % 2
                                sname = s0 if slot == 0 else s1
                                P.wait_min(e, sname, 16 * (t // 2 + 1))
                                for i8 in range(8):
                                    i = e.bn_stats(stats[:, i8, :], xb[slot][:, i8 * 512:(i8 + 1) * 512]) if E else None
                                P.inc(i, "v", ("p1bn", jn, t))
                                P.wait(e, ("p1bn", jn, t))
                                i = e.bn_aggr(mv[:], stats[:]) if E else None
                                P.inc(i, "v", ("p1ag", jn, t))
                                P.wait(e, ("p1ag", jn, t))
                                i = e.tensor_scalar(nmn[:], mv[:, 0:1], -1.0, None, OP.mult) if E else None
                                P.inc(i, "v", ("p1nm", jn, t))
                                P.wait(e, ("p1nm", jn, t))
                                P.wait(e, ("p1rs", jn, t))
                                P.buf_claim(e, "xnb", ("p1tp", jn, t, 3))
                                i = e.tensor_scalar(xnb[:], xb[slot][:], nmn[:], rstd[:], OP.add, OP.mult) if E else None
                                P.inc(i, "v", ("p1xn", jn, t))

                    def p1_scalar(e):
                        for jn, src, nt, dst, s0, s1, ws in jobs:
                            for t in range(nt):
                                P.wait(e, ("p1ag", jn, t))
                                i = e.activation(lnv[:], mv[:, 1:2], AF.Ln, bias=c_eps[:], scale=1.0) if E else None
                                P.inc(i, "a", ("p1ln", jn, t))
                                P.wait(e, ("p1ln", jn, t))
                                i = e.activation(rstd[:], lnv[:], AF.Exp, scale=-0.5) if E else None
                                P.inc(i, "a", ("p1rs", jn, t))
                                for b in range(4):
                                    if t % 4 == 0 and b == 0:
                                        if jn == "own" and t == 0:
                                            P.wait_min(e, "wxnt", 16 * 8)
                                        g = t // 4
                                        P.wait_min(e, ws, 16 * g)
                                    P.wait(e, ("p1tp", jn, t, b))
                                    i = e.copy(col[:, b * 8:(b + 1) * 8, t % 4, :],
                                               tp[b % 2][:].rearrange("p (c n) -> p c n", n=128)) if E else None
                                    P.inc(i, "a", ("p1ev", jn, t, b))
                                if t % 4 == 3:
                                    g = t // 4
                                    P.wait(e, ("p1ev", jn, t, 3))
                                    for _tt in range(t - 3, t):
                                        P.wait(e, ("p1ev", jn, _tt, 3))
                                    if E:
                                        dd = dst.rearrange("(c p) s -> p c s", p=128)
                                        i = e.dma_start(out=dd[:, :, g * 512:(g + 1) * 512],
                                                        in_=col[:].rearrange("p c f n -> p c (f n)"))
                                    else:
                                        i = None
                                    P.inc(i, ws, ("p1st", jn, g), 16)

                    def p1_tensor(e):
                        for jn, src, nt, dst, s0, s1, ws in jobs:
                            for t in range(nt):
                                P.wait(e, ("p1xn", jn, t))
                                for b in range(4):
                                    P.psum_claim(e, ("p1", b % 2), [("p1ev", jn, t, b)])
                                    for i8 in range(8):
                                        hcix = b * 8 + i8
                                        i = e.transpose(tp[b % 2][:, i8 * 128:(i8 + 1) * 128],
                                                        xnb[:, hcix * 128:(hcix + 1) * 128], c_id[:]) if E else None
                                    P.inc(i, "t", ("p1tp", jn, t, b))

                    eng_sections(dict(sync=p1_sync, vector=p1_vector,
                                        scalar=p1_scalar, tensor=p1_tensor))
                    barrier()

            if upto >= 2:
                with contextlib.ExitStack() as ph:
                    wkb = (ph.enter_context(nc.sbuf_tensor("wkb", [128, HC, NKV * DH], BF16)) if E else None)
                    xnbk = (ph.enter_context(nc.sbuf_tensor("xnbk", [128, HC, 512], BF16)) if E else None)
                    ksb = (ph.enter_context(nc.sbuf_tensor("ksb", [128, 512], BF16)) if E else None)
                    kt1 = (ph.enter_context(nc.sbuf_tensor("kt1", [128, 512], F32)) if E else None)
                    kt2 = (ph.enter_context(nc.sbuf_tensor("kt2", [128, 512], F32)) if E else None)
                    krsb = (ph.enter_context(nc.sbuf_tensor("krsb", [128, 512], BF16)) if E else None)
                    kout = (ph.enter_context(nc.sbuf_tensor("kout", [128, 512], BF16)) if E else None)
                    cskl = (ph.enter_context(nc.sbuf_tensor("cskl", [128, SEQ], F32)) if E else None)
                    sskl = (ph.enter_context(nc.sbuf_tensor("sskl", [128, SEQ], F32)) if E else None)
                    kpr = [(ph.enter_context(nc.psum_tensor(f"kpr{i}", [128, 512], F32)) if E else None) for i in range(2)]
                    rpr = [(ph.enter_context(nc.psum_tensor(f"rpr{i}", [128, 512], F32)) if E else None) for i in range(2)]

                    def p2k_gp(e):
                        i = e.dma_start(out=wkb[:], in_=wkT.rearrange("(c p) n -> p c n", p=128)) if E else None
                        P.inc(i, "gwk", ("gwk",), 16)
                        P.wait_min(e, "dcst", 16 * 10)
                        for tb in range(8):
                            for kh in range(NKV):
                                n = tb * NKV + kh
                                P.wait(e, ("p2kev", tb, kh))
                                i = e.tensor_tensor(kt1[:], ksb[:], cskl[:, tb * 512:(tb + 1) * 512], OP.mult) if E else None
                                P.inc(i, "p", ("p2kt1", tb, kh))
                                P.wait(e, ("p2krev", tb, kh))
                                i = e.tensor_tensor(kt2[:], krsb[:], sskl[:, tb * 512:(tb + 1) * 512], OP.mult) if E else None
                                P.inc(i, "p", ("p2kt2", tb, kh))
                                P.wait(e, ("p2kt2", tb, kh))
                                P.wait_min(e, "wkt", 16 * n)
                                i = e.tensor_tensor(kout[:], kt1[:], kt2[:], OP.add) if E else None
                                P.inc(i, "p", ("p2kr3", tb, kh))

                    def p2k_sync(e):
                        i = e.dma_start(out=cskl[:], in_=cosk) if E else None
                        P.inc(i, "dcst", ("csk",), 16)
                        i = e.dma_start(out=sskl[:], in_=sink) if E else None
                        P.inc(i, "dcst", ("ssk",), 16)
                        for tb in range(8):
                            if tb >= 1:
                                P.wait(e, ("p2kmm", tb - 1, NKV - 1))
                            if E:
                                dd = xnT_d.rearrange("(c p) s -> p c s", p=128)
                                i = e.dma_start(out=xnbk[:], in_=dd[:, :, tb * 512:(tb + 1) * 512])
                            else:
                                i = None
                            P.inc(i, "dnb0", ("p2kx", tb), 16)

                    def p2k_tensor(e):
                        P.wait_min(e, "gwk", 16)
                        for tb in range(8):
                            P.wait_min(e, "dnb0", 16 * (tb + 1))
                            for kh in range(NKV):
                                s = kh % 2
                                P.psum_claim(e, ("p2kk", s), [("p2kev", tb, kh)])
                                for hc in range(HC):
                                    i = e.matmul(kpr[s][:], wkb[:, hc, kh * 128:(kh + 1) * 128],
                                                 xnbk[:, hc, :], start=(hc == 0), stop=(hc == HC - 1)) if E else None
                                P.inc(i, "t", ("p2kmm", tb, kh))
                                P.wait(e, ("p2kev", tb, kh))
                                P.psum_claim(e, ("p2kr", s), [("p2krev", tb, kh)])
                                i = e.matmul(rpr[s][:], c_rt[:], ksb[:], start=True, stop=True) if E else None
                                P.inc(i, "t", ("p2krk", tb, kh))

                    def p2k_scalar(e):
                        for tb in range(8):
                            for kh in range(NKV):
                                n = tb * NKV + kh
                                s = kh % 2
                                P.wait(e, ("p2kmm", tb, kh))
                                if n >= 1:
                                    ptb, pkh = divmod(n - 1, NKV)
                                    P.wait(e, ("p2kt1", ptb, pkh))
                                    P.wait(e, ("p2krk", ptb, pkh))
                                i = e.copy(ksb[:], kpr[s][:]) if E else None
                                P.inc(i, "a", ("p2kev", tb, kh))
                                P.wait(e, ("p2krk", tb, kh))
                                if n >= 1:
                                    ptb, pkh = divmod(n - 1, NKV)
                                    P.wait(e, ("p2kt2", ptb, pkh))
                                i = e.copy(krsb[:], rpr[s][:]) if E else None
                                P.inc(i, "a", ("p2krev", tb, kh))
                                P.wait(e, ("p2kr3", tb, kh))
                                i = e.dma_start(out=kT_d[kh, :, tb * 512:(tb + 1) * 512], in_=kout[:]) if E else None
                                P.inc(i, "wkt", ("p2kst", tb, kh), 16)

                    eng_sections(dict(gpsimd=p2k_gp, sync=p2k_sync,
                                        tensor=p2k_tensor, scalar=p2k_scalar))
                    barrier()

            if upto >= 3:
                with contextlib.ExitStack() as ph:
                    wvb = (ph.enter_context(nc.sbuf_tensor("wvb", [128, HC, NKV * DH], BF16)) if E else None)
                    xnbv = (ph.enter_context(nc.sbuf_tensor("xnbv", [128, HC, 512], BF16)) if E else None)
                    vsb = [(ph.enter_context(nc.sbuf_tensor(f"vsb{i}", [128, 512], BF16)) if E else None) for i in range(2)]
                    vpr = [(ph.enter_context(nc.psum_tensor(f"vpr{i}", [128, 512], F32)) if E else None) for i in range(2)]

                    def p2v_gp(e):
                        i = e.dma_start(out=wvb[:], in_=wvT.rearrange("(c p) n -> p c n", p=128)) if E else None
                        P.inc(i, "gwv", ("gwv",), 16)

                    def p2v_sync(e):
                        for tb in range(8):
                            if tb >= 1:
                                P.wait(e, ("p2vmm", tb - 1, 3, 1))
                            if E:
                                dd = xnT_d.rearrange("(c p) s -> p c s", p=128)
                                i = e.dma_start(out=xnbv[:], in_=dd[:, :, tb * 512:(tb + 1) * 512])
                            else:
                                i = None
                            P.inc(i, "dnb1", ("p2vx", tb), 16)

                    def p2v_tensor(e):
                        P.wait_min(e, "gwv", 16)
                        for tb in range(8):
                            P.wait_min(e, "dnb1", 16 * (tb + 1))
                            for tt in range(4):
                                for vbk in range(2):
                                    n = tb * 8 + tt * 2 + vbk
                                    s = n % 2
                                    P.psum_claim(e, ("p2v", s), [("p2vev", tb, tt, vbk)])
                                    for hc in range(HC):
                                        i = e.matmul(vpr[s][:], xnbv[:, hc, tt * 128:(tt + 1) * 128],
                                                     wvb[:, hc, vbk * 512:(vbk + 1) * 512],
                                                     start=(hc == 0), stop=(hc == HC - 1)) if E else None
                                    P.inc(i, "t", ("p2vmm", tb, tt, vbk))

                    def p2v_scalar(e):
                        for tb in range(8):
                            for tt in range(4):
                                for vbk in range(2):
                                    n = tb * 8 + tt * 2 + vbk
                                    s = n % 2
                                    wsem = "wvs0" if s == 0 else "wvs1"
                                    P.wait(e, ("p2vmm", tb, tt, vbk))
                                    P.wait_min(e, wsem, 16 * 4 * (n // 2))
                                    i = e.copy(vsb[s][:], vpr[s][:]) if E else None
                                    P.inc(i, "a", ("p2vev", tb, tt, vbk))
                                    P.wait(e, ("p2vev", tb, tt, vbk))
                                    trow = tb * 512 + tt * 128
                                    for hh in range(4):
                                        kh = vbk * 4 + hh
                                        i = e.dma_start(out=vS_d[kh, trow:trow + 128, :],
                                                        in_=vsb[s][:, hh * 128:(hh + 1) * 128]) if E else None
                                        P.inc(i, wsem, ("p2vst", tb, tt, vbk, hh), 16)

                    eng_sections(dict(gpsimd=p2v_gp, sync=p2v_sync,
                                        tensor=p2v_tensor, scalar=p2v_scalar))
                    barrier()

            if upto >= 4:
                with contextlib.ExitStack() as ph:
                    wqb = (ph.enter_context(nc.sbuf_tensor("wqb", [128, HC, 8 * DH], BF16)) if E else None)
                    xnbq = (ph.enter_context(nc.sbuf_tensor("xnbq", [128, HC, 512], BF16)) if E else None)
                    qsb = (ph.enter_context(nc.sbuf_tensor("qsb", [128, 512], BF16)) if E else None)
                    qt1 = (ph.enter_context(nc.sbuf_tensor("qt1", [128, 512], F32)) if E else None)
                    qt2 = (ph.enter_context(nc.sbuf_tensor("qt2", [128, 512], F32)) if E else None)
                    qrsb = (ph.enter_context(nc.sbuf_tensor("qrsb", [128, 512], BF16)) if E else None)
                    qout = (ph.enter_context(nc.sbuf_tensor("qout", [128, 512], BF16)) if E else None)
                    cql = (ph.enter_context(nc.sbuf_tensor("cql", [128, OWN], F32)) if E else None)
                    sql = (ph.enter_context(nc.sbuf_tensor("sql", [128, OWN], F32)) if E else None)
                    qpr = [(ph.enter_context(nc.psum_tensor(f"qpr{i}", [128, 512], F32)) if E else None) for i in range(2)]
                    qrp = [(ph.enter_context(nc.psum_tensor(f"qrp{i}", [128, 512], F32)) if E else None) for i in range(2)]

                    def p3_gp(e):
                        for qq in range(4):
                            if qq >= 1:
                                P.wait(e, ("p3mm", qq - 1, 3, 7))
                            if E:
                                i = e.dma_start(out=wqb[:],
                                                in_=wqT.rearrange("(c p) n -> p c n", p=128)[:, :, qq * 1024:(qq + 1) * 1024])
                            else:
                                i = None
                            P.inc(i, "gwq", ("gwq", qq), 16)
                            for sp in range(4):
                                for q8 in range(8):
                                    P.wait(e, ("p3ev", qq, sp, q8))
                                    i = e.tensor_tensor(qt1[:], qsb[:], cql[:, sp * 512:(sp + 1) * 512], OP.mult) if E else None
                                    P.inc(i, "p", ("p3t1", qq, sp, q8))
                                    P.wait(e, ("p3rev", qq, sp, q8))
                                    i = e.tensor_tensor(qt2[:], qrsb[:], sql[:, sp * 512:(sp + 1) * 512], OP.mult) if E else None
                                    P.inc(i, "p", ("p3t2", qq, sp, q8))
                                    P.wait(e, ("p3t2", qq, sp, q8))
                                    idx = (qq * 4 + sp) * 8 + q8
                                    P.wait_min(e, "wqt", 16 * idx)
                                    i = e.tensor_tensor(qout[:], qt1[:], qt2[:], OP.add) if E else None
                                    P.inc(i, "p", ("p3r3", qq, sp, q8))

                    def p3_sync(e):
                        i = e.dma_start(out=cql[:], in_=cosq) if E else None
                        P.inc(i, "dcst", ("csq",), 16)
                        i = e.dma_start(out=sql[:], in_=sinq) if E else None
                        P.inc(i, "dcst", ("ssq",), 16)
                        n = 0
                        for qq in range(4):
                            for sp in range(4):
                                if n >= 1:
                                    pq, psp = divmod(n - 1, 4)
                                    P.wait(e, ("p3mm", pq, psp, 7))
                                if E:
                                    dd = xnoT_d.rearrange("(c p) s -> p c s", p=128)
                                    i = e.dma_start(out=xnbq[:], in_=dd[:, :, sp * 512:(sp + 1) * 512])
                                else:
                                    i = None
                                P.inc(i, "dnb0", ("p3x", qq, sp), 16)
                                n += 1

                    def p3_tensor(e):
                        P.wait_min(e, "dcst", 16 * 12)
                        n = 0
                        for qq in range(4):
                            P.wait_min(e, "gwq", 16 * (qq + 1))
                            for sp in range(4):
                                n += 1
                                P.wait_min(e, "dnb0", 16 * (8 + n))
                                for q8 in range(8):
                                    s = q8 % 2
                                    P.psum_claim(e, ("p3q", s), [("p3ev", qq, sp, q8)])
                                    for hc in range(HC):
                                        i = e.matmul(qpr[s][:], wqb[:, hc, q8 * 128:(q8 + 1) * 128],
                                                     xnbq[:, hc, :], start=(hc == 0), stop=(hc == HC - 1)) if E else None
                                    P.inc(i, "t", ("p3mm", qq, sp, q8))
                                    P.wait(e, ("p3ev", qq, sp, q8))
                                    P.psum_claim(e, ("p3r", s), [("p3rev", qq, sp, q8)])
                                    i = e.matmul(qrp[s][:], c_rt[:], qsb[:], start=True, stop=True) if E else None
                                    P.inc(i, "t", ("p3rk", qq, sp, q8))

                    def p3_scalar(e):
                        for qq in range(4):
                            for sp in range(4):
                                for q8 in range(8):
                                    n = (qq * 4 + sp) * 8 + q8
                                    s = q8 % 2
                                    P.wait(e, ("p3mm", qq, sp, q8))
                                    if n >= 1:
                                        pq, prem = divmod(n - 1, 32)
                                        psp, pq8 = divmod(prem, 8)
                                        P.wait(e, ("p3t1", pq, psp, pq8))
                                        P.wait(e, ("p3rk", pq, psp, pq8))
                                    i = e.copy(qsb[:], qpr[s][:]) if E else None
                                    P.inc(i, "a", ("p3ev", qq, sp, q8))
                                    P.wait(e, ("p3rk", qq, sp, q8))
                                    if n >= 1:
                                        P.wait(e, ("p3t2", pq, psp, pq8))
                                    i = e.copy(qrsb[:], qrp[s][:]) if E else None
                                    P.inc(i, "a", ("p3rev", qq, sp, q8))
                                    P.wait(e, ("p3r3", qq, sp, q8))
                                    qh = qq * 8 + q8
                                    i = e.dma_start(out=qT_d[qh, :, sp * 512:(sp + 1) * 512], in_=qout[:]) if E else None
                                    P.inc(i, "wqt", ("p3st", qq, sp, q8), 16)

                    eng_sections(dict(gpsimd=p3_gp, sync=p3_sync,
                                        tensor=p3_tensor, scalar=p3_scalar))
                    barrier()

            if upto >= 5:
                def geom(g, qt):
                    j = 4 * g + qt
                    ext = 256 * (j + 1)
                    n512 = (j + 2) // 2
                    nkb = 2 * (j + 1)
                    nbat = (nkb + 7) // 8
                    return j, ext, n512, nkb, nbat

                with contextlib.ExitStack() as ph:
                    kbuf = [(ph.enter_context(nc.sbuf_tensor(f"kbuf{i}", [128, SEQ], BF16)) if E else None) for i in range(2)]
                    vbuf = [(ph.enter_context(nc.sbuf_tensor(f"vbuf{i}", [128, 32, 128], BF16)) if E else None) for i in range(2)]
                    qbuf = [(ph.enter_context(nc.sbuf_tensor(f"qbuf{i}", [128, 512], BF16)) if E else None) for i in range(2)]
                    Ssb = (ph.enter_context(nc.sbuf_tensor("Ssb", [128, SEQ], F32)) if E else None)
                    praw = (ph.enter_context(nc.sbuf_tensor("praw", [128, SEQ], BF16)) if E else None)
                    Pm = (ph.enter_context(nc.sbuf_tensor("Pm", [128, SEQ], BF16)) if E else None)
                    cmax = (ph.enter_context(nc.sbuf_tensor("cmax", [128, 256], F32)) if E else None)
                    m8 = (ph.enter_context(nc.sbuf_tensor("m8", [128, 8], F32)) if E else None)
                    nm1 = (ph.enter_context(nc.sbuf_tensor("nm1", [128, 1], F32)) if E else None)
                    K16 = (ph.enter_context(nc.sbuf_tensor("K16", [128, 256], F32)) if E else None)
                    KbD = (ph.enter_context(nc.sbuf_tensor("KbD", [128, 256], F32)) if E else None)
                    KbP = (ph.enter_context(nc.sbuf_tensor("KbP", [128, 128], F32)) if E else None)
                    racc = (ph.enter_context(nc.sbuf_tensor("racc", [128, 3], F32)) if E else None)
                    rr = (ph.enter_context(nc.sbuf_tensor("rr", [128, 1], F32)) if E else None)
                    rcp = (ph.enter_context(nc.sbuf_tensor("rcp", [128, 1], F32)) if E else None)
                    PTs = [(ph.enter_context(nc.sbuf_tensor(f"PTs{i}", [128, 1024], BF16)) if E else None) for i in range(2)]
                    ob = [(ph.enter_context(nc.sbuf_tensor(f"ob{i}", [128, 128], BF16)) if E else None) for i in range(2)]
                    Sps = [(ph.enter_context(nc.psum_tensor(f"Sps{i}", [128, 512], F32)) if E else None) for i in range(4)]
                    PTp = [(ph.enter_context(nc.psum_tensor(f"PTp{i}", [128, 1024], BF16)) if E else None) for i in range(2)]
                    AVp = (ph.enter_context(nc.psum_tensor("AVp", [128, 512], F32)) if E else None)

                    ptpar = {}
                    _c = 0
                    for g in range(NGRP):
                        for lh in range(NQH):
                            for qt in range(4):
                                j, ext, n512, nkb, nbat = geom(g, qt)
                                for bat in range(nbat):
                                    ptpar[(g, lh, qt, bat)] = _c % 2
                                    _c += 1

                    def tile_readers(g, lh, qt):
                        j, ext, n512, nkb, nbat = geom(g, qt)
                        out = []
                        for b in range(n512):
                            rd = []
                            if not (j % 2 == 0 and b == n512 - 1):
                                rd.append(("sev", g, lh, qt, b))
                            if b == n512 - 1:
                                rd.append(("dstt", g, lh, qt))
                            out.append(rd)
                        return out

                    def prev_tile(g, lh, qt):
                        idx = (g * NQH + lh) * 4 + qt
                        if idx == 0:
                            return None
                        pg, rem = divmod(idx - 1, NQH * 4)
                        plh, pqt = divmod(rem, 4)
                        return (pg, plh, pqt)

                    def p4_sync(e):
                        P.wait_min(e, "wkt", 16 * 64)
                        P.wait_min(e, "wvs0", 16 * 128)
                        P.wait_min(e, "wvs1", 16 * 128)
                        P.wait_min(e, "wqt", 16 * 128)
                        kvi = 0
                        qi = 0
                        for g in range(NGRP):
                            ext_g = 1024 * (g + 1)
                            for lh in range(NQH):
                                kv = lh // 4
                                if lh % 4 == 0:
                                    slot = kvi % 2
                                    if kvi >= 2:
                                        og, okv = divmod(kvi - 2, 8)
                                        olh = okv * 4 + 3
                                        oj, oext, on512, onkb, onbat = geom(og, 3)
                                        P.wait(e, ("qk", og, olh, 3, on512 - 1))
                                        P.wait(e, ("avb", og, olh, 3, onbat - 1))
                                    i = e.dma_start(out=kbuf[slot][:, 0:ext_g], in_=kT_d[kv, :, 0:ext_g]) if E else None
                                    P.inc(i, "dk0" if slot == 0 else "dk1", ("p4k", kvi), 16)
                                    if E:
                                        i = e.dma_start(out=vbuf[slot][:, 0:8 * (g + 1), :],
                                                        in_=vS_d[kv].rearrange("(n p) d -> p n d", p=128)[:, 0:8 * (g + 1), :])
                                    else:
                                        i = None
                                    P.inc(i, "dv0" if slot == 0 else "dv1", ("p4v", kvi), 16)
                                    kvi += 1
                                qslot = qi % 2
                                if qi >= 2:
                                    og, olh = divmod(qi - 2, NQH)
                                    oj, oext, on512, onkb, onbat = geom(og, 3)
                                    P.wait(e, ("qk", og, olh, 3, on512 - 1))
                                i = e.dma_start(out=qbuf[qslot][:], in_=qT_d[lh, :, g * 512:(g + 1) * 512]) if E else None
                                P.inc(i, "dq0" if qslot == 0 else "dq1", ("p4q", qi), 16)
                                qi += 1

                    def p4_tensor(e):
                        ptctr = [0]
                        kvi = 0
                        qi = 0
                        kv_slot = {}
                        q_slot = {}

                        def emit_qk(g, lh, qt):
                            j, ext, n512, nkb, nbat = geom(g, qt)
                            rds = tile_readers(g, lh, qt)
                            for b in range(n512):
                                bank = 2 * (qt % 2) + (b % 2)
                                P.psum_claim(e, ("S", bank), rds[b])
                                N = blkN(j, b, n512, ext)
                                i = e.matmul(Sps[bank][:, 0:N], qbuf[q_slot[(g, lh)]][:, qt * 128:(qt + 1) * 128],
                                             kbuf[kv_slot[(g, lh)]][:, b * 512:b * 512 + N],
                                             start=True, stop=True) if E else None
                                P.inc(i, "t", ("qk", g, lh, qt, b))

                        def emit_ptav(g, lh, qt):
                            j, ext, n512, nkb, nbat = geom(g, qt)
                            P.psum_claim(e, ("AV",), [("oev", g, lh, qt)])
                            for bat in range(nbat):
                                lo = bat * 8
                                hi = min(nkb, lo + 8)
                                pslot = ptpar[(g, lh, qt, bat)]
                                P.psum_claim(e, ("PT", pslot), [("ptev", g, lh, qt, bat)])
                                if bat == 0:
                                    P.wait(e, ("norm", g, lh, qt))
                                for k in range(lo, hi):
                                    i = e.transpose(PTp[pslot][:, (k - lo) * 128:(k - lo + 1) * 128],
                                                    Pm[:, k * 128:(k + 1) * 128], c_id[:]) if E else None
                                P.inc(i, "t", ("pt", g, lh, qt, bat))
                            for bat in range(nbat):
                                lo = bat * 8
                                hi = min(nkb, lo + 8)
                                pslot = ptpar[(g, lh, qt, bat)]
                                P.wait(e, ("ptev", g, lh, qt, bat))
                                for k in range(lo, hi):
                                    i = e.matmul(AVp[:, 0:128], vbuf[kv_slot[(g, lh)]][:, k, :],
                                                 PTs[pslot][:, (k - lo) * 128:(k - lo + 1) * 128],
                                                 start=(k == 0), stop=(k == nkb - 1)) if E else None
                                P.inc(i, "t", ("avb", g, lh, qt, bat))

                        for g in range(NGRP):
                            for lh in range(NQH):
                                if lh % 4 == 0:
                                    kvi += 1
                                kv_slot[(g, lh)] = (kvi - 1) % 2
                                q_slot[(g, lh)] = qi % 2
                                P.wait(e, ("p4k", kvi - 1))
                                P.wait(e, ("p4v", kvi - 1))
                                P.wait(e, ("p4q", qi))
                                qi += 1
                                emit_qk(g, lh, 0)
                                emit_qk(g, lh, 1)
                                emit_ptav(g, lh, 0)
                                emit_qk(g, lh, 2)
                                emit_ptav(g, lh, 1)
                                emit_qk(g, lh, 3)
                                emit_ptav(g, lh, 2)
                                emit_ptav(g, lh, 3)

                    def p4_scalar(e):
                        oti = [0]

                        def emit_se(g, lh, qt):
                            j, ext, n512, nkb, nbat = geom(g, qt)
                            pt_ = prev_tile(g, lh, qt)
                            first = True
                            for b in range(n512):
                                if j % 2 == 0 and b == n512 - 1:
                                    continue
                                P.wait(e, ("qk", g, lh, qt, b))
                                if first and pt_ is not None:
                                    P.wait(e, ("cmax",) + pt_)
                                    P.wait(e, ("exp",) + pt_)
                                first = False
                                bank = 2 * (qt % 2) + (b % 2)
                                N = blkN(j, b, n512, ext)
                                Nc = N - 256 if b == n512 - 1 else N
                                i = e.copy(Ssb[:, b * 512:b * 512 + Nc], Sps[bank][:, 0:Nc]) if E else None
                                P.inc(i, "a", ("sev", g, lh, qt, b))
                            P.wait(e, ("nm1", g, lh, qt))
                            P.wait(e, ("dstt", g, lh, qt))
                            i = e.activation(praw[:, 0:ext], Ssb[:, 0:ext], AF.Exp, bias=nm1[:], scale=1.0) if E else None
                            P.inc(i, "a", ("exp", g, lh, qt))

                        def emit_po(g, lh, qt):
                            j, ext, n512, nkb, nbat = geom(g, qt)
                            for bat in range(nbat):
                                lo = bat * 8
                                hi = min(nkb, lo + 8)
                                pslot = ptpar[(g, lh, qt, bat)]
                                P.wait(e, ("pt", g, lh, qt, bat))
                                P.buf_claim(e, ("PTs", pslot), ("avb", g, lh, qt, bat))
                                i = e.copy(PTs[pslot][:, 0:(hi - lo) * 128],
                                           PTp[pslot][:, 0:(hi - lo) * 128]) if E else None
                                P.inc(i, "a", ("ptev", g, lh, qt, bat))
                            P.wait(e, ("avb", g, lh, qt, nbat - 1))
                            oslot = oti[0] % 2
                            wsem = "wot0" if oslot == 0 else "wot1"
                            P.wait_min(e, wsem, 16 * (oti[0] // 2))
                            i = e.copy(ob[oslot][:], AVp[:, 0:128]) if E else None
                            P.inc(i, "a", ("oev", g, lh, qt))
                            P.wait(e, ("oev", g, lh, qt))
                            vt = 4 * g + qt
                            i = e.dma_start(out=oT_d[vt, lh, :, :], in_=ob[oslot][:]) if E else None
                            P.inc(i, wsem, ("otst", g, lh, qt), 16)
                            oti[0] += 1

                        for g in range(NGRP):
                            for lh in range(NQH):
                                emit_se(g, lh, 0)
                                emit_se(g, lh, 1)
                                emit_po(g, lh, 0)
                                emit_se(g, lh, 2)
                                emit_po(g, lh, 1)
                                emit_se(g, lh, 3)
                                emit_po(g, lh, 2)
                                emit_po(g, lh, 3)

                    def p4_vector(e):
                        for g in range(NGRP):
                            for lh in range(NQH):
                                for qt in range(4):
                                    j, ext, n512, nkb, nbat = geom(g, qt)
                                    pt_ = prev_tile(g, lh, qt)
                                    i = e.memset(cmax[:], NEG) if E else None
                                    P.inc(i, "v", ("cm0", g, lh, qt))
                                    i = e.memset(K16[:, 0:8], 1.0) if E else None
                                    P.inc(i, "v", ("k160", g, lh, qt))
                                    b = n512 - 1
                                    bank = 2 * (qt % 2) + (b % 2)
                                    N = blkN(j, b, n512, ext)
                                    P.wait(e, ("qk", g, lh, qt, b))
                                    if pt_ is not None:
                                        P.wait(e, ("exp",) + pt_)
                                    if j % 2 == 1:
                                        P.wait(e, ("sev", g, lh, qt, b))
                                    off = N - 256
                                    i = e.scalar_tensor_tensor(Ssb[:, ext - 256:ext], Sps[bank][:, off:off + 256],
                                                               1.0, c_tri[:], OP.mult, OP.add) if E else None
                                    P.inc(i, "v", ("dstt", g, lh, qt))
                                    if n512 >= 2:
                                        lastsev = n512 - 2 if (j % 2 == 0) else n512 - 1
                                        P.wait(e, ("sev", g, lh, qt, lastsev))
                                    P.wait(e, ("dstt", g, lh, qt))
                                    i = e.tensor_reduce(cmax[:, 0:16 * (j + 1)],
                                                        Ssb[:, 0:ext].rearrange("p (c s) -> p c s", s=16),
                                                        AX.X, OP.max) if E else None
                                    P.inc(i, "v", ("cmax", g, lh, qt))
                                    P.wait(e, ("cmax", g, lh, qt))
                                    i = e.max(m8[:], cmax[:]) if E else None
                                    P.inc(i, "v", ("max8", g, lh, qt))
                                    P.wait(e, ("max8", g, lh, qt))
                                    i = e.tensor_scalar(K16[:, 8:256], cmax[:, 8:256], m8[:, 7:8], None, OP.is_ge) if E else None
                                    P.inc(i, "v", ("k16", g, lh, qt))
                                    i = e.tensor_scalar(nm1[:], m8[:, 0:1], -1.0, None, OP.mult) if E else None
                                    P.inc(i, "v", ("nm1", g, lh, qt))
                                    P.wait(e, ("k16", g, lh, qt))
                                    i = e.tensor_tensor(KbD[:], bc16(K16[:, 16 * j:16 * j + 16]),
                                                        c_rec[:].rearrange("p (c s) -> p c s", s=16), OP.max) if E else None
                                    P.inc(i, "v", ("kbd", g, lh, qt))
                                    if j >= 1:
                                        i = e.tensor_tensor(KbP[:], bc16(K16[:, 16 * j - 8:16 * j]),
                                                            c_utp[:].rearrange("p (c s) -> p c s", s=16), OP.max) if E else None
                                        P.inc(i, "v", ("kbp", g, lh, qt))
                                    P.wait(e, ("exp", g, lh, qt))
                                    P.wait(e, ("kbd", g, lh, qt))
                                    if j >= 1:
                                        P.wait(e, ("kbp", g, lh, qt))
                                    P.buf_claim(e, "Pm", ("pt", g, lh, qt, nbat - 1))
                                    if j >= 1:
                                        i = e.scalar_tensor_tensor(
                                            Pm[:, 0:256 * j - 128].rearrange("p (c s) -> p c s", s=16),
                                            praw[:, 0:256 * j - 128].rearrange("p (c s) -> p c s", s=16),
                                            1.0, bc16(K16[:, 0:16 * j - 8]),
                                            OP.mult, OP.mult, accum_out=racc[:, 0:1]) if E else None
                                    if j >= 1:
                                        i = e.scalar_tensor_tensor(
                                            Pm[:, 256 * j - 128:256 * j], praw[:, 256 * j - 128:256 * j],
                                            1.0, KbP[:], OP.mult, OP.mult, accum_out=racc[:, 1:2]) if E else None
                                    i = e.scalar_tensor_tensor(
                                        Pm[:, 256 * j:ext], praw[:, 256 * j:ext],
                                        1.0, KbD[:], OP.mult, OP.mult, accum_out=racc[:, 2:3]) if E else None
                                    P.inc(i, "v", ("pmask", g, lh, qt))
                                    P.wait(e, ("pmask", g, lh, qt))
                                    if j >= 1:
                                        i = e.tensor_reduce(rr[:], racc[:], AX.X, OP.add) if E else None
                                    else:
                                        i = e.tensor_copy(rr[:], racc[:, 2:3]) if E else None
                                    P.inc(i, "v", ("rsum", g, lh, qt))
                                    P.wait(e, ("rsum", g, lh, qt))
                                    i = e.reciprocal(rcp[:], rr[:]) if E else None
                                    P.inc(i, "v", ("rcp", g, lh, qt))
                                    P.wait(e, ("rcp", g, lh, qt))
                                    i = e.tensor_scalar(Pm[:, 0:ext], Pm[:, 0:ext], rcp[:], None, OP.mult) if E else None
                                    P.inc(i, "v", ("norm", g, lh, qt))

                    eng_sections(dict(sync=p4_sync, tensor=p4_tensor,
                                        scalar=p4_scalar, vector=p4_vector))
                    barrier()

            if upto >= 6:
                with contextlib.ExitStack() as ph:
                    wob = [(ph.enter_context(nc.sbuf_tensor(f"wob{i}", [128, HC, 512], BF16)) if E else None) for i in range(2)]
                    wub = [(ph.enter_context(nc.sbuf_tensor(f"wub{i}", [128, HC, 512], BF16)) if E else None) for i in range(2)]
                    Ab = (ph.enter_context(nc.sbuf_tensor("Ab", [128, HC, 128], BF16)) if E else None)
                    Bb = (ph.enter_context(nc.sbuf_tensor("Bb", [128, HID], BF16)) if E else None)
                    otb = [(ph.enter_context(nc.sbuf_tensor(f"otb{i}", [128, 32, 128], BF16)) if E else None) for i in range(2)]
                    xnl = [(ph.enter_context(nc.sbuf_tensor(f"xnl{i}", [128, HC, 128], BF16)) if E else None) for i in range(2)]
                    xsl = [(ph.enter_context(nc.sbuf_tensor(f"xsl{i}", [128, 512], F32)) if E else None) for i in range(2)]
                    zT = [(ph.enter_context(nc.sbuf_tensor(f"zT{i}", [128, 128], BF16)) if E else None) for i in range(NVT)]
                    lg = (ph.enter_context(nc.sbuf_tensor("lg", [128, 4], F32)) if E else None)
                    mmx = (ph.enter_context(nc.sbuf_tensor("mmx", [128, 1], F32)) if E else None)
                    ge4 = (ph.enter_context(nc.sbuf_tensor("ge4", [128, 4], F32)) if E else None)
                    oh4 = (ph.enter_context(nc.sbuf_tensor("oh4", [128, 4], F32)) if E else None)
                    sc1 = (ph.enter_context(nc.sbuf_tensor("sc1", [128, 1], F32)) if E else None)
                    sc2 = (ph.enter_context(nc.sbuf_tensor("sc2", [128, 1], F32)) if E else None)
                    zsel = (ph.enter_context(nc.sbuf_tensor("zsel", [128, 128], BF16)) if E else None)
                    gsl = (ph.enter_context(nc.sbuf_tensor("gsl", [128, 512], F32)) if E else None)
                    ssl2 = (ph.enter_context(nc.sbuf_tensor("ssl2", [128, 512], F32)) if E else None)
                    sg = (ph.enter_context(nc.sbuf_tensor("sg", [128, 512], F32)) if E else None)
                    yb = (ph.enter_context(nc.sbuf_tensor("yb", [128, 512], F32)) if E else None)
                    spsP = (ph.enter_context(nc.psum_tensor("spsP", [128, 512], F32)) if E else None)
                    fpsP = (ph.enter_context(nc.psum_tensor("fpsP", [128, 512], F32)) if E else None)
                    zpsP = (ph.enter_context(nc.psum_tensor("zpsP", [128, 512], F32)) if E else None)
                    ztpP = (ph.enter_context(nc.psum_tensor("ztpP", [128, 512], BF16)) if E else None)

                    def p6_gp(e):
                        i = e.dma_start(out=Ab[:], in_=A_padT.rearrange("(c p) n -> p c n", p=128)) if E else None
                        P.inc(i, "gab", ("gA",), 16)
                        i = e.dma_start(out=Bb[:], in_=B_padT) if E else None
                        P.inc(i, "gab", ("gB",), 16)
                        for mb in range(8):
                            for lt in range(NVT):
                                idx = mb * NVT + lt
                                P.wait(e, ("sev6", mb, lt))
                                P.wait(e, ("silu", mb, lt))
                                i = e.tensor_tensor(sg[:], ssl2[:], gsl[:], OP.mult) if E else None
                                P.inc(i, "p", ("sg", mb, lt))
                                P.wait(e, ("sg", mb, lt))
                                P.wait_min(e, "wy", 16 * idx)
                                P.wait_min(e, "dxs0" if idx % 2 == 0 else "dxs1", 16 * (idx // 2 + 1))
                                i = e.tensor_tensor(yb[:], sg[:], xsl[idx % 2][:], OP.add) if E else None
                                P.inc(i, "p", ("yb", mb, lt))

                    def p6_sync(e):
                        P.wait_min(e, "wot0", 16 * 256)
                        P.wait_min(e, "wot1", 16 * 256)
                        P.wait_min(e, "wxno", 16 * 4)
                        P.wait_min(e, "gcwo", 16 * 8)
                        P.wait_min(e, "gcwup", 16 * 8)

                        def wload(mb):
                            if E:
                                i = e.dma_start(out=wob[mb % 2][:],
                                                in_=wo_bf_d.rearrange("(c p) m -> p c m", p=128)[:, :, mb * 512:(mb + 1) * 512])
                            else:
                                i = None
                            P.inc(i, "gwo" if mb % 2 == 0 else "gwo1", ("gwo", mb), 16)
                            if E:
                                i = e.dma_start(out=wub[mb % 2][:],
                                                in_=wup_bf_d.rearrange("(c p) m -> p c m", p=128)[:, :, mb * 512:(mb + 1) * 512])
                            else:
                                i = None
                            P.inc(i, "gwup" if mb % 2 == 0 else "gwup1", ("gwup", mb), 16)

                        wload(0)
                        wload(1)
                        n = 0
                        for mb in range(8):
                            if mb >= 2:
                                P.wait(e, ("op", mb - 2, NVT - 1))
                                P.wait(e, ("fp", mb - 2, NVT - 1))
                                wload(mb)
                            for lt in range(NVT):
                                slot = n % 2
                                if n >= 2:
                                    pmb, plt = divmod(n - 2, NVT)
                                    P.wait(e, ("op", pmb, plt))
                                    P.wait(e, ("fp", pmb, plt))
                                    P.wait(e, ("yb", pmb, plt))
                                if E:
                                    i = e.dma_start(out=otb[slot][:], in_=oT_d[lt].rearrange("h d q -> d h q"))
                                else:
                                    i = None
                                P.inc(i, "dot0" if slot == 0 else "dot1", ("p6o", mb, lt), 16)
                                if E:
                                    dd = xnoT_d.rearrange("(c p) s -> p c s", p=128)
                                    i = e.dma_start(out=xnl[slot][:], in_=dd[:, :, lt * 128:(lt + 1) * 128])
                                else:
                                    i = None
                                P.inc(i, "dxl0" if slot == 0 else "dxl1", ("p6n", mb, lt), 16)
                                i = e.dma_start(out=xsl[slot][:],
                                                in_=x_own[lt * 128:(lt + 1) * 128, mb * 512:(mb + 1) * 512]) if E else None
                                P.inc(i, "dxs0" if slot == 0 else "dxs1", ("p6x", mb, lt), 16)
                                n += 1

                    def p6_tensor(e):
                        P.wait_min(e, "gab", 32)
                        n = 0
                        for mb in range(8):
                            P.wait_min(e, "gwo" if mb % 2 == 0 else "gwo1", 16 * (mb // 2 + 1))
                            P.wait_min(e, "gwup" if mb % 2 == 0 else "gwup1", 16 * (mb // 2 + 1))
                            for lt in range(NVT):
                                slot = n % 2
                                sname_o = "dot0" if slot == 0 else "dot1"
                                sname_n = "dxl0" if slot == 0 else "dxl1"
                                P.wait_min(e, sname_o, 16 * (n // 2 + 1))
                                P.wait_min(e, sname_n, 16 * (n // 2 + 1))
                                if mb == 0:
                                    P.psum_claim(e, ("z",), [("zsel", lt)])
                                    for hc in range(HC):
                                        i = e.matmul(zpsP[:, 0:128], xnl[slot][:, hc, :], Ab[:, hc, :],
                                                     start=(hc == 0), stop=(hc == HC - 1)) if E else None
                                    P.inc(i, "t", ("zmm", lt))
                                    P.wait(e, ("zsel", lt))
                                    P.psum_claim(e, ("zt",), [("ztev", lt)])
                                    i = e.transpose(ztpP[:, 0:128], zsel[:], c_id[:]) if E else None
                                    P.inc(i, "t", ("ztr", lt))
                                P.psum_claim(e, ("sps",), [("sev6", mb, lt)])
                                for hd in range(NQH):
                                    i = e.matmul(spsP[:], otb[slot][:, hd, :], wob[mb % 2][:, hd, :],
                                                 start=(hd == 0), stop=(hd == NQH - 1)) if E else None
                                P.inc(i, "t", ("op", mb, lt))
                                P.psum_claim(e, ("fps",), [("silu", mb, lt)])
                                P.wait(e, ("ztev", lt))
                                for hc in range(HC):
                                    i = e.matmul(fpsP[:], xnl[slot][:, hc, :], wub[mb % 2][:, hc, :],
                                                 start=(hc == 0), stop=False) if E else None
                                i = e.matmul(fpsP[:], zT[lt][:], Bb[:, mb * 512:(mb + 1) * 512],
                                             start=False, stop=True) if E else None
                                P.inc(i, "t", ("fp", mb, lt))
                                n += 1

                    def p6_vector(e):
                        for lt in range(NVT):
                            P.wait(e, ("zmm", lt))
                            if E:
                                zz = zpsP[:, 0:128].rearrange("p (e r) -> p e r", r=32)[:, :, 31:32]
                                cb = bass.AP(c_bb[:].tensor, c_bb[:].offset, list(c_bb[:].ap) + [[0, 1]])
                                i = e.tensor_tensor(lg[:], zz, cb, OP.add)
                            else:
                                i = None
                            P.inc(i, "v", ("lg", lt))
                            P.wait(e, ("lg", lt))
                            i = e.tensor_reduce(mmx[:], lg[:], AX.X, OP.max) if E else None
                            P.inc(i, "v", ("lmx", lt))
                            P.wait(e, ("lmx", lt))
                            i = e.tensor_scalar(ge4[:], lg[:], mmx[:], None, OP.is_ge) if E else None
                            P.inc(i, "v", ("ge4", lt))
                            P.wait(e, ("ge4", lt))
                            i = e.tensor_copy(oh4[:, 0:1], ge4[:, 0:1]) if E else None
                            P.inc(i, "v", ("oh0", lt))
                            P.wait(e, ("oh0", lt))
                            i = e.tensor_scalar(sc1[:], oh4[:, 0:1], -1.0, 1.0, OP.mult, OP.add) if E else None
                            P.inc(i, "v", ("nh0", lt))
                            P.wait(e, ("nh0", lt))
                            i = e.tensor_tensor(oh4[:, 1:2], ge4[:, 1:2], sc1[:], OP.mult) if E else None
                            P.inc(i, "v", ("oh1", lt))
                            P.wait(e, ("oh1", lt))
                            i = e.tensor_scalar(sc2[:], oh4[:, 1:2], -1.0, 1.0, OP.mult, OP.add) if E else None
                            P.inc(i, "v", ("nh1", lt))
                            P.wait(e, ("nh1", lt))
                            i = e.tensor_tensor(sc1[:], sc1[:], sc2[:], OP.mult) if E else None
                            P.inc(i, "v", ("n01", lt))
                            P.wait(e, ("n01", lt))
                            i = e.tensor_tensor(oh4[:, 2:3], ge4[:, 2:3], sc1[:], OP.mult) if E else None
                            P.inc(i, "v", ("oh2", lt))
                            P.wait(e, ("oh2", lt))
                            i = e.tensor_scalar(sc2[:], oh4[:, 2:3], -1.0, 1.0, OP.mult, OP.add) if E else None
                            P.inc(i, "v", ("nh2", lt))
                            P.wait(e, ("nh2", lt))
                            i = e.tensor_tensor(sc1[:], sc1[:], sc2[:], OP.mult) if E else None
                            P.inc(i, "v", ("n012", lt))
                            P.wait(e, ("n012", lt))
                            i = e.tensor_tensor(oh4[:, 3:4], ge4[:, 3:4], sc1[:], OP.mult) if E else None
                            P.inc(i, "v", ("oh3", lt))
                            P.wait(e, ("oh3", lt))
                            for ex in range(4):
                                i = e.tensor_scalar(zsel[:, ex * 32:(ex + 1) * 32], zpsP[:, ex * 32:(ex + 1) * 32],
                                                    oh4[:, ex:ex + 1], None, OP.mult) if E else None
                            P.inc(i, "v", ("zsel", lt))
                            P.wait(e, ("ztr", lt))
                            i = e.tensor_copy(zT[lt][:], ztpP[:, 0:128]) if E else None
                            P.inc(i, "v", ("ztev", lt))

                    def p6_scalar(e):
                        for mb in range(8):
                            for lt in range(NVT):
                                idx = mb * NVT + lt
                                P.wait(e, ("op", mb, lt))
                                if idx >= 1:
                                    pmb, plt = divmod(idx - 1, NVT)
                                    P.wait(e, ("sg", pmb, plt))
                                i = e.copy(ssl2[:], spsP[:]) if E else None
                                P.inc(i, "a", ("sev6", mb, lt))
                                P.wait(e, ("fp", mb, lt))
                                i = e.activation(gsl[:], fpsP[:], AF.Silu) if E else None
                                P.inc(i, "a", ("silu", mb, lt))
                                P.wait(e, ("yb", mb, lt))
                                i = e.dma_start(out=y_out[lt * 128:(lt + 1) * 128, mb * 512:(mb + 1) * 512],
                                                in_=yb[:]) if E else None
                                P.inc(i, "wy", ("yst", mb, lt), 16)

                    eng_sections(dict(gpsimd=p6_gp, sync=p6_sync, tensor=p6_tensor,
                                        vector=p6_vector, scalar=p6_scalar))

                    def fin_sync(e):
                        P.wait_min(e, "wy", 16 * 128)
                    eng_sections(dict(sync=fin_sync))

            if upto < 6:
                def dbg_fin(e):
                    for s in ["wxnt", "wxno", "wkt", "wvs0", "wvs1", "wqt",
                              "wot0", "wot1"]:
                        if P.cnt.get(s):
                            P.wait_min(e, s, P.cnt[s])
                eng_sections(dict(sync=dbg_fin))

        P.begin(False)
        run()
        P.begin(True)
        run()

    return nc


_CACHED = {}


def kernel(L0_attn_hidden_states, L0_attn_wq, L0_attn_wk, L0_attn_wv, L0_attn_wo,
           L0_ffn_W_up, L0_ffn_gate_w, L0_ffn_expert_bias, L0_ffn_lora_A, L0_ffn_lora_B):
    x = np.ascontiguousarray(np.asarray(L0_attn_hidden_states, dtype=np.float32))
    wq = np.asarray(L0_attn_wq, np.float32)
    wk = np.asarray(L0_attn_wk, np.float32)
    wv = np.asarray(L0_attn_wv, np.float32)
    wo = np.asarray(L0_attn_wo, np.float32)
    wup = np.asarray(L0_ffn_W_up, np.float32)
    gw = np.asarray(L0_ffn_gate_w, np.float32)
    eb = np.asarray(L0_ffn_expert_bias, np.float32)
    lA = np.asarray(L0_ffn_lora_A, np.float32)
    lB = np.asarray(L0_ffn_lora_B, np.float32)

    inv = 1.0 / (10000.0 ** (np.arange(0, DH, 2, dtype=np.float32) / DH))
    ang = np.arange(SEQ, dtype=np.float32)[:, None] * inv[None, :]
    emb = np.concatenate([ang, ang], axis=1)
    cosf = np.cos(emb).astype(np.float32)
    sinf = np.sin(emb).astype(np.float32)
    scale = 1.0 / math.sqrt(DH)
    identb = np.eye(128, dtype=ml_dtypes.bfloat16)
    R = np.zeros((128, 128), np.float32)
    R[0:64, 64:128] = -np.eye(64)
    R[64:128, 0:64] = np.eye(64)
    rotT = np.ascontiguousarray(R.T).astype(ml_dtypes.bfloat16)
    ones1 = np.ones((1, 128), np.float32)

    A_padT = np.zeros((HID, 128), np.float32)
    for ex in range(4):
        A_padT[:, ex * 32:ex * 32 + 8] = lA[ex].T
        A_padT[:, ex * 32 + 31] = gw[ex]
    B_padT = np.zeros((128, HID), np.float32)
    for ex in range(4):
        B_padT[ex * 32:ex * 32 + 8, :] = lB[ex].T
    ebias = eb.reshape(1, 4).astype(np.float32)

    pcol = np.arange(128)[:, None]
    j1 = np.arange(256)[None, :]
    j2 = np.arange(128)[None, :]

    wqT = np.ascontiguousarray(wq.T)
    wkT = np.ascontiguousarray(wk.T)
    wvT = np.ascontiguousarray(wv.T)
    woT = np.ascontiguousarray(wo.T)
    wupT = np.ascontiguousarray(wup.T)

    in_maps = []
    for c in range(8):
        b, h = divmod(c, 2)
        own = np.arange(OWN) * 2 + h
        trim = np.where(j1 <= 2 * pcol + h, 0.0, NEG).astype(np.float32)
        rec = (j1 > 2 * pcol + h - 128).astype(np.float32)
        utp = (j2 > 2 * pcol + h).astype(np.float32)
        in_maps.append(dict(
            x_full=x[b],
            x_own=np.ascontiguousarray(x[b, own]),
            wqT=wqT, wkT=wkT, wvT=wvT, woT=woT, wupT=wupT,
            A_padT=A_padT, B_padT=B_padT, ebias=ebias,
            cosq=np.ascontiguousarray((cosf[own] * scale).T),
            sinq=np.ascontiguousarray((sinf[own] * scale).T),
            cosk=np.ascontiguousarray(cosf.T),
            sink=np.ascontiguousarray(sinf.T),
            identb=identb, identf=np.eye(128, dtype=np.float32), rotT=rotT,
            trimask2=trim, recent2=rec, ut2prev=utp,
            ones1=ones1,
        ))

    if "prog" not in _CACHED:
        _CACHED["prog"] = build_program()
    nc = _CACHED["prog"]

    res = run_bass_kernel_spmd(nc, in_maps, core_ids=list(range(8)))

    out = np.empty((NB, SEQ, HID), np.float32)
    for c in range(8):
        b, h = divmod(c, 2)
        out[b, h::2] = np.asarray(res.results[c]["y"], np.float32)
    return out
